# version 01
# baseline (speedup 1.0000x reference)
import numpy as np
from contextlib import ExitStack

import concourse.bass as bass
import concourse.mybir as mybir
from concourse.alu_op_type import AluOpType as OP
from concourse.bass_utils import run_bass_kernel_spmd

F32 = mybir.dt.float32
BF16 = mybir.dt.bfloat16
U32 = mybir.dt.uint32
I32 = mybir.dt.int32
AF = mybir.ActivationFunctionType
AX = mybir.AxisListType

B, S, H = 4, 2048, 4096
NH, D = 32, 128
INTER = 11008
F2 = 2 * INTER
NQKV = H + 2 * D
T = 1024
C = 512
EPS = 1e-5
SCALE = float(np.float32(D) ** -0.5)
NEGBIG = -1e30

BLOCKS = {0: [0, 3, 4, 7, 8, 11, 12, 15], 1: [1, 2, 5, 6, 9, 10, 13, 14]}
SEM_INC_REL = 0x802701800
L2P = [4, 5, 6, 7, 2, 3, 0, 1]
TPB_BASES = [0x2000000000, 0x3000000000, 0x6000000000, 0x7000000000,
             0x802000000000, 0x803000000000, 0x806000000000, 0x807000000000]

STAGE = 3
DBG = False


def _alibi_slopes():
    import math
    n = NH
    cp2 = 2 ** math.floor(math.log2(n))
    base = 2.0 ** (-(2.0 ** (-(math.log2(cp2) - 3))))
    slopes = base ** np.arange(1, cp2 + 1, dtype=np.float64)
    if cp2 != n:
        extra_base = 2.0 ** (-(2.0 ** (-(math.log2(2 * cp2) - 3))))
        rem = min(cp2, n - cp2)
        extra = extra_base ** np.arange(1, 1 + 2 * rem, 2, dtype=np.float64)
        slopes = np.concatenate([slopes, extra])
    return slopes.astype(np.float32)


class Sem:
    def __init__(self, es, nc, name):
        self.h = es.enter_context(nc.semaphore(name))
        self.n = 0

    def inc(self, inst, k=1):
        inst.then_inc(self.h, k)
        self.n += k

    def wait(self, eng, n=None):
        v = self.n if n is None else n
        if v > 0:
            eng.wait_ge(self.h, v)


class Pipe:

    def __init__(self, es, nc, name, nslots, slot_fill_sems=False):
        if slot_fill_sems:
            self.fillS = [Sem(es, nc, f"{name}f{k}") for k in range(nslots)]
            self.fill = None
        else:
            self.fill = Sem(es, nc, name + "f")
            self.fillS = None
        self.free = Sem(es, nc, name + "e")
        self.n = nslots
        self.fill_marks = []
        self.free_marks = []

    def fill_inc(self, inst, i, k):
        s = self.fillS[i % self.n] if self.fillS is not None else self.fill
        s.inc(inst, k)

    def fill_mark(self, i):
        s = self.fillS[i % self.n] if self.fillS is not None else self.fill
        return (s.h, s.n)

    @staticmethod
    def _wait_marks(eng, marks):
        if isinstance(marks, tuple):
            marks = [marks]
        for h, v in marks:
            if v > 0:
                eng.wait_ge(h, v)

    def p_wait(self, eng, i):
        if i >= self.n:
            self._wait_marks(eng, self.free_marks[i - self.n])

    def p_done(self, i, mark=None):
        assert i == len(self.fill_marks), (i, len(self.fill_marks))
        self.fill_marks.append(mark if mark is not None
                               else self.fill_mark(i))

    def c_wait(self, eng, i):
        self._wait_marks(eng, self.fill_marks[i])

    def c_done(self, i, mark=None):
        while len(self.free_marks) <= i:
            self.free_marks.append(None)
        self.free_marks[i] = (mark if mark is not None
                              else (self.free.h, self.free.n))


def build_program():
    nc = bass.Bass("TRN2", target_bir_lowering=False)
    es = ExitStack()
    dt = nc.dram_tensor

    x0_d = dt("x0", [T, H], F32, kind="ExternalInput").ap()
    wqkv_d = dt("wqkv", [H, NQKV], F32, kind="ExternalInput").ap()
    wo_d = dt("wo", [H, H], F32, kind="ExternalInput").ap()
    qwgu_d = dt("qwgu", [512, F2], U32, kind="ExternalInput").ap()
    qzgu_d = dt("qzgu", [32, F2 // 8], U32, kind="ExternalInput").ap()
    scgu_d = dt("scgu", [32, F2], F32, kind="ExternalInput").ap()
    qwdn_d = dt("qwdn", [1376, H], U32, kind="ExternalInput").ap()
    qzdn_d = dt("qzdn", [86, H // 8], U32, kind="ExternalInput").ap()
    scdn_d = dt("scdn", [86, H], F32, kind="ExternalInput").ap()
    posq_d = dt("posq", [128, 8], F32, kind="ExternalInput").ap()
    poskv_d = dt("poskv", [1, 2048], F32, kind="ExternalInput").ap()
    augq_d = dt("augq", [2, NH * T], F32, kind="ExternalInput").ap()
    augk_d = dt("augk", [2, NH * 2048], F32, kind="ExternalInput").ap()
    ident_d = dt("ident", [128, 128], F32, kind="ExternalInput").ap()
    par_d = dt("par", [1, 1], I32, kind="ExternalInput").ap()
    paddr_d = dt("paddr", [1, 2], U32, kind="ExternalInput").ap()
    out_d = dt("out", [T, H], F32, kind="ExternalOutput").ap()
    if DBG:
        dbg_xt_d = dt("dbg_xt", [128, 32 * C], BF16, kind="ExternalOutput").ap()
        dbg_h_d = dt("dbg_h", [128, 86 * C], BF16, kind="ExternalOutput").ap()
        dbg_cgu_d = dt("dbg_cgu", [1024, 512], BF16, kind="ExternalOutput").ap()
        dbg_cdn_d = dt("dbg_cdn", [1024, H], BF16, kind="ExternalOutput").ap()

    xs1_d = dt("xs1", [T, H], F32, kind="Internal").ap()
    xs2_d = dt("xs2", [T, H], F32, kind="Internal").ap()
    zsgu_d = dt("zsgu", [32, F2], BF16, kind="Internal").ap()
    zsdn_d = dt("zsdn", [86, H], BF16, kind="Internal").ap()
    scgub_d = dt("scgub", [32, F2], BF16, kind="Internal").ap()
    scdnb_d = dt("scdnb", [86, H], BF16, kind="Internal").ap()
    cgu_d = dt("cgu", [43 * 32 * 128, 512], BF16, kind="Internal").ap()
    cdn_d = dt("cdn", [INTER, H], BF16, kind="Internal").ap()
    kvsh_d = dt("kvsh", [2048, 256], BF16, kind="Internal",
                addr_space="Shared").ap()

    def psbuf(name, shape, dtyp):
        return es.enter_context(nc.sbuf_tensor(name, shape, dtyp))

    identbf = psbuf("identbf", [128, 128], BF16)
    poskb = psbuf("poskb", [128, 2048], F32)
    posq_pp = psbuf("posq_pp", [128, 8], F32)
    par_sb = psbuf("par_sb", [1, 1], I32)
    paddr_sb = psbuf("paddr_sb", [1, 2], U32)

    P = es.enter_context(nc.psum_tensor("P", [128, 4096], F32))
    Pap = P[:]
    Pbf_ap = P[:].bitcast(BF16)

    def bank_f32(i, w=512):
        return Pap[:, i * 512:i * 512 + w]

    def bank_bf16_tile(i):
        return Pbf_ap[:, i * 1024:i * 1024 + 128]

    def sem(name):
        return Sem(es, nc, name)

    def pipe(name, n, slot_fill_sems=False):
        return Pipe(es, nc, name, n, slot_fill_sems=slot_fill_sems)

    s_in = sem("s_in")
    s_misc = sem("s_misc")
    s_kvbar = sem("s_kvbar")
    kvbar_num = s_kvbar.h.num

    lnx_p = pipe("lnx", 2, slot_fill_sems=True)
    lnP_p = pipe("lnP", 2)
    pT_p = pipe("pT", 2)
    s_lnv = sem("s_lnv")
    s_lns = sem("s_lns")
    s_xt = sem("s_xt")

    q_p = pipe("q", 2, slot_fill_sems=True)
    sc_p = pipe("sc", 2, slot_fill_sems=True)
    u_p = pipe("u", 2)
    w_p = pipe("w", 4, slot_fill_sems=True)
    pg_p = pipe("pg", 2)
    sil_p = pipe("sil", 2)
    s_h = sem("s_h")
    wd_p = pipe("wd", 2, slot_fill_sems=True)
    pd_p = pipe("pd", 2)
    xr_p = pipe("xr", 2, slot_fill_sems=True)
    xo_p = pipe("xo", 2)
    xoe = [sem("xoe0"), sem("xoe1")]
    s_ffn = sem("s_ffn")

    wq_p = pipe("wq", 2, slot_fill_sems=True)
    qkv_p = pipe("qkv", 1)
    pwo_p = pipe("pwo", 2)
    aug_p = pipe("aug", 1, slot_fill_sems=True)
    ssc_p = pipe("ssc", 2)
    prb_p = pipe("prb", 2)
    pt2_p = pipe("pt2", 2)
    pbt_p = pipe("pbt", 1 << 30)
    s_vch = sem("s_vch")
    s_sch = sem("s_sch")

    def vchain(eng, inst, semobj):
        semobj.inc(inst, 1)
        eng.wait_ge(semobj.h, semobj.n)

    s_attS = sem("s_attS")
    s_attV = sem("s_attV")
    s_pv = sem("s_pv")
    s_cwS = [sem(f"s_cw{k}") for k in range(4)]

    cnt = {k: 0 for k in ["li", "ti", "fci", "ui", "uj", "di", "wdi", "ri",
                          "qi", "wqi", "ki", "pbi", "ai", "si", "oi"]}

    blk = es.enter_context(nc.Block())

    def fence_all():
        for engdec in (blk.tensor, blk.vector, blk.scalar, blk.sync,
                       blk.gpsimd):
            @engdec
            def _(eng):
                xoe[0].wait(eng)
                xoe[1].wait(eng)

    with ExitStack() as zscope:
        ident_f = zscope.enter_context(nc.sbuf_tensor("ident_f", [128, 128], F32))

        @blk.sync
        def _(sync):
            sync.dma_start(out=posq_pp[:], in_=posq_d).then_inc(s_in.h, 16)
            sync.dma_start(out=par_sb[:], in_=par_d).then_inc(s_in.h, 16)
            sync.dma_start(out=paddr_sb[:], in_=paddr_d).then_inc(s_in.h, 16)
            sync.dma_start(out=ident_f[:], in_=ident_d).then_inc(s_in.h, 16)
        s_in.n += 64

        @blk.gpsimd
        def _(gpsimd):
            gpsimd.dma_start(
                out=poskb[:],
                in_=poskv_d.to_broadcast([128, 2048])).then_inc(s_in.h, 16)
            gpsimd.dma_start(out=scgub_d, in_=scgu_d).then_inc(s_in.h, 16)
            gpsimd.dma_start(out=scdnb_d, in_=scdn_d).then_inc(s_in.h, 16)
        s_in.n += 48

        @blk.vector
        def _(vector):
            s_in.wait(vector)
            vector.tensor_copy(identbf[:], ident_f[:]).then_inc(s_misc.h, 1)
        s_misc.n += 1

        s_z = sem("s_z")
        s_zc = sem("s_zc")
        s_zw = sem("s_zw")
        zq_sb = zscope.enter_context(nc.sbuf_tensor("zq_sb", [32, 2752], U32))
        zpl = zscope.enter_context(nc.sbuf_tensor("zpl", [86, 2752], U32))
        sc_sb = zscope.enter_context(nc.sbuf_tensor("sc_sb", [32, 11008], F32))
        z1f = zscope.enter_context(nc.sbuf_tensor("z1f", [32, 11008], F32))
        z1b = zscope.enter_context(nc.sbuf_tensor("z1b", [32, 11008], BF16))

        for half in range(2):
            c0 = half * 11008

            @blk.sync
            def _(sync):
                if half == 1:
                    s_zw.wait(sync)
                sync.dma_start(out=zq_sb[:, 0:1376],
                               in_=qzgu_d[:, c0 // 8:(c0 + 11008) // 8]
                               ).then_inc(s_z.h, 16)
                sync.dma_start(out=sc_sb[:],
                               in_=scgu_d[:, c0:c0 + 11008]).then_inc(s_z.h, 16)
            s_z.n += 32

            @blk.vector
            def _(vector):
                s_z.wait(vector)
                for jj in range(8):
                    i0 = vector.tensor_scalar(
                        zpl[0:32, 0:1376], zq_sb[:, 0:1376], 4 * jj, 15,
                        OP.logical_shift_right, OP.bitwise_and)
                    vchain(vector, i0, s_vch)
                    i0 = vector.tensor_copy(
                        z1f[:].rearrange("p (c j) -> p c j", j=8)[:, :, jj],
                        zpl[0:32, 0:1376])
                    vchain(vector, i0, s_vch)
                i0 = vector.tensor_scalar(z1f[:], z1f[:], 1.0, None, OP.add)
                vchain(vector, i0, s_vch)
                vector.tensor_tensor(z1b[:], z1f[:], sc_sb[:],
                                     OP.mult).then_inc(s_zc.h, 1)
            s_zc.n += 1

            @blk.sync
            def _(sync):
                s_zc.wait(sync)
                sync.dma_start(out=zsgu_d[:, c0:c0 + 11008],
                               in_=z1b[:]).then_inc(s_zw.h, 16)
            s_zw.n += 16

        zqd_sb = zscope.enter_context(nc.sbuf_tensor("zqd_sb", [86, 512], U32))
        scd_sb = zscope.enter_context(nc.sbuf_tensor("scd_sb", [86, 4096], F32))
        z1fd = zscope.enter_context(nc.sbuf_tensor("z1fd", [86, 4096], F32))
        z1bd = zscope.enter_context(nc.sbuf_tensor("z1bd", [86, 4096], BF16))

        @blk.sync
        def _(sync):
            sync.dma_start(out=zqd_sb[:], in_=qzdn_d).then_inc(s_z.h, 16)
            sync.dma_start(out=scd_sb[:], in_=scdn_d).then_inc(s_z.h, 16)
        s_z.n += 32

        @blk.vector
        def _(vector):
            s_z.wait(vector)
            for jj in range(8):
                i0 = vector.tensor_scalar(
                    zpl[0:86, 0:512], zqd_sb[:], 4 * jj, 15,
                    OP.logical_shift_right, OP.bitwise_and)
                vchain(vector, i0, s_vch)
                i0 = vector.tensor_copy(
                    z1fd[:].rearrange("p (c j) -> p c j", j=8)[:, :, jj],
                    zpl[0:86, 0:512])
                vchain(vector, i0, s_vch)
            i0 = vector.tensor_scalar(z1fd[:], z1fd[:], 1.0, None, OP.add)
            vchain(vector, i0, s_vch)
            vector.tensor_tensor(z1bd[:], z1fd[:], scd_sb[:],
                                 OP.mult).then_inc(s_zc.h, 1)
        s_zc.n += 1

        @blk.sync
        def _(sync):
            s_zc.wait(sync)
            sync.dma_start(out=zsdn_d, in_=z1bd[:]).then_inc(s_zw.h, 16)
            s_zw.wait(sync)
        s_zw.n += 16
    zdone = s_zw.n

    def emit_ln_xt(src_d, xt_sb, tok0, ntt, ln, psum_guard=None):
        lnx, lnP = ln["lnx"], ln["lnP"]
        st1, nm, vr = ln["st1"], ln["nm"], ln["vr"]
        Ctok = ntt * 128
        first_guard = [psum_guard]
        for tt in range(ntt):
            li = cnt["li"]
            cnt["li"] += 1

            @blk.sync
            def _(sync):
                lnx_p.p_wait(sync, li)
                i1 = sync.dma_start(
                    out=lnx[li % 2][:],
                    in_=src_d[tok0 + tt * 128: tok0 + (tt + 1) * 128, :])
                lnx_p.fill_inc(i1, li, 16)
            lnx_p.p_done(li)

            @blk.vector
            def _(vector):
                lnx_p.c_wait(vector, li)
                i0 = vector.tensor_reduce(st1[:], lnx[li % 2][:], AX.X, OP.add)
                vchain(vector, i0, s_vch)
                i1 = vector.tensor_scalar(nm[:], st1[:], -1.0 / H, None,
                                          OP.mult)
                s_lnv.inc(i1, 1)

            @blk.scalar
            def _(scalar):
                s_lnv.wait(scalar)
                lnP_p.p_wait(scalar, li)
                i1 = scalar.activation(lnP[li % 2][:], lnx[li % 2][:],
                                       AF.Square, bias=nm[:], scale=1.0,
                                       accum_out=vr[:])
                s_lns.inc(i1, 1)

            @blk.vector
            def _(vector):
                s_lns.wait(vector)
                i1 = vector.tensor_scalar(vr[:], vr[:], 1.0 / H, EPS, OP.mult,
                                          OP.add)
                s_lnv.inc(i1, 1)

            @blk.scalar
            def _(scalar):
                s_lnv.wait(scalar)
                i1 = scalar.activation(vr[:], vr[:], AF.Sqrt)
                s_lns.inc(i1, 1)

            @blk.vector
            def _(vector):
                s_lns.wait(vector)
                i0 = vector.reciprocal(st1[:], vr[:])
                vchain(vector, i0, s_vch)
                lnP_p.p_wait(vector, li)
                i1 = None
                for jj in range(8):
                    i1 = vector.tensor_scalar(
                        lnP[li % 2][:, jj * 512:(jj + 1) * 512],
                        lnx[li % 2][:].rearrange("p (r j) -> p r j", j=8)
                            [:, :, jj],
                        nm[:], st1[:], OP.add, OP.mult)
                lnP_p.fill.inc(i1, 1)
            lnP_p.p_done(li)
            lnx_p.c_done(li, mark=(lnP_p.fill.h, lnP_p.fill.n))

            for t in range(32):
                ti = cnt["ti"]
                cnt["ti"] += 1

                @blk.tensor
                def _(tensor):
                    if t == 0:
                        lnP_p.c_wait(tensor, li)
                        if first_guard[0] is not None:
                            g, gn = first_guard[0]
                            if gn > 0:
                                tensor.wait_ge(g, gn)
                            first_guard[0] = None
                    pT_p.p_wait(tensor, ti)
                    i1 = tensor.transpose(
                        bank_bf16_tile(ti % 2),
                        lnP[li % 2][:, t * 128:(t + 1) * 128], identbf[:])
                    pT_p.fill.inc(i1, 1)
                pT_p.p_done(ti)
                if t == 31:
                    lnP_p.c_done(li, mark=(pT_p.fill.h, pT_p.fill.n))

                @blk.scalar
                def _(scalar):
                    pT_p.c_wait(scalar, ti)
                    i1 = scalar.copy(
                        xt_sb[:, t * Ctok + tt * 128: t * Ctok + (tt + 1) * 128],
                        bank_bf16_tile(ti % 2))
                    s_xt.inc(i1, 1)
                pT_p.c_done(ti, mark=(s_xt.h, s_xt.n))

    def emit_dequant_unit(fbx, ui, rows, qload, scload, cache_tile_out,
                          consume_j):
        usl = ui % 2
        qb, sgb, z1gb, ub, wb = (fbx["qb"], fbx["sgb"], fbx["z1gb"],
                                 fbx["ub"], fbx["wb"])

        @blk.sync
        def _(sync):
            q_p.p_wait(sync, ui)
            qload(sync, qb[usl])
        q_p.p_done(ui)

        @blk.scalar
        def _(scalar):
            sc_p.p_wait(scalar, ui)
            scload(scalar, sgb[usl], z1gb[usl])
        sc_p.p_done(ui)

        for j in range(8):
            uj = cnt["uj"]
            cnt["uj"] += 1

            @blk.vector
            def _(vector):
                if j == 0:
                    q_p.c_wait(vector, ui)
                    sc_p.c_wait(vector, ui)
                w_p.p_wait(vector, uj)
                wsl = wb[uj % 4][0:rows, :]
                i0 = vector.tensor_scalar(
                    ub[uj % 2][0:rows, :], qb[usl][0:rows, :], 4 * j, 15,
                    OP.logical_shift_right, OP.bitwise_and)
                vchain(vector, i0, s_vch)
                i0 = vector.tensor_tensor(wsl, ub[uj % 2][0:rows, :],
                                          sgb[usl][0:rows, :], OP.mult)
                vchain(vector, i0, s_vch)
                i1 = vector.tensor_tensor(wsl, wsl, z1gb[usl][0:rows, :],
                                          OP.subtract)
                w_p.fill_inc(i1, uj, 1)
            w_p.p_done(uj)
            if j == 7:
                q_p.c_done(ui, mark=w_p.fill_mark(uj))
                sc_p.c_done(ui, mark=w_p.fill_mark(uj))

            @blk.sync
            def _(sync):
                w_p.c_wait(sync, uj)
                i1 = cache_tile_out(sync, wb[uj % 4], rows, j)
                s_cwS[uj % 4].inc(i1, 16)

            consume_j(uj, j)

    def emit_ffn_pass(fbx, ln, src_d, dst_ap, dequant):
        xt, hbuf = fbx["xt"], fbx["h"]
        wb, wdb, sgact = fbx["wb"], fbx["wdb"], fbx["sgact"]
        xres, xout = fbx["xres"], fbx["xout"]

        for ch in range(1 if (DBG and dequant) else 2):
            tok0 = ch * C
            if ch == 1:
                @blk.scalar
                def _(scalar):
                    scalar.wait_ge(w_p.free.h, w_p.free.n)

                @blk.vector
                def _(vector):
                    vector.wait_ge(wd_p.free.h, wd_p.free.n)

                @blk.tensor
                def _(tensor):
                    tensor.wait_ge(xo_p.fill.h, xo_p.fill.n)
            guard = (xo_p.fill.h, xo_p.fill.n)
            emit_ln_xt(src_d, xt, tok0, C // 128, ln, psum_guard=guard)
            xt_ready = s_xt.n

            for fc in range(43):
                fci = cnt["fci"]
                cnt["fci"] += 1
                pp = [bank_f32(4 * (fci % 2) + i) for i in range(4)]
                for R in range(4):
                    ui = cnt["ui"]
                    cnt["ui"] += 1
                    gcol, ucol = fc * 256, INTER + fc * 256

                    def consume_j(uj, j, fc=fc, R=R, fci=fci, pp=pp,
                                  xt_ready=xt_ready):
                        @blk.tensor
                        def _(tensor):
                            w_p.c_wait(tensor, uj)
                            if R == 0 and j == 0:
                                if fc == 0:
                                    tensor.wait_ge(s_xt.h, xt_ready)
                                pg_p.p_wait(tensor, fci)
                            t = j * 4 + R
                            i1 = None
                            for fm in range(4):
                                i1 = tensor.matmul(
                                    pp[fm],
                                    wb[uj % 4][:, fm * 128:(fm + 1) * 128],
                                    xt[:, t * C:(t + 1) * C],
                                    start=(R == 0 and j == 0),
                                    stop=(R == 3 and j == 7))
                            w_p.free.inc(i1, 1)
                        if dequant:
                            w_p.c_done(uj, mark=[
                                (w_p.free.h, w_p.free.n),
                                (s_cwS[uj % 4].h, s_cwS[uj % 4].n)])
                        else:
                            w_p.c_done(uj)
                        if R == 3 and j == 7:
                            pg_p.p_done(fci, mark=(w_p.free.h, w_p.free.n))

                    if dequant:
                        def qload(sync, qtile, R=R, gcol=gcol, ucol=ucol):
                            i1 = sync.dma_start(
                                out=qtile[:, 0:256],
                                in_=qwgu_d[R * 128:(R + 1) * 128,
                                           gcol:gcol + 256])
                            q_p.fill_inc(i1, ui, 16)
                            i2 = sync.dma_start(
                                out=qtile[:, 256:512],
                                in_=qwgu_d[R * 128:(R + 1) * 128,
                                           ucol:ucol + 256])
                            q_p.fill_inc(i2, ui, 16)

                        def scload(scalar, sg, z1, R=R, gcol=gcol, ucol=ucol):
                            for hl, cc in ((0, gcol), (1, ucol)):
                                for dst, tbl in ((sg, scgub_d), (z1, zsgu_d)):
                                    i1 = scalar.dma_start(
                                        out=dst[:, hl * 256:(hl + 1) * 256],
                                        in_=tbl[8 * R:8 * R + 8, cc:cc + 256]
                                            .unsqueeze(1)
                                            .to_broadcast([8, 16, 256]))
                                    sc_p.fill_inc(i1, ui, 16)

                        def cache_out(sync, wtile, rows, j, fc=fc, R=R):
                            t = j * 4 + R
                            row = (fc * 32 + t) * 128
                            return sync.dma_start(out=cgu_d[row:row + 128, :],
                                                  in_=wtile[:])

                        emit_dequant_unit(fbx, ui, 128, qload, scload,
                                          cache_out, consume_j)
                    else:
                        for j in range(8):
                            uj = cnt["uj"]
                            cnt["uj"] += 1

                            @blk.sync
                            def _(sync):
                                w_p.p_wait(sync, uj)
                                t = j * 4 + R
                                row = (fc * 32 + t) * 128
                                i1 = sync.dma_start(
                                    out=wb[uj % 4][:],
                                    in_=cgu_d[row:row + 128, :])
                                w_p.fill_inc(i1, uj, 16)
                            w_p.p_done(uj)
                            consume_j(uj, j)

                @blk.scalar
                def _(scalar):
                    pg_p.c_wait(scalar, fci)
                    sil_p.p_wait(scalar, fci)
                    i0 = scalar.activation(sgact[fci % 2][:, 0:C], pp[0],
                                           AF.Silu)
                    vchain(scalar, i0, s_sch)
                    i1 = scalar.activation(sgact[fci % 2][:, C:2 * C], pp[1],
                                           AF.Silu)
                    sil_p.fill.inc(i1, 1)
                sil_p.p_done(fci)

                @blk.vector
                def _(vector):
                    sil_p.c_wait(vector, fci)
                    vector.tensor_tensor(
                        hbuf[:, (2 * fc) * C:(2 * fc + 1) * C],
                        sgact[fci % 2][:, 0:C], pp[2], OP.mult)
                    i1 = vector.tensor_tensor(
                        hbuf[:, (2 * fc + 1) * C:(2 * fc + 2) * C],
                        sgact[fci % 2][:, C:2 * C], pp[3], OP.mult)
                    s_h.inc(i1, 1)
                sil_p.c_done(fci, mark=(s_h.h, s_h.n))
                pg_p.c_done(fci, mark=(s_h.h, s_h.n))

            if dequant and ch == 0:
                for R in range(11):
                    rows = 128 if R < 10 else 96
                    ng = rows // 16
                    for nch in range(8):
                        ui = cnt["ui"]
                        cnt["ui"] += 1

                        def qload(sync, qtile, R=R, nch=nch, rows=rows):
                            i1 = sync.dma_start(
                                out=qtile[0:rows, :],
                                in_=qwdn_d[R * 128:R * 128 + rows,
                                           nch * 512:(nch + 1) * 512])
                            q_p.fill_inc(i1, ui, 16)

                        def scload(scalar, sg, z1, R=R, nch=nch, rows=rows,
                                   ng=ng):
                            for dst, tbl in ((sg, scdnb_d), (z1, zsdn_d)):
                                i1 = scalar.dma_start(
                                    out=dst[0:rows, 0:512],
                                    in_=tbl[8 * R:8 * R + ng,
                                            nch * 512:(nch + 1) * 512]
                                        .unsqueeze(1)
                                        .to_broadcast([ng, 16, 512]))
                                sc_p.fill_inc(i1, ui, 16)

                        def cache_out(sync, wtile, rows, j, R=R, nch=nch):
                            return sync.dma_start(
                                out=cdn_d.rearrange("(qq e) c -> qq e c", e=8)
                                    [R * 128:R * 128 + rows, j,
                                     nch * 512:(nch + 1) * 512],
                                in_=wtile[0:rows, :])

                        def consume_j(uj, j):
                            w_p.c_done(uj, mark=(s_cwS[uj % 4].h,
                                                 s_cwS[uj % 4].n))

                        emit_dequant_unit(fbx, ui, rows, qload, scload,
                                          cache_out, consume_j)
                fbx["dn_mark"] = [(s.h, s.n) for s in s_cwS]

            @blk.scalar
            def _(scalar):
                for hh, vv in fbx.get("dn_mark") or []:
                    if vv > 0:
                        scalar.wait_ge(hh, vv)

            for n in range(8):
                di = cnt["di"]
                cnt["di"] += 1
                ppd = [bank_f32(4 * (di % 2) + i) for i in range(4)]
                for kf in range(86):
                    wdi = cnt["wdi"]
                    cnt["wdi"] += 1

                    @blk.scalar
                    def _(scalar):
                        wd_p.p_wait(scalar, wdi)
                        i1 = scalar.dma_start(
                            out=wdb[wdi % 2][:],
                            in_=cdn_d[kf * 128:(kf + 1) * 128,
                                      n * 512:(n + 1) * 512])
                        wd_p.fill_inc(i1, wdi, 16)
                    wd_p.p_done(wdi)

                    @blk.tensor
                    def _(tensor):
                        wd_p.c_wait(tensor, wdi)
                        if kf == 0:
                            s_h.wait(tensor)
                            pd_p.p_wait(tensor, di)
                        i1 = None
                        for m in range(4):
                            i1 = tensor.matmul(
                                ppd[m],
                                hbuf[:, kf * C + m * 128:kf * C + (m + 1) * 128],
                                wdb[wdi % 2][:],
                                start=(kf == 0), stop=(kf == 85))
                        wd_p.free.inc(i1, 1)
                    wd_p.c_done(wdi)
                    if kf == 85:
                        pd_p.p_done(di, mark=(wd_p.free.h, wd_p.free.n))

                for m in range(4):
                    ri = cnt["ri"]
                    cnt["ri"] += 1

                    @blk.sync
                    def _(sync):
                        xr_p.p_wait(sync, ri)
                        i1 = sync.dma_start(
                            out=xres[ri % 2][:],
                            in_=src_d[tok0 + m * 128:tok0 + (m + 1) * 128,
                                      n * 512:(n + 1) * 512])
                        xr_p.fill_inc(i1, ri, 16)
                    xr_p.p_done(ri)

                    @blk.vector
                    def _(vector):
                        if m == 0:
                            pd_p.c_wait(vector, di)
                        xr_p.c_wait(vector, ri)
                        xo_p.p_wait(vector, ri)
                        i1 = vector.scalar_tensor_tensor(
                            xout[ri % 2][:], ppd[m], 0.5, xres[ri % 2][:],
                            OP.mult, OP.add)
                        xo_p.fill.inc(i1, 1)
                    xo_p.p_done(ri)
                    xr_p.c_done(ri, mark=(xo_p.fill.h, xo_p.fill.n))
                    if m == 3:
                        pd_p.c_done(di, mark=(xo_p.fill.h, xo_p.fill.n))

                    @blk.sync
                    def _(sync):
                        xo_p.c_wait(sync, ri)
                        i1 = sync.dma_start(
                            out=dst_ap[tok0 + m * 128:tok0 + (m + 1) * 128,
                                       n * 512:(n + 1) * 512],
                            in_=xout[ri % 2][:])
                        xoe[ri % 2].inc(i1, 16)
                    xo_p.c_done(ri, mark=(xoe[ri % 2].h, xoe[ri % 2].n))

    def emit_attention(src_d, dst_ap, qt, kts, vsb):
        with ExitStack() as sa:
            sb = lambda n, s, d: sa.enter_context(nc.sbuf_tensor(n, s, d))
            ln = dict(
                lnx=[sb("a_lnx0", [128, H], F32), sb("a_lnx1", [128, H], F32)],
                lnP=[sb("a_lnP0", [128, H], BF16), sb("a_lnP1", [128, H], BF16)],
                st1=sb("a_st1", [128, 1], F32), nm=sb("a_nm", [128, 1], F32),
                vr=sb("a_vr", [128, 1], F32),
            )
            xt = sb("a_xt", [128, 32 * T], BF16)
            wstg = [sb("a_ws0", [128, 512], F32), sb("a_ws1", [128, 512], F32)]
            wbf = [sb("a_wb0", [128, 512], BF16), sb("a_wb1", [128, 512], BF16)]
            kvout = sb("a_kvout", [128, 2048], BF16)

            emit_ln_xt(src_d, xt, 0, T // 128, ln, psum_guard=None)
            xt_ready = s_xt.n

            mgroups = [list(range(g * 4, min(34, g * 4 + 4))) for g in range(9)]
            for gi, mg in enumerate(mgroups):
                qi = cnt["qi"]
                cnt["qi"] += 1
                for k in range(32):
                    wqi = cnt["wqi"]
                    cnt["wqi"] += 1
                    jph, R = k // 4, k % 4
                    c0, c1 = mg[0] * 128, (mg[-1] + 1) * 128

                    @blk.sync
                    def _(sync):
                        wq_p.p_wait(sync, wqi)
                        i1 = sync.dma_start(
                            out=wstg[wqi % 2][:, 0:c1 - c0],
                            in_=wqkv_d.rearrange("(r j) c -> r j c", j=8)
                                [R * 128:(R + 1) * 128, jph, c0:c1])
                        wq_p.fill_inc(i1, wqi, 16)
                    wq_p.p_done(wqi)

                    @blk.vector
                    def _(vector):
                        wq_p.c_wait(vector, wqi)
                        i1 = vector.tensor_copy(wbf[wqi % 2][:, 0:c1 - c0],
                                                wstg[wqi % 2][:, 0:c1 - c0])
                        s_misc.inc(i1, 1)

                    @blk.tensor
                    def _(tensor):
                        s_misc.wait(tensor)
                        if k == 0:
                            if gi == 0:
                                tensor.wait_ge(s_xt.h, xt_ready)
                            qkv_p.p_wait(tensor, qi)
                        i1 = None
                        for mi, m in enumerate(mg):
                            for chh in range(2):
                                i1 = tensor.matmul(
                                    bank_f32(mi * 2 + chh),
                                    wbf[wqi % 2][:, mi * 128:(mi + 1) * 128],
                                    xt[:, k * T + chh * 512:
                                       k * T + (chh + 1) * 512],
                                    start=(k == 0), stop=(k == 31))
                        wq_p.free.inc(i1, 1)
                    wq_p.c_done(wqi)
                    if k == 31:
                        qkv_p.p_done(qi, mark=(wq_p.free.h, wq_p.free.n))

                @blk.scalar
                def _(scalar):
                    qkv_p.c_wait(scalar, qi)
                    i1 = None
                    for mi, m in enumerate(mg):
                        for chh in range(2):
                            src = bank_f32(mi * 2 + chh)
                            col = chh * 512
                            if i1 is not None:
                                vchain(scalar, i1, s_sch)
                            if m < 32:
                                i1 = scalar.activation(
                                    qt[:, m * T + col: m * T + col + 512],
                                    src, AF.Copy, scale=SCALE)
                            elif m == 32:
                                i1 = scalar.copy(
                                    kts[:, 2048 + col:2048 + col + 512], src)
                            else:
                                i1 = scalar.copy(
                                    vsb[:, 2048 + col:2048 + col + 512], src)
                    s_attS.inc(i1, 1)
                qkv_p.c_done(qi, mark=(s_attS.h, s_attS.n))

            for i in range(8):
                for which, srcbuf in ((0, kts), (1, vsb)):
                    ki = cnt["ki"]
                    cnt["ki"] += 1

                    @blk.tensor
                    def _(tensor):
                        if i == 0 and which == 0:
                            s_attS.wait(tensor)
                        pt2_p.p_wait(tensor, ki)
                        i1 = tensor.transpose(
                            bank_bf16_tile(ki % 2),
                            srcbuf[:, 2048 + i * 128:2048 + (i + 1) * 128],
                            identbf[:])
                        pt2_p.fill.inc(i1, 1)
                    pt2_p.p_done(ki)

                    @blk.vector
                    def _(vector):
                        pt2_p.c_wait(vector, ki)
                        i1 = vector.tensor_copy(
                            kvout[:, i * 256 + which * 128:
                                  i * 256 + (which + 1) * 128],
                            bank_bf16_tile(ki % 2))
                        s_attV.inc(i1, 1)
                    pt2_p.c_done(ki, mark=(s_attV.h, s_attV.n))

            @blk.gpsimd
            def _(gpsimd):
                s_attS.wait(gpsimd)
                s_attV.wait(gpsimd)
                pv = gpsimd.value_load(par_sb[0:1, 0:1])
                for i in range(8):
                    i1 = gpsimd.dma_start(
                        out=kvsh_d[bass.ts(pv * 8 + i, 128), :],
                        in_=kvout[:, i * 256:(i + 1) * 256])
                    s_misc.inc(i1, 16)
                s_misc.wait(gpsimd)
                with gpsimd.register64("pa64") as pa64:
                    gpsimd.reg_load(pa64, paddr_sb[0:1, 0:2])
                    gpsimd.store(pa64, 16)
                s_kvbar.n += 16
                s_kvbar.wait(gpsimd)
                i1 = gpsimd.dma_start(
                    out=vsb[:, 0:2048],
                    in_=kvsh_d.rearrange("(t p) d -> p t d", p=128)
                        [:, :, 128:256])
                s_misc.inc(i1, 16)
                i1 = gpsimd.dma_start(
                    out=kvout[:],
                    in_=kvsh_d.rearrange("(t p) d -> p t d", p=128)
                        [:, :, 0:128])
                s_misc.inc(i1, 16)

            for i in range(16):
                ki = cnt["ki"]
                cnt["ki"] += 1

                @blk.tensor
                def _(tensor):
                    if i == 0:
                        s_misc.wait(tensor)
                    pt2_p.p_wait(tensor, ki)
                    i1 = tensor.transpose(
                        bank_bf16_tile(ki % 2),
                        kvout[:, i * 128:(i + 1) * 128], identbf[:])
                    pt2_p.fill.inc(i1, 1)
                pt2_p.p_done(ki)

                @blk.vector
                def _(vector):
                    pt2_p.c_wait(vector, ki)
                    i1 = vector.tensor_copy(kts[:, i * 128:(i + 1) * 128],
                                            bank_bf16_tile(ki % 2))
                    s_attV.inc(i1, 1)
                pt2_p.c_done(ki, mark=(s_attV.h, s_attV.n))

        markA = (s_attS.n, s_attV.n)
        with ExitStack() as sbp:
            @blk.sync
            def _(sync):
                sync.wait_ge(s_attV.h, markA[1])

            @blk.scalar
            def _(scalar):
                scalar.wait_ge(s_attV.h, markA[1])

            sb = lambda n, s, d: sbp.enter_context(nc.sbuf_tensor(n, s, d))
            attnT = sb("b_at", [128, 32 * T], BF16)
            probs = [sb("b_pr0", [128, 2048], BF16),
                     sb("b_pr1", [128, 2048], BF16)]
            probsT = sb("b_pt", [128, 2048], BF16)
            gneg = sb("b_gneg", [128, 16 * 128], F32)
            augq = sb("b_aq", [2, T], F32)
            augk = sb("b_ak", [2, 2048], F32)
            nmv = sb("b_nmv", [128, 1], F32)
            sums = sb("b_sums", [128, 1], F32)
            rsum = sb("b_rsum", [128, 1], F32)
            wstg = [sb("b_ws0", [128, 512], F32), sb("b_ws1", [128, 512], F32)]
            wbf = [sb("b_wb0", [128, 512], BF16), sb("b_wb1", [128, 512], BF16)]
            xres = [sb("b_xr0", [128, 512], F32), sb("b_xr1", [128, 512], F32)]
            xout = [sb("b_xo0", [128, 512], F32), sb("b_xo1", [128, 512], F32)]

            @blk.vector
            def _(vector):
                s_attV.wait(vector)
                i1 = None
                for qb in range(8):
                    for which, kv0 in ((0, qb * 128), (1, 1024 + qb * 128)):
                        tmp = gneg[:, (qb * 2 + which) * 128:
                                   (qb * 2 + which + 1) * 128]
                        i0 = vector.tensor_scalar(
                            tmp, poskb[:, kv0:kv0 + 128],
                            posq_pp[:, qb:qb + 1], None, OP.subtract)
                        vchain(vector, i0, s_vch)
                        i1 = vector.tensor_scalar(tmp, tmp, 0.0, NEGBIG,
                                                  OP.is_gt, OP.mult)
                s_attV.inc(i1, 1)

            @blk.tensor
            def _(tensor):
                s_attS.wait(tensor)
                s_attV.wait(tensor)
                s_misc.wait(tensor)

            for h in range(NH):
                ai = cnt["ai"]
                cnt["ai"] += 1

                @blk.sync
                def _(sync):
                    aug_p.p_wait(sync, ai)
                    i1 = sync.dma_start(out=augq[:],
                                        in_=augq_d[:, h * T:(h + 1) * T])
                    i2 = sync.dma_start(out=augk[:],
                                        in_=augk_d[:, h * 2048:(h + 1) * 2048])
                    aug_p.fill_inc(i1, ai, 16)
                    aug_p.fill_inc(i2, ai, 16)
                aug_p.p_done(ai)

                for qb in range(8):
                    sblk = qb
                    w1 = 128 * (sblk + 1)
                    wtot = 2 * w1
                    si = cnt["si"]
                    cnt["si"] += 1
                    r0 = (si % 2) * 2048

                    def rg(cc, w):
                        return Pap[:, r0 + cc:r0 + cc + w]

                    @blk.tensor
                    def _(tensor):
                        if qb == 0:
                            aug_p.c_wait(tensor, ai)
                        ssc_p.p_wait(tensor, si)
                        lhs = qt[:, h * T + qb * 128: h * T + (qb + 1) * 128]
                        first = True
                        for (cc0, k0) in ((0, 0), (w1, 1024)):
                            off = 0
                            while off < w1:
                                nn = min(512, w1 - off)
                                tensor.matmul(rg(cc0 + off, nn), lhs,
                                              kts[:, k0 + off:k0 + off + nn],
                                              start=first, stop=False)
                                first = False
                                off += nn
                        i1 = None
                        for (cc0, k0) in ((0, 0), (w1, 1024)):
                            off = 0
                            while off < w1:
                                nn = min(512, w1 - off)
                                last = (cc0 == w1 and off + nn >= w1)
                                i1 = tensor.matmul(
                                    rg(cc0 + off, nn),
                                    augq[:, qb * 128:(qb + 1) * 128],
                                    augk[:, k0 + off:k0 + off + nn],
                                    start=False, stop=last)
                                off += nn
                        ssc_p.fill.inc(i1, 1)
                    ssc_p.p_done(si)
                    if qb == 7:
                        aug_p.c_done(ai, mark=(ssc_p.fill.h, ssc_p.fill.n))

                    @blk.vector
                    def _(vector):
                        ssc_p.c_wait(vector, si)
                        vector.tensor_tensor(
                            rg(sblk * 128, 128), rg(sblk * 128, 128),
                            gneg[:, (qb * 2) * 128:(qb * 2 + 1) * 128], OP.add)
                        i0 = vector.tensor_tensor(
                            rg(w1 + sblk * 128, 128), rg(w1 + sblk * 128, 128),
                            gneg[:, (qb * 2 + 1) * 128:(qb * 2 + 2) * 128],
                            OP.add)
                        vchain(vector, i0, s_vch)
                        i1 = vector.tensor_reduce(nmv[:], rg(0, wtot), AX.X,
                                                  OP.max, negate=True)
                        s_attV.inc(i1, 1)

                    @blk.scalar
                    def _(scalar):
                        s_attV.wait(scalar)
                        prb_p.p_wait(scalar, si)
                        i1 = scalar.activation(
                            probs[si % 2][:, 0:wtot], rg(0, wtot), AF.Exp,
                            bias=nmv[:], scale=1.0, accum_out=sums[:])
                        s_attS.inc(i1, 1)

                    @blk.vector
                    def _(vector):
                        s_attS.wait(vector)
                        i0 = vector.reciprocal(rsum[:], sums[:])
                        vchain(vector, i0, s_vch)
                        i1 = vector.tensor_scalar(
                            probs[si % 2][:, 0:wtot], probs[si % 2][:, 0:wtot],
                            rsum[:], None, OP.mult)
                        prb_p.fill.inc(i1, 1)
                    prb_p.p_done(si)

                    nkt = wtot // 128
                    pbase = cnt["pbi"]
                    for ktile in range(nkt):
                        ki = cnt["ki"]
                        cnt["ki"] += 1
                        pbi = cnt["pbi"]
                        cnt["pbi"] += 1
                        tslot = Pbf_ap[:, (si % 2) * 4096 + (ki % 2) * 1024:
                                       (si % 2) * 4096 + (ki % 2) * 1024 + 128]

                        @blk.tensor
                        def _(tensor):
                            if ktile == 0:
                                prb_p.c_wait(tensor, si)
                            pt2_p.p_wait(tensor, ki)
                            i1 = tensor.transpose(
                                tslot,
                                probs[si % 2][:, ktile * 128:(ktile + 1) * 128],
                                identbf[:])
                            pt2_p.fill.inc(i1, 1)
                        pt2_p.p_done(ki)

                        @blk.scalar
                        def _(scalar):
                            pt2_p.c_wait(scalar, ki)
                            i1 = scalar.copy(
                                probsT[:, ktile * 128:(ktile + 1) * 128], tslot)
                            pbt_p.fill.inc(i1, 1)
                        pbt_p.p_done(pbi)
                        pt2_p.c_done(ki, mark=(pbt_p.fill.h, pbt_p.fill.n))

                    for ktile in range(nkt):
                        pbi = pbase + ktile

                        @blk.tensor
                        def _(tensor):
                            pbt_p.c_wait(tensor, pbi)
                            if ktile <= sblk:
                                vcol = ktile * 128
                            else:
                                vcol = 1024 + (ktile - (sblk + 1)) * 128
                            i1 = tensor.matmul(
                                rg(1024, 128),
                                vsb[:, vcol:vcol + 128],
                                probsT[:, ktile * 128:(ktile + 1) * 128],
                                start=(ktile == 0), stop=(ktile == nkt - 1))
                            if ktile == nkt - 1:
                                s_pv.inc(i1, 1)
                        pbt_p.c_done(pbi)
                    prb_p.c_done(si, mark=(s_pv.h, s_pv.n))

                    @blk.vector
                    def _(vector):
                        s_pv.wait(vector)
                        i1 = vector.tensor_copy(
                            attnT[:, h * T + qb * 128: h * T + (qb + 1) * 128],
                            rg(1024, 128))
                        s_attV.inc(i1, 1)
                    ssc_p.c_done(si, mark=(s_attV.h, s_attV.n))

            @blk.tensor
            def _(tensor):
                s_attV.wait(tensor)

            for ch in range(2):
                for n in range(8):
                    oi = cnt["oi"]
                    cnt["oi"] += 1
                    ppq = [bank_f32(4 * (oi % 2) + i) for i in range(4)]
                    for k in range(32):
                        wqi = cnt["wqi"]
                        cnt["wqi"] += 1

                        @blk.sync
                        def _(sync):
                            wq_p.p_wait(sync, wqi)
                            i1 = sync.dma_start(
                                out=wstg[wqi % 2][:],
                                in_=wo_d[k * 128:(k + 1) * 128,
                                         n * 512:(n + 1) * 512])
                            wq_p.fill_inc(i1, wqi, 16)
                        wq_p.p_done(wqi)

                        @blk.vector
                        def _(vector):
                            wq_p.c_wait(vector, wqi)
                            i1 = vector.tensor_copy(wbf[wqi % 2][:, 0:512],
                                                    wstg[wqi % 2][:, 0:512])
                            s_misc.inc(i1, 1)

                        @blk.tensor
                        def _(tensor):
                            s_misc.wait(tensor)
                            if k == 0:
                                pwo_p.p_wait(tensor, oi)
                            i1 = None
                            for m in range(4):
                                i1 = tensor.matmul(
                                    ppq[m],
                                    attnT[:, k * T + ch * 512 + m * 128:
                                          k * T + ch * 512 + (m + 1) * 128],
                                    wbf[wqi % 2][:, 0:512],
                                    start=(k == 0), stop=(k == 31))
                            wq_p.free.inc(i1, 1)
                        wq_p.c_done(wqi)
                        if k == 31:
                            pwo_p.p_done(oi, mark=(wq_p.free.h, wq_p.free.n))

                    for m in range(4):
                        ri = cnt["ri"]
                        cnt["ri"] += 1

                        @blk.sync
                        def _(sync):
                            xr_p.p_wait(sync, ri)
                            i1 = sync.dma_start(
                                out=xres[ri % 2][:],
                                in_=src_d[ch * 512 + m * 128:
                                          ch * 512 + (m + 1) * 128,
                                          n * 512:(n + 1) * 512])
                            xr_p.fill_inc(i1, ri, 16)
                        xr_p.p_done(ri)

                        @blk.vector
                        def _(vector):
                            if m == 0:
                                pwo_p.c_wait(vector, oi)
                            xr_p.c_wait(vector, ri)
                            xo_p.p_wait(vector, ri)
                            i1 = vector.tensor_tensor(
                                xout[ri % 2][:], ppq[m], xres[ri % 2][:],
                                OP.add)
                            xo_p.fill.inc(i1, 1)
                        xo_p.p_done(ri)
                        xr_p.c_done(ri, mark=(xo_p.fill.h, xo_p.fill.n))
                        if m == 3:
                            pwo_p.c_done(oi, mark=(xo_p.fill.h, xo_p.fill.n))

                        @blk.sync
                        def _(sync):
                            xo_p.c_wait(sync, ri)
                            i1 = sync.dma_start(
                                out=dst_ap[ch * 512 + m * 128:
                                           ch * 512 + (m + 1) * 128,
                                           n * 512:(n + 1) * 512],
                                in_=xout[ri % 2][:])
                            xoe[ri % 2].inc(i1, 16)
                        xo_p.c_done(ri, mark=(xoe[ri % 2].h, xoe[ri % 2].n))

    def make_ffn(scope, pref):
        sb = lambda n, s, d: scope.enter_context(nc.sbuf_tensor(pref + n, s, d))
        return dict(
            xt=sb("xt", [128, 32 * C], BF16),
            h=sb("h", [128, 86 * C], BF16),
            qb=[sb("qb0", [128, 512], U32), sb("qb1", [128, 512], U32)],
            sgb=[sb("sg0", [128, 512], BF16), sb("sg1", [128, 512], BF16)],
            z1gb=[sb("z10", [128, 512], BF16), sb("z11", [128, 512], BF16)],
            ub=[sb("ub0", [128, 512], U32), sb("ub1", [128, 512], U32)],
            wb=[sb("wb0", [128, 512], BF16), sb("wb1", [128, 512], BF16),
                sb("wb2", [128, 512], BF16), sb("wb3", [128, 512], BF16)],
            wdb=[sb("wdb0", [128, 512], BF16), sb("wdb1", [128, 512], BF16)],
            sgact=[sb("sga", [128, 2 * C], BF16), sb("sgc", [128, 2 * C], BF16)],
            xres=[sb("xr0", [128, 512], F32), sb("xr1", [128, 512], F32)],
            xout=[sb("xo0", [128, 512], F32), sb("xo1", [128, 512], F32)],
            dn_mark=0,
        )

    def make_ln(scope, pref):
        sb = lambda n, s, d: scope.enter_context(nc.sbuf_tensor(pref + n, s, d))
        return dict(
            lnx=[sb("lnx0", [128, H], F32), sb("lnx1", [128, H], F32)],
            lnP=[sb("lnP0", [128, H], BF16), sb("lnP1", [128, H], BF16)],
            st1=sb("st1", [128, 1], F32), nm=sb("nm", [128, 1], F32),
            vr=sb("vr", [128, 1], F32),
        )

    with ExitStack() as scope:
        ln = make_ln(scope, "f_")
        fbx = make_ffn(scope, "f_")
        dbg_fbx = fbx

        @blk.sync
        def _(sync):
            sync.wait_ge(s_zw.h, zdone)

        @blk.scalar
        def _(scalar):
            scalar.wait_ge(s_zw.h, zdone)
            s_misc.wait(scalar, 1)

        dst1 = out_d if STAGE == 1 else xs1_d
        emit_ffn_pass(fbx, ln, x0_d, dst1, True)
        if DBG:
            s_dbg = sem("s_dbg")

            @blk.sync
            def _(sync):
                xoe[0].wait(sync)
                xoe[1].wait(sync)
                sync.dma_start(out=dbg_xt_d, in_=dbg_fbx["xt"][:]).then_inc(
                    s_dbg.h, 16)
                sync.dma_start(out=dbg_h_d, in_=dbg_fbx["h"][:]).then_inc(
                    s_dbg.h, 16)
                sync.dma_start(out=dbg_cgu_d,
                               in_=cgu_d[0:1024, :]).then_inc(s_dbg.h, 16)
                sync.dma_start(out=dbg_cdn_d,
                               in_=cdn_d[0:1024, :]).then_inc(s_dbg.h, 16)
                sync.wait_ge(s_dbg.h, 64)
            s_dbg.n += 64

    if STAGE >= 2:
        fence_all()
        with ExitStack() as scope:
            sb = lambda n, s, d: scope.enter_context(nc.sbuf_tensor(n, s, d))
            qt = sb("at_qt", [128, 32 * T], BF16)
            kts = sb("at_kt", [128, 3072], BF16)
            vsb = sb("at_v", [128, 3072], BF16)
            dst2 = out_d if STAGE == 2 else xs2_d
            emit_attention(xs1_d, dst2, qt, kts, vsb)

    if STAGE >= 3:
        fence_all()
        with ExitStack() as scope:
            ln = make_ln(scope, "g_")
            fbx = make_ffn(scope, "g_")
            emit_ffn_pass(fbx, ln, xs2_d, out_d, False)

    @blk.sync
    def _(sync):
        xoe[0].wait(sync)
        xoe[1].wait(sync)

    es.close()
    return nc, kvbar_num


_cache = {}


def kernel(**inputs):
    if "nc" not in _cache:
        _cache["nc"], _cache["kvbar"] = build_program()
    nc, kvbar_num = _cache["nc"], _cache["kvbar"]

    x = np.ascontiguousarray(inputs["L0_attn_hidden_states"])
    slopes = _alibi_slopes()

    pos_kv = np.concatenate(
        [np.concatenate([np.arange(g * 128, (g + 1) * 128)
                         for g in BLOCKS[0]]),
         np.concatenate([np.arange(g * 128, (g + 1) * 128)
                         for g in BLOCKS[1]])]).astype(np.float32)
    augk = np.empty((2, NH * 2048), np.float32)
    for h in range(NH):
        augk[0, h * 2048:(h + 1) * 2048] = slopes[h] * pos_kv
        augk[1, h * 2048:(h + 1) * 2048] = 1.0

    in_maps = []
    shard_rows = {}
    for c in range(8):
        b, par = c // 2, c & 1
        rows = np.concatenate([np.arange(g * 128, (g + 1) * 128)
                               for g in BLOCKS[par]])
        shard_rows[c] = rows
        pos_own = rows.astype(np.float32)
        posq_pp = np.ascontiguousarray(pos_own.reshape(8, 128).T)
        augq = np.empty((2, NH * T), np.float32)
        for h in range(NH):
            augq[0, h * T:(h + 1) * T] = 1.0
            augq[1, h * T:(h + 1) * T] = -slopes[h] * pos_own
        addr = TPB_BASES[L2P[c ^ 1]] + SEM_INC_REL + 4 * kvbar_num
        paddr = np.array([[addr & 0xFFFFFFFF, addr >> 32]], np.uint32)
        in_maps.append(dict(
            x0=np.ascontiguousarray(x[b, rows, :]),
            wqkv=inputs["L0_attn_Wqkv"],
            wo=inputs["L0_attn_Wo"],
            qwgu=inputs["L0_ffn_qweight_gu"],
            qzgu=inputs["L0_ffn_qzeros_gu"],
            scgu=inputs["L0_ffn_scales_gu"],
            qwdn=inputs["L0_ffn_qweight_down"],
            qzdn=inputs["L0_ffn_qzeros_down"],
            scdn=inputs["L0_ffn_scales_down"],
            posq=posq_pp,
            poskv=pos_kv.reshape(1, 2048),
            augq=augq,
            augk=augk,
            ident=np.eye(128, dtype=np.float32),
            par=np.array([[par]], np.int32),
            paddr=paddr,
        ))

    res = run_bass_kernel_spmd(nc, in_maps, core_ids=list(range(8)))
    out = np.empty((B, S, H), np.float32)
    for c in range(8):
        out[c // 2, shard_rows[c], :] = np.asarray(res.results[c]["out"])
    return out


# version 02
# speedup vs baseline: 1.5396x; 1.5396x over previous
import numpy as np
from contextlib import ExitStack

import concourse.bass as bass
import concourse.mybir as mybir
from concourse.alu_op_type import AluOpType as OP
from concourse.bass_utils import run_bass_kernel_spmd

F32 = mybir.dt.float32
BF16 = mybir.dt.bfloat16
U32 = mybir.dt.uint32
I32 = mybir.dt.int32
AF = mybir.ActivationFunctionType
AX = mybir.AxisListType

B, S, H = 4, 2048, 4096
NH, D = 32, 128
INTER = 11008
F2 = 2 * INTER
NQKV = H + 2 * D
T = 1024
C = 512
EPS = 1e-5
SCALE = float(np.float32(D) ** -0.5)
NEGBIG = -1e30

BLOCKS = {0: [0, 3, 4, 7, 8, 11, 12, 15], 1: [1, 2, 5, 6, 9, 10, 13, 14]}
SEM_INC_REL = 0x802701800
L2P = [4, 5, 6, 7, 2, 3, 0, 1]
TPB_BASES = [0x2000000000, 0x3000000000, 0x6000000000, 0x7000000000,
             0x802000000000, 0x803000000000, 0x806000000000, 0x807000000000]

STAGE = 3
DBG = False


def _alibi_slopes():
    import math
    n = NH
    cp2 = 2 ** math.floor(math.log2(n))
    base = 2.0 ** (-(2.0 ** (-(math.log2(cp2) - 3))))
    slopes = base ** np.arange(1, cp2 + 1, dtype=np.float64)
    if cp2 != n:
        extra_base = 2.0 ** (-(2.0 ** (-(math.log2(2 * cp2) - 3))))
        rem = min(cp2, n - cp2)
        extra = extra_base ** np.arange(1, 1 + 2 * rem, 2, dtype=np.float64)
        slopes = np.concatenate([slopes, extra])
    return slopes.astype(np.float32)


class Sem:
    def __init__(self, es, nc, name):
        self.h = es.enter_context(nc.semaphore(name))
        self.n = 0

    def inc(self, inst, k=1):
        inst.then_inc(self.h, k)
        self.n += k

    def wait(self, eng, n=None):
        v = self.n if n is None else n
        if v > 0:
            eng.wait_ge(self.h, v)


class Pipe:

    def __init__(self, es, nc, name, nslots, slot_fill_sems=False):
        if slot_fill_sems:
            self.fillS = [Sem(es, nc, f"{name}f{k}") for k in range(nslots)]
            self.fill = None
        else:
            self.fill = Sem(es, nc, name + "f")
            self.fillS = None
        self.free = Sem(es, nc, name + "e")
        self.n = nslots
        self.fill_marks = []
        self.free_marks = []

    def fill_inc(self, inst, i, k):
        s = self.fillS[i % self.n] if self.fillS is not None else self.fill
        s.inc(inst, k)

    def fill_mark(self, i):
        s = self.fillS[i % self.n] if self.fillS is not None else self.fill
        return (s.h, s.n)

    @staticmethod
    def _wait_marks(eng, marks):
        if isinstance(marks, tuple):
            marks = [marks]
        for h, v in marks:
            if v > 0:
                eng.wait_ge(h, v)

    def p_wait(self, eng, i):
        if i >= self.n:
            self._wait_marks(eng, self.free_marks[i - self.n])

    def p_done(self, i, mark=None):
        assert i == len(self.fill_marks), (i, len(self.fill_marks))
        self.fill_marks.append(mark if mark is not None
                               else self.fill_mark(i))

    def c_wait(self, eng, i):
        self._wait_marks(eng, self.fill_marks[i])

    def c_done(self, i, mark=None):
        while len(self.free_marks) <= i:
            self.free_marks.append(None)
        self.free_marks[i] = (mark if mark is not None
                              else (self.free.h, self.free.n))


def build_program():
    nc = bass.Bass("TRN2", target_bir_lowering=False)
    es = ExitStack()
    dt = nc.dram_tensor

    x0_d = dt("x0", [T, H], F32, kind="ExternalInput").ap()
    wqkv_d = dt("wqkv", [H, NQKV], F32, kind="ExternalInput").ap()
    wo_d = dt("wo", [H, H], F32, kind="ExternalInput").ap()
    qwgu_d = dt("qwgu", [512, F2], U32, kind="ExternalInput").ap()
    qzgu_d = dt("qzgu", [32, F2 // 8], U32, kind="ExternalInput").ap()
    scgu_d = dt("scgu", [32, F2], F32, kind="ExternalInput").ap()
    qwdn_d = dt("qwdn", [1376, H], U32, kind="ExternalInput").ap()
    qzdn_d = dt("qzdn", [86, H // 8], U32, kind="ExternalInput").ap()
    scdn_d = dt("scdn", [86, H], F32, kind="ExternalInput").ap()
    posq_d = dt("posq", [128, 8], F32, kind="ExternalInput").ap()
    poskv_d = dt("poskv", [1, 2048], F32, kind="ExternalInput").ap()
    augq_d = dt("augq", [2, NH * T], F32, kind="ExternalInput").ap()
    augk_d = dt("augk", [2, NH * 2048], F32, kind="ExternalInput").ap()
    ident_d = dt("ident", [128, 128], F32, kind="ExternalInput").ap()
    par_d = dt("par", [1, 1], I32, kind="ExternalInput").ap()
    paddr_d = dt("paddr", [1, 2], U32, kind="ExternalInput").ap()
    out_d = dt("out", [T, H], F32, kind="ExternalOutput").ap()
    if DBG:
        dbg_xt_d = dt("dbg_xt", [128, 32 * C], BF16, kind="ExternalOutput").ap()
        dbg_h_d = dt("dbg_h", [128, 86 * C], BF16, kind="ExternalOutput").ap()
        dbg_cgu_d = dt("dbg_cgu", [1024, 512], BF16, kind="ExternalOutput").ap()
        dbg_cdn_d = dt("dbg_cdn", [1024, H], BF16, kind="ExternalOutput").ap()

    xs1_d = dt("xs1", [T, H], F32, kind="Internal").ap()
    xs2_d = dt("xs2", [T, H], F32, kind="Internal").ap()
    zsgu_d = dt("zsgu", [32, F2], BF16, kind="Internal").ap()
    zsdn_d = dt("zsdn", [86, H], BF16, kind="Internal").ap()
    scgub_d = dt("scgub", [32, F2], BF16, kind="Internal").ap()
    scdnb_d = dt("scdnb", [86, H], BF16, kind="Internal").ap()
    cgu_d = dt("cgu", [43 * 32 * 128, 512], BF16, kind="Internal").ap()
    cdn_d = dt("cdn", [INTER, H], BF16, kind="Internal").ap()
    kvsh_d = dt("kvsh", [2048, 256], BF16, kind="Internal",
                addr_space="Shared").ap()

    def psbuf(name, shape, dtyp):
        return es.enter_context(nc.sbuf_tensor(name, shape, dtyp))

    identbf = psbuf("identbf", [128, 128], BF16)
    poskb = psbuf("poskb", [128, 2048], F32)
    posq_pp = psbuf("posq_pp", [128, 8], F32)
    par_sb = psbuf("par_sb", [1, 1], I32)
    paddr_sb = psbuf("paddr_sb", [1, 2], U32)

    P = es.enter_context(nc.psum_tensor("P", [128, 4096], F32))
    Pap = P[:]
    Pbf_ap = P[:].bitcast(BF16)

    def bank_f32(i, w=512):
        return Pap[:, i * 512:i * 512 + w]

    def bank_bf16_tile(i):
        return Pbf_ap[:, i * 1024:i * 1024 + 128]

    def sem(name):
        return Sem(es, nc, name)

    def pipe(name, n, slot_fill_sems=False):
        return Pipe(es, nc, name, n, slot_fill_sems=slot_fill_sems)

    s_in = sem("s_in")
    s_misc = sem("s_misc")
    s_kvbar = sem("s_kvbar")
    kvbar_num = s_kvbar.h.num

    lnx_p = pipe("lnx", 2, slot_fill_sems=True)
    lnP_p = pipe("lnP", 2)
    pT_p = pipe("pT", 2)
    s_lnv = sem("s_lnv")
    s_lns = sem("s_lns")
    s_xt = sem("s_xt")

    q_p = pipe("q", 2, slot_fill_sems=True)
    sc_p = pipe("sc", 2, slot_fill_sems=True)
    u_p = pipe("u", 2)
    w_p = pipe("w", 4, slot_fill_sems=True)
    pg_p = pipe("pg", 2)
    sil_p = pipe("sil", 2)
    s_h = sem("s_h")
    wd_p = pipe("wd", 4, slot_fill_sems=True)
    pd_p = pipe("pd", 2)
    xr_p = pipe("xr", 2, slot_fill_sems=True)
    xo_p = pipe("xo", 2)
    xoe = [sem("xoe0"), sem("xoe1")]
    s_ffn = sem("s_ffn")

    wq_p = pipe("wq", 2, slot_fill_sems=True)
    qkv_p = pipe("qkv", 1)
    pwo_p = pipe("pwo", 2)
    aug_p = pipe("aug", 1, slot_fill_sems=True)
    ssc_p = pipe("ssc", 2)
    prb_p = pipe("prb", 2)
    pt2_p = pipe("pt2", 2)
    pbt_p = pipe("pbt", 1 << 30)
    s_vch = sem("s_vch")
    s_sch = sem("s_sch")

    def vchain(eng, inst, semobj):
        semobj.inc(inst, 1)
        eng.wait_ge(semobj.h, semobj.n)

    s_attS = sem("s_attS")
    s_attV = sem("s_attV")
    s_pv = sem("s_pv")
    s_cwS = [sem(f"s_cw{k}") for k in range(4)]

    cnt = {k: 0 for k in ["li", "ti", "fci", "ui", "uj", "di", "wdi", "ri",
                          "qi", "wqi", "ki", "pbi", "ai", "si", "oi"]}

    blk = es.enter_context(nc.Block())

    def fence_all():
        for engdec in (blk.tensor, blk.vector, blk.scalar, blk.sync,
                       blk.gpsimd):
            @engdec
            def _(eng):
                xoe[0].wait(eng)
                xoe[1].wait(eng)

    with ExitStack() as zscope:
        ident_f = zscope.enter_context(nc.sbuf_tensor("ident_f", [128, 128], F32))

        @blk.sync
        def _(sync):
            sync.dma_start(out=posq_pp[:], in_=posq_d).then_inc(s_in.h, 16)
            sync.dma_start(out=par_sb[:], in_=par_d).then_inc(s_in.h, 16)
            sync.dma_start(out=paddr_sb[:], in_=paddr_d).then_inc(s_in.h, 16)
            sync.dma_start(out=ident_f[:], in_=ident_d).then_inc(s_in.h, 16)
        s_in.n += 64

        @blk.gpsimd
        def _(gpsimd):
            gpsimd.dma_start(
                out=poskb[:],
                in_=poskv_d.to_broadcast([128, 2048])).then_inc(s_in.h, 16)
            gpsimd.dma_start(out=scgub_d, in_=scgu_d).then_inc(s_in.h, 16)
            gpsimd.dma_start(out=scdnb_d, in_=scdn_d).then_inc(s_in.h, 16)
        s_in.n += 48

        @blk.vector
        def _(vector):
            s_in.wait(vector)
            vector.tensor_copy(identbf[:], ident_f[:]).then_inc(s_misc.h, 1)
        s_misc.n += 1

        s_z = sem("s_z")
        s_zc = sem("s_zc")
        s_zw = sem("s_zw")
        zq_sb = zscope.enter_context(nc.sbuf_tensor("zq_sb", [32, 2752], U32))
        zpl = zscope.enter_context(nc.sbuf_tensor("zpl", [86, 2752], U32))
        sc_sb = zscope.enter_context(nc.sbuf_tensor("sc_sb", [32, 11008], F32))
        z1f = zscope.enter_context(nc.sbuf_tensor("z1f", [32, 11008], F32))
        z1b = zscope.enter_context(nc.sbuf_tensor("z1b", [32, 11008], BF16))

        for half in range(2):
            c0 = half * 11008

            @blk.sync
            def _(sync):
                if half == 1:
                    s_zw.wait(sync)
                sync.dma_start(out=zq_sb[:, 0:1376],
                               in_=qzgu_d[:, c0 // 8:(c0 + 11008) // 8]
                               ).then_inc(s_z.h, 16)
                sync.dma_start(out=sc_sb[:],
                               in_=scgu_d[:, c0:c0 + 11008]).then_inc(s_z.h, 16)
            s_z.n += 32

            @blk.vector
            def _(vector):
                s_z.wait(vector)
                for jj in range(8):
                    i0 = vector.tensor_scalar(
                        zpl[0:32, 0:1376], zq_sb[:, 0:1376], 4 * jj, 15,
                        OP.logical_shift_right, OP.bitwise_and)
                    vchain(vector, i0, s_vch)
                    i0 = vector.tensor_copy(
                        z1f[:].rearrange("p (c j) -> p c j", j=8)[:, :, jj],
                        zpl[0:32, 0:1376])
                    vchain(vector, i0, s_vch)
                i0 = vector.tensor_scalar(z1f[:], z1f[:], 1.0, None, OP.add)
                vchain(vector, i0, s_vch)
                vector.tensor_tensor(z1b[:], z1f[:], sc_sb[:],
                                     OP.mult).then_inc(s_zc.h, 1)
            s_zc.n += 1

            @blk.sync
            def _(sync):
                s_zc.wait(sync)
                sync.dma_start(out=zsgu_d[:, c0:c0 + 11008],
                               in_=z1b[:]).then_inc(s_zw.h, 16)
            s_zw.n += 16

        zqd_sb = zscope.enter_context(nc.sbuf_tensor("zqd_sb", [86, 512], U32))
        scd_sb = zscope.enter_context(nc.sbuf_tensor("scd_sb", [86, 4096], F32))
        z1fd = zscope.enter_context(nc.sbuf_tensor("z1fd", [86, 4096], F32))
        z1bd = zscope.enter_context(nc.sbuf_tensor("z1bd", [86, 4096], BF16))

        @blk.sync
        def _(sync):
            sync.dma_start(out=zqd_sb[:], in_=qzdn_d).then_inc(s_z.h, 16)
            sync.dma_start(out=scd_sb[:], in_=scdn_d).then_inc(s_z.h, 16)
        s_z.n += 32

        @blk.vector
        def _(vector):
            s_z.wait(vector)
            for jj in range(8):
                i0 = vector.tensor_scalar(
                    zpl[0:86, 0:512], zqd_sb[:], 4 * jj, 15,
                    OP.logical_shift_right, OP.bitwise_and)
                vchain(vector, i0, s_vch)
                i0 = vector.tensor_copy(
                    z1fd[:].rearrange("p (c j) -> p c j", j=8)[:, :, jj],
                    zpl[0:86, 0:512])
                vchain(vector, i0, s_vch)
            i0 = vector.tensor_scalar(z1fd[:], z1fd[:], 1.0, None, OP.add)
            vchain(vector, i0, s_vch)
            vector.tensor_tensor(z1bd[:], z1fd[:], scd_sb[:],
                                 OP.mult).then_inc(s_zc.h, 1)
        s_zc.n += 1

        @blk.sync
        def _(sync):
            s_zc.wait(sync)
            sync.dma_start(out=zsdn_d, in_=z1bd[:]).then_inc(s_zw.h, 16)
            s_zw.wait(sync)
        s_zw.n += 16
    zdone = s_zw.n

    def emit_ln_xt(src_d, xt_sb, tok0, ntt, ln, psum_guard=None):
        lnx, lnP = ln["lnx"], ln["lnP"]
        st1, nm, vr = ln["st1"], ln["nm"], ln["vr"]
        vr2 = ln["vr2"]
        Ctok = ntt * 128
        first_guard = [psum_guard]
        for tt in range(ntt):
            li = cnt["li"]
            cnt["li"] += 1

            @blk.sync
            def _(sync):
                lnx_p.p_wait(sync, li)
                i1 = sync.dma_start(
                    out=lnx[li % 2][:],
                    in_=src_d[tok0 + tt * 128: tok0 + (tt + 1) * 128, :])
                lnx_p.fill_inc(i1, li, 16)
            lnx_p.p_done(li)

            @blk.vector
            def _(vector):
                lnx_p.c_wait(vector, li)
                i0 = vector.tensor_reduce(st1[:], lnx[li % 2][:], AX.X, OP.add)
                vchain(vector, i0, s_vch)
                i1 = vector.tensor_scalar(nm[:], st1[:], -1.0 / H, None,
                                          OP.mult)
                s_lnv.inc(i1, 1)

            @blk.scalar
            def _(scalar):
                s_lnv.wait(scalar)
                lnP_p.p_wait(scalar, li)
                i1 = scalar.activation(lnP[li % 2][:], lnx[li % 2][:],
                                       AF.Square, bias=nm[:], scale=1.0,
                                       accum_out=vr[:])
                s_lns.inc(i1, 1)

            @blk.vector
            def _(vector):
                s_lns.wait(vector)
                i1 = vector.tensor_scalar(vr[:], vr[:], 1.0 / H, EPS, OP.mult,
                                          OP.add)
                s_lnv.inc(i1, 1)

            @blk.scalar
            def _(scalar):
                s_lnv.wait(scalar)
                i1 = scalar.activation(vr2[:], vr[:], AF.Ln)
                vchain(scalar, i1, s_sch)
                i1 = scalar.activation(st1[:], vr2[:], AF.Exp, scale=-0.5)
                s_lns.inc(i1, 1)

            @blk.vector
            def _(vector):
                s_lns.wait(vector)
                lnP_p.p_wait(vector, li)
                i1 = None
                for jj in range(8):
                    i1 = vector.tensor_scalar(
                        lnP[li % 2][:, jj * 512:(jj + 1) * 512],
                        lnx[li % 2][:].rearrange("p (r j) -> p r j", j=8)
                            [:, :, jj],
                        nm[:], st1[:], OP.add, OP.mult)
                lnP_p.fill.inc(i1, 1)
            lnP_p.p_done(li)
            lnx_p.c_done(li, mark=(lnP_p.fill.h, lnP_p.fill.n))

            for t in range(32):
                ti = cnt["ti"]
                cnt["ti"] += 1

                @blk.tensor
                def _(tensor):
                    if t == 0:
                        lnP_p.c_wait(tensor, li)
                        if first_guard[0] is not None:
                            g, gn = first_guard[0]
                            if gn > 0:
                                tensor.wait_ge(g, gn)
                            first_guard[0] = None
                    pT_p.p_wait(tensor, ti)
                    i1 = tensor.transpose(
                        bank_bf16_tile(ti % 2),
                        lnP[li % 2][:, t * 128:(t + 1) * 128], identbf[:])
                    pT_p.fill.inc(i1, 1)
                pT_p.p_done(ti)
                if t == 31:
                    lnP_p.c_done(li, mark=(pT_p.fill.h, pT_p.fill.n))

                @blk.scalar
                def _(scalar):
                    pT_p.c_wait(scalar, ti)
                    i1 = scalar.copy(
                        xt_sb[:, t * Ctok + tt * 128: t * Ctok + (tt + 1) * 128],
                        bank_bf16_tile(ti % 2))
                    s_xt.inc(i1, 1)
                pT_p.c_done(ti, mark=(s_xt.h, s_xt.n))

    def emit_dequant_unit(fbx, ui, rows, qload, scload, cache_tile_out,
                          consume_j):
        usl = ui % 2
        qb, sgb, z1gb, ub, wb = (fbx["qb"], fbx["sgb"], fbx["z1gb"],
                                 fbx["ub"], fbx["wb"])

        @blk.sync
        def _(sync):
            q_p.p_wait(sync, ui)
            qload(sync, qb[usl])
        q_p.p_done(ui)

        @blk.scalar
        def _(scalar):
            sc_p.p_wait(scalar, ui)
            scload(scalar, sgb[usl], z1gb[usl])
        sc_p.p_done(ui)

        for j in range(8):
            uj = cnt["uj"]
            cnt["uj"] += 1

            @blk.vector
            def _(vector):
                if j == 0:
                    q_p.c_wait(vector, ui)
                    sc_p.c_wait(vector, ui)
                w_p.p_wait(vector, uj)
                wsl = wb[uj % 4][0:rows, :]
                i0 = vector.tensor_scalar(
                    ub[uj % 2][0:rows, :], qb[usl][0:rows, :], 4 * j, 15,
                    OP.logical_shift_right, OP.bitwise_and)
                vchain(vector, i0, s_vch)
                i0 = vector.tensor_tensor(wsl, ub[uj % 2][0:rows, :],
                                          sgb[usl][0:rows, :], OP.mult)
                vchain(vector, i0, s_vch)
                i1 = vector.tensor_tensor(wsl, wsl, z1gb[usl][0:rows, :],
                                          OP.subtract)
                w_p.fill_inc(i1, uj, 1)
            w_p.p_done(uj)
            if j == 7:
                q_p.c_done(ui, mark=w_p.fill_mark(uj))
                sc_p.c_done(ui, mark=w_p.fill_mark(uj))

            @blk.sync
            def _(sync):
                w_p.c_wait(sync, uj)
                i1 = cache_tile_out(sync, wb[uj % 4], rows, j)
                s_cwS[uj % 4].inc(i1, 16)

            consume_j(uj, j)

    def emit_ffn_pass(fbx, ln, src_d, dst_ap, dequant):
        xt, hbuf = fbx["xt"], fbx["h"]
        wb, wdb, sgact = fbx["wb"], fbx["wdb"], fbx["sgact"]
        xres, xout = fbx["xres"], fbx["xout"]

        for ch in range(1 if (DBG and dequant) else 2):
            tok0 = ch * C
            if ch == 1:
                @blk.scalar
                def _(scalar):
                    scalar.wait_ge(w_p.free.h, w_p.free.n)

                @blk.vector
                def _(vector):
                    vector.wait_ge(wd_p.free.h, wd_p.free.n)

                @blk.tensor
                def _(tensor):
                    tensor.wait_ge(xo_p.fill.h, xo_p.fill.n)
            guard = (xo_p.fill.h, xo_p.fill.n)
            emit_ln_xt(src_d, xt, tok0, C // 128, ln, psum_guard=guard)
            xt_ready = s_xt.n
            dequant_now = dequant and ch == 0
            if dequant and ch == 1:
                @blk.sync
                def _(sync):
                    for s in s_cwS:
                        s.wait(sync)

            for fc in range(43):
                fci = cnt["fci"]
                cnt["fci"] += 1
                pp = [bank_f32(4 * (fci % 2) + i) for i in range(4)]
                for R in range(4):
                    ui = cnt["ui"]
                    cnt["ui"] += 1
                    gcol, ucol = fc * 256, INTER + fc * 256

                    def consume_j(uj, j, fc=fc, R=R, fci=fci, pp=pp,
                                  xt_ready=xt_ready):
                        @blk.tensor
                        def _(tensor):
                            w_p.c_wait(tensor, uj)
                            if R == 0 and j == 0:
                                if fc == 0:
                                    tensor.wait_ge(s_xt.h, xt_ready)
                                pg_p.p_wait(tensor, fci)
                            t = j * 4 + R
                            i1 = None
                            for fm in range(4):
                                i1 = tensor.matmul(
                                    pp[fm],
                                    wb[uj % 4][:, fm * 128:(fm + 1) * 128],
                                    xt[:, t * C:(t + 1) * C],
                                    start=(R == 0 and j == 0),
                                    stop=(R == 3 and j == 7))
                            w_p.free.inc(i1, 1)
                        if dequant_now:
                            w_p.c_done(uj, mark=[
                                (w_p.free.h, w_p.free.n),
                                (s_cwS[uj % 4].h, s_cwS[uj % 4].n)])
                        else:
                            w_p.c_done(uj)
                        if R == 3 and j == 7:
                            pg_p.p_done(fci, mark=(w_p.free.h, w_p.free.n))

                    if dequant_now:
                        def qload(sync, qtile, R=R, gcol=gcol, ucol=ucol):
                            i1 = sync.dma_start(
                                out=qtile[:, 0:256],
                                in_=qwgu_d[R * 128:(R + 1) * 128,
                                           gcol:gcol + 256])
                            q_p.fill_inc(i1, ui, 16)
                            i2 = sync.dma_start(
                                out=qtile[:, 256:512],
                                in_=qwgu_d[R * 128:(R + 1) * 128,
                                           ucol:ucol + 256])
                            q_p.fill_inc(i2, ui, 16)

                        def scload(scalar, sg, z1, R=R, gcol=gcol, ucol=ucol):
                            for hl, cc in ((0, gcol), (1, ucol)):
                                for dst, tbl in ((sg, scgub_d), (z1, zsgu_d)):
                                    i1 = scalar.dma_start(
                                        out=dst[:, hl * 256:(hl + 1) * 256],
                                        in_=tbl[8 * R:8 * R + 8, cc:cc + 256]
                                            .unsqueeze(1)
                                            .to_broadcast([8, 16, 256]))
                                    sc_p.fill_inc(i1, ui, 16)

                        def cache_out(sync, wtile, rows, j, fc=fc, R=R):
                            t = j * 4 + R
                            row = (fc * 32 + t) * 128
                            return sync.dma_start(out=cgu_d[row:row + 128, :],
                                                  in_=wtile[:])

                        emit_dequant_unit(fbx, ui, 128, qload, scload,
                                          cache_out, consume_j)
                    else:
                        for j in range(8):
                            uj = cnt["uj"]
                            cnt["uj"] += 1

                            @blk.sync
                            def _(sync):
                                w_p.p_wait(sync, uj)
                                t = j * 4 + R
                                row = (fc * 32 + t) * 128
                                i1 = sync.dma_start(
                                    out=wb[uj % 4][:],
                                    in_=cgu_d[row:row + 128, :])
                                w_p.fill_inc(i1, uj, 16)
                            w_p.p_done(uj)
                            consume_j(uj, j)

                @blk.scalar
                def _(scalar):
                    pg_p.c_wait(scalar, fci)
                    sil_p.p_wait(scalar, fci)
                    i0 = scalar.activation(sgact[fci % 2][:, 0:C], pp[0],
                                           AF.Silu)
                    vchain(scalar, i0, s_sch)
                    i1 = scalar.activation(sgact[fci % 2][:, C:2 * C], pp[1],
                                           AF.Silu)
                    sil_p.fill.inc(i1, 1)
                sil_p.p_done(fci)

                @blk.vector
                def _(vector):
                    sil_p.c_wait(vector, fci)
                    vector.tensor_tensor(
                        hbuf[:, (2 * fc) * C:(2 * fc + 1) * C],
                        sgact[fci % 2][:, 0:C], pp[2], OP.mult)
                    i1 = vector.tensor_tensor(
                        hbuf[:, (2 * fc + 1) * C:(2 * fc + 2) * C],
                        sgact[fci % 2][:, C:2 * C], pp[3], OP.mult)
                    s_h.inc(i1, 1)
                sil_p.c_done(fci, mark=(s_h.h, s_h.n))
                pg_p.c_done(fci, mark=(s_h.h, s_h.n))

            if dequant and ch == 0:
                for R in range(11):
                    rows = 128 if R < 10 else 96
                    ng = rows // 16
                    for nch in range(8):
                        ui = cnt["ui"]
                        cnt["ui"] += 1

                        def qload(sync, qtile, R=R, nch=nch, rows=rows):
                            i1 = sync.dma_start(
                                out=qtile[0:rows, :],
                                in_=qwdn_d[R * 128:R * 128 + rows,
                                           nch * 512:(nch + 1) * 512])
                            q_p.fill_inc(i1, ui, 16)

                        def scload(scalar, sg, z1, R=R, nch=nch, rows=rows,
                                   ng=ng):
                            for dst, tbl in ((sg, scdnb_d), (z1, zsdn_d)):
                                i1 = scalar.dma_start(
                                    out=dst[0:rows, 0:512],
                                    in_=tbl[8 * R:8 * R + ng,
                                            nch * 512:(nch + 1) * 512]
                                        .unsqueeze(1)
                                        .to_broadcast([ng, 16, 512]))
                                sc_p.fill_inc(i1, ui, 16)

                        def cache_out(sync, wtile, rows, j, R=R, nch=nch):
                            return sync.dma_start(
                                out=cdn_d.rearrange("(qq e) c -> qq e c", e=8)
                                    [R * 128:R * 128 + rows, j,
                                     nch * 512:(nch + 1) * 512],
                                in_=wtile[0:rows, :])

                        def consume_j(uj, j):
                            w_p.c_done(uj, mark=(s_cwS[uj % 4].h,
                                                 s_cwS[uj % 4].n))

                        emit_dequant_unit(fbx, ui, rows, qload, scload,
                                          cache_out, consume_j)
                fbx["dn_mark"] = [(s.h, s.n) for s in s_cwS]

            @blk.scalar
            def _(scalar):
                for hh, vv in fbx.get("dn_mark") or []:
                    if vv > 0:
                        scalar.wait_ge(hh, vv)

            for n in range(8):
                di = cnt["di"]
                cnt["di"] += 1
                ppd = [bank_f32(4 * (di % 2) + i) for i in range(4)]
                for kf in range(86):
                    wdi = cnt["wdi"]
                    cnt["wdi"] += 1

                    @blk.scalar
                    def _(scalar):
                        wd_p.p_wait(scalar, wdi)
                        i1 = scalar.dma_start(
                            out=wdb[wdi % 4][:],
                            in_=cdn_d[kf * 128:(kf + 1) * 128,
                                      n * 512:(n + 1) * 512])
                        wd_p.fill_inc(i1, wdi, 16)
                    wd_p.p_done(wdi)

                    @blk.tensor
                    def _(tensor):
                        wd_p.c_wait(tensor, wdi)
                        if kf == 0:
                            s_h.wait(tensor)
                            pd_p.p_wait(tensor, di)
                        i1 = None
                        for m in range(4):
                            i1 = tensor.matmul(
                                ppd[m],
                                hbuf[:, kf * C + m * 128:kf * C + (m + 1) * 128],
                                wdb[wdi % 4][:],
                                start=(kf == 0), stop=(kf == 85))
                        wd_p.free.inc(i1, 1)
                    wd_p.c_done(wdi)
                    if kf == 85:
                        pd_p.p_done(di, mark=(wd_p.free.h, wd_p.free.n))

                for m in range(4):
                    ri = cnt["ri"]
                    cnt["ri"] += 1

                    @blk.sync
                    def _(sync):
                        xr_p.p_wait(sync, ri)
                        i1 = sync.dma_start(
                            out=xres[ri % 2][:],
                            in_=src_d[tok0 + m * 128:tok0 + (m + 1) * 128,
                                      n * 512:(n + 1) * 512])
                        xr_p.fill_inc(i1, ri, 16)
                    xr_p.p_done(ri)

                    @blk.vector
                    def _(vector):
                        if m == 0:
                            pd_p.c_wait(vector, di)
                        xr_p.c_wait(vector, ri)
                        xo_p.p_wait(vector, ri)
                        i1 = vector.scalar_tensor_tensor(
                            xout[ri % 2][:], ppd[m], 0.5, xres[ri % 2][:],
                            OP.mult, OP.add)
                        xo_p.fill.inc(i1, 1)
                    xo_p.p_done(ri)
                    xr_p.c_done(ri, mark=(xo_p.fill.h, xo_p.fill.n))
                    if m == 3:
                        pd_p.c_done(di, mark=(xo_p.fill.h, xo_p.fill.n))

                    @blk.sync
                    def _(sync):
                        xo_p.c_wait(sync, ri)
                        i1 = sync.dma_start(
                            out=dst_ap[tok0 + m * 128:tok0 + (m + 1) * 128,
                                       n * 512:(n + 1) * 512],
                            in_=xout[ri % 2][:])
                        xoe[ri % 2].inc(i1, 16)
                    xo_p.c_done(ri, mark=(xoe[ri % 2].h, xoe[ri % 2].n))

    def emit_attention(src_d, dst_ap, qt, kts, vsb):
        with ExitStack() as sa:
            sb = lambda n, s, d: sa.enter_context(nc.sbuf_tensor(n, s, d))
            ln = dict(
                lnx=[sb("a_lnx0", [128, H], F32), sb("a_lnx1", [128, H], F32)],
                lnP=[sb("a_lnP0", [128, H], BF16), sb("a_lnP1", [128, H], BF16)],
                st1=sb("a_st1", [128, 1], F32), nm=sb("a_nm", [128, 1], F32),
                vr=sb("a_vr", [128, 1], F32), vr2=sb("a_vr2", [128, 1], F32),
            )
            xt = sb("a_xt", [128, 32 * T], BF16)
            wstg = [sb("a_ws0", [128, 512], F32), sb("a_ws1", [128, 512], F32)]
            wbf = [sb("a_wb0", [128, 512], BF16), sb("a_wb1", [128, 512], BF16)]
            kvout = sb("a_kvout", [128, 2048], BF16)

            emit_ln_xt(src_d, xt, 0, T // 128, ln, psum_guard=None)
            xt_ready = s_xt.n

            mgroups = [list(range(g * 4, min(34, g * 4 + 4))) for g in range(9)]
            for gi, mg in enumerate(mgroups):
                qi = cnt["qi"]
                cnt["qi"] += 1
                for k in range(32):
                    wqi = cnt["wqi"]
                    cnt["wqi"] += 1
                    jph, R = k // 4, k % 4
                    c0, c1 = mg[0] * 128, (mg[-1] + 1) * 128

                    @blk.sync
                    def _(sync):
                        wq_p.p_wait(sync, wqi)
                        i1 = sync.dma_start(
                            out=wstg[wqi % 2][:, 0:c1 - c0],
                            in_=wqkv_d.rearrange("(r j) c -> r j c", j=8)
                                [R * 128:(R + 1) * 128, jph, c0:c1])
                        wq_p.fill_inc(i1, wqi, 16)
                    wq_p.p_done(wqi)

                    @blk.vector
                    def _(vector):
                        wq_p.c_wait(vector, wqi)
                        i1 = vector.tensor_copy(wbf[wqi % 2][:, 0:c1 - c0],
                                                wstg[wqi % 2][:, 0:c1 - c0])
                        s_misc.inc(i1, 1)

                    @blk.tensor
                    def _(tensor):
                        s_misc.wait(tensor)
                        if k == 0:
                            if gi == 0:
                                tensor.wait_ge(s_xt.h, xt_ready)
                            qkv_p.p_wait(tensor, qi)
                        i1 = None
                        for mi, m in enumerate(mg):
                            for chh in range(2):
                                i1 = tensor.matmul(
                                    bank_f32(mi * 2 + chh),
                                    wbf[wqi % 2][:, mi * 128:(mi + 1) * 128],
                                    xt[:, k * T + chh * 512:
                                       k * T + (chh + 1) * 512],
                                    start=(k == 0), stop=(k == 31))
                        wq_p.free.inc(i1, 1)
                    wq_p.c_done(wqi)
                    if k == 31:
                        qkv_p.p_done(qi, mark=(wq_p.free.h, wq_p.free.n))

                @blk.scalar
                def _(scalar):
                    qkv_p.c_wait(scalar, qi)
                    i1 = None
                    for mi, m in enumerate(mg):
                        for chh in range(2):
                            src = bank_f32(mi * 2 + chh)
                            col = chh * 512
                            if i1 is not None:
                                vchain(scalar, i1, s_sch)
                            if m < 32:
                                i1 = scalar.activation(
                                    qt[:, m * T + col: m * T + col + 512],
                                    src, AF.Copy, scale=SCALE)
                            elif m == 32:
                                i1 = scalar.copy(
                                    kts[:, 2048 + col:2048 + col + 512], src)
                            else:
                                i1 = scalar.copy(
                                    vsb[:, 2048 + col:2048 + col + 512], src)
                    s_attS.inc(i1, 1)
                qkv_p.c_done(qi, mark=(s_attS.h, s_attS.n))

            for i in range(8):
                for which, srcbuf in ((0, kts), (1, vsb)):
                    ki = cnt["ki"]
                    cnt["ki"] += 1

                    @blk.tensor
                    def _(tensor):
                        if i == 0 and which == 0:
                            s_attS.wait(tensor)
                        pt2_p.p_wait(tensor, ki)
                        i1 = tensor.transpose(
                            bank_bf16_tile(ki % 2),
                            srcbuf[:, 2048 + i * 128:2048 + (i + 1) * 128],
                            identbf[:])
                        pt2_p.fill.inc(i1, 1)
                    pt2_p.p_done(ki)

                    @blk.vector
                    def _(vector):
                        pt2_p.c_wait(vector, ki)
                        i1 = vector.tensor_copy(
                            kvout[:, i * 256 + which * 128:
                                  i * 256 + (which + 1) * 128],
                            bank_bf16_tile(ki % 2))
                        s_attV.inc(i1, 1)
                    pt2_p.c_done(ki, mark=(s_attV.h, s_attV.n))

            @blk.gpsimd
            def _(gpsimd):
                s_attS.wait(gpsimd)
                s_attV.wait(gpsimd)
                pv = gpsimd.value_load(par_sb[0:1, 0:1])
                for i in range(8):
                    i1 = gpsimd.dma_start(
                        out=kvsh_d[bass.ts(pv * 8 + i, 128), :],
                        in_=kvout[:, i * 256:(i + 1) * 256])
                    s_misc.inc(i1, 16)
                s_misc.wait(gpsimd)
                with gpsimd.register64("pa64") as pa64:
                    gpsimd.reg_load(pa64, paddr_sb[0:1, 0:2])
                    gpsimd.store(pa64, 16)
                s_kvbar.n += 16
                s_kvbar.wait(gpsimd)
                i1 = gpsimd.dma_start(
                    out=vsb[:, 0:2048],
                    in_=kvsh_d.rearrange("(t p) d -> p t d", p=128)
                        [:, :, 128:256])
                s_misc.inc(i1, 16)
                i1 = gpsimd.dma_start(
                    out=kvout[:],
                    in_=kvsh_d.rearrange("(t p) d -> p t d", p=128)
                        [:, :, 0:128])
                s_misc.inc(i1, 16)

            for i in range(16):
                ki = cnt["ki"]
                cnt["ki"] += 1

                @blk.tensor
                def _(tensor):
                    if i == 0:
                        s_misc.wait(tensor)
                    pt2_p.p_wait(tensor, ki)
                    i1 = tensor.transpose(
                        bank_bf16_tile(ki % 2),
                        kvout[:, i * 128:(i + 1) * 128], identbf[:])
                    pt2_p.fill.inc(i1, 1)
                pt2_p.p_done(ki)

                @blk.vector
                def _(vector):
                    pt2_p.c_wait(vector, ki)
                    i1 = vector.tensor_copy(kts[:, i * 128:(i + 1) * 128],
                                            bank_bf16_tile(ki % 2))
                    s_attV.inc(i1, 1)
                pt2_p.c_done(ki, mark=(s_attV.h, s_attV.n))

        markA = (s_attS.n, s_attV.n)
        with ExitStack() as sbp:
            @blk.sync
            def _(sync):
                sync.wait_ge(s_attV.h, markA[1])

            @blk.scalar
            def _(scalar):
                scalar.wait_ge(s_attV.h, markA[1])

            sb = lambda n, s, d: sbp.enter_context(nc.sbuf_tensor(n, s, d))
            attnT = sb("b_at", [128, 32 * T], BF16)
            probs = [sb("b_pr0", [128, 2048], BF16),
                     sb("b_pr1", [128, 2048], BF16)]
            probsT = sb("b_pt", [128, 2048], BF16)
            gneg = sb("b_gneg", [128, 16 * 128], F32)
            augq = sb("b_aq", [2, T], F32)
            augk = sb("b_ak", [2, 2048], F32)
            nmv = sb("b_nmv", [128, 1], F32)
            sums = sb("b_sums", [128, 1], F32)
            rsum = sb("b_rsum", [128, 1], F32)
            wstg = [sb("b_ws0", [128, 512], F32), sb("b_ws1", [128, 512], F32)]
            wbf = [sb("b_wb0", [128, 512], BF16), sb("b_wb1", [128, 512], BF16)]
            xres = [sb("b_xr0", [128, 512], F32), sb("b_xr1", [128, 512], F32)]
            xout = [sb("b_xo0", [128, 512], F32), sb("b_xo1", [128, 512], F32)]

            @blk.vector
            def _(vector):
                s_attV.wait(vector)
                i1 = None
                for qb in range(8):
                    for which, kv0 in ((0, qb * 128), (1, 1024 + qb * 128)):
                        tmp = gneg[:, (qb * 2 + which) * 128:
                                   (qb * 2 + which + 1) * 128]
                        i0 = vector.tensor_scalar(
                            tmp, poskb[:, kv0:kv0 + 128],
                            posq_pp[:, qb:qb + 1], None, OP.subtract)
                        vchain(vector, i0, s_vch)
                        i1 = vector.tensor_scalar(tmp, tmp, 0.0, NEGBIG,
                                                  OP.is_gt, OP.mult)
                s_attV.inc(i1, 1)

            @blk.tensor
            def _(tensor):
                s_attS.wait(tensor)
                s_attV.wait(tensor)
                s_misc.wait(tensor)

            def emit_tpv(ctx):
                h, qb, si, sblk, w1, wtot, r0 = (ctx["h"], ctx["qb"],
                                                 ctx["si"], ctx["sblk"],
                                                 ctx["w1"], ctx["wtot"],
                                                 ctx["r0"])

                def rg(cc, w):
                    return Pap[:, r0 + cc:r0 + cc + w]

                nkt = wtot // 128
                pbase = cnt["pbi"]
                for ktile in range(nkt):
                    ki = cnt["ki"]
                    cnt["ki"] += 1
                    pbi = cnt["pbi"]
                    cnt["pbi"] += 1
                    tslot = Pbf_ap[:, (si % 2) * 4096 + (ki % 2) * 1024:
                                   (si % 2) * 4096 + (ki % 2) * 1024 + 128]

                    @blk.tensor
                    def _(tensor):
                        if ktile == 0:
                            prb_p.c_wait(tensor, si)
                        pt2_p.p_wait(tensor, ki)
                        i1 = tensor.transpose(
                            tslot,
                            probs[si % 2][:, ktile * 128:(ktile + 1) * 128],
                            identbf[:])
                        pt2_p.fill.inc(i1, 1)
                    pt2_p.p_done(ki)

                    @blk.vector
                    def _(vector):
                        pt2_p.c_wait(vector, ki)
                        i1 = vector.tensor_copy(
                            probsT[:, ktile * 128:(ktile + 1) * 128], tslot)
                        pbt_p.fill.inc(i1, 1)
                    pbt_p.p_done(pbi)
                    pt2_p.c_done(ki, mark=(pbt_p.fill.h, pbt_p.fill.n))

                for ktile in range(nkt):
                    pbi = pbase + ktile

                    @blk.tensor
                    def _(tensor):
                        pbt_p.c_wait(tensor, pbi)
                        if ktile <= sblk:
                            vcol = ktile * 128
                        else:
                            vcol = 1024 + (ktile - (sblk + 1)) * 128
                        i1 = tensor.matmul(
                            rg(1024, 128),
                            vsb[:, vcol:vcol + 128],
                            probsT[:, ktile * 128:(ktile + 1) * 128],
                            start=(ktile == 0), stop=(ktile == nkt - 1))
                        if ktile == nkt - 1:
                            s_pv.inc(i1, 1)
                    pbt_p.c_done(pbi)
                prb_p.c_done(si, mark=(s_pv.h, s_pv.n))

                @blk.vector
                def _(vector):
                    s_pv.wait(vector)
                    i1 = vector.tensor_copy(
                        attnT[:, h * T + qb * 128: h * T + (qb + 1) * 128],
                        rg(1024, 128))
                    s_attV.inc(i1, 1)
                ssc_p.c_done(si, mark=(s_attV.h, s_attV.n))

            pending = [None]
            for h in range(NH):
                ai = cnt["ai"]
                cnt["ai"] += 1

                @blk.sync
                def _(sync):
                    aug_p.p_wait(sync, ai)
                    i1 = sync.dma_start(out=augq[:],
                                        in_=augq_d[:, h * T:(h + 1) * T])
                    i2 = sync.dma_start(out=augk[:],
                                        in_=augk_d[:, h * 2048:(h + 1) * 2048])
                    aug_p.fill_inc(i1, ai, 16)
                    aug_p.fill_inc(i2, ai, 16)
                aug_p.p_done(ai)

                for qb in range(8):
                    sblk = qb
                    w1 = 128 * (sblk + 1)
                    wtot = 2 * w1
                    si = cnt["si"]
                    cnt["si"] += 1
                    r0 = (si % 2) * 2048

                    def rg(cc, w):
                        return Pap[:, r0 + cc:r0 + cc + w]

                    @blk.tensor
                    def _(tensor):
                        if qb == 0:
                            aug_p.c_wait(tensor, ai)
                        ssc_p.p_wait(tensor, si)
                        lhs = qt[:, h * T + qb * 128: h * T + (qb + 1) * 128]
                        first = True
                        for (cc0, k0) in ((0, 0), (w1, 1024)):
                            off = 0
                            while off < w1:
                                nn = min(512, w1 - off)
                                tensor.matmul(rg(cc0 + off, nn), lhs,
                                              kts[:, k0 + off:k0 + off + nn],
                                              start=first, stop=False)
                                first = False
                                off += nn
                        i1 = None
                        for (cc0, k0) in ((0, 0), (w1, 1024)):
                            off = 0
                            while off < w1:
                                nn = min(512, w1 - off)
                                last = (cc0 == w1 and off + nn >= w1)
                                i1 = tensor.matmul(
                                    rg(cc0 + off, nn),
                                    augq[:, qb * 128:(qb + 1) * 128],
                                    augk[:, k0 + off:k0 + off + nn],
                                    start=False, stop=last)
                                off += nn
                        ssc_p.fill.inc(i1, 1)
                    ssc_p.p_done(si)
                    if qb == 7:
                        aug_p.c_done(ai, mark=(ssc_p.fill.h, ssc_p.fill.n))

                    @blk.vector
                    def _(vector):
                        ssc_p.c_wait(vector, si)
                        vector.tensor_tensor(
                            rg(sblk * 128, 128), rg(sblk * 128, 128),
                            gneg[:, (qb * 2) * 128:(qb * 2 + 1) * 128], OP.add)
                        i0 = vector.tensor_tensor(
                            rg(w1 + sblk * 128, 128), rg(w1 + sblk * 128, 128),
                            gneg[:, (qb * 2 + 1) * 128:(qb * 2 + 2) * 128],
                            OP.add)
                        vchain(vector, i0, s_vch)
                        i1 = vector.tensor_reduce(nmv[:], rg(0, wtot), AX.X,
                                                  OP.max, negate=True)
                        s_attV.inc(i1, 1)

                    @blk.scalar
                    def _(scalar):
                        s_attV.wait(scalar)
                        prb_p.p_wait(scalar, si)
                        i1 = scalar.activation(
                            probs[si % 2][:, 0:wtot], rg(0, wtot), AF.Exp,
                            bias=nmv[:], scale=1.0, accum_out=sums[:])
                        s_attS.inc(i1, 1)

                    @blk.vector
                    def _(vector):
                        s_attS.wait(vector)
                        i0 = vector.reciprocal(rsum[:], sums[:])
                        vchain(vector, i0, s_vch)
                        i1 = vector.tensor_scalar(
                            probs[si % 2][:, 0:wtot], probs[si % 2][:, 0:wtot],
                            rsum[:], None, OP.mult)
                        prb_p.fill.inc(i1, 1)
                    prb_p.p_done(si)

                    if pending[0] is not None:
                        emit_tpv(pending[0])
                    pending[0] = dict(h=h, qb=qb, si=si, sblk=sblk, w1=w1,
                                      wtot=wtot, r0=r0)

            emit_tpv(pending[0])

            @blk.tensor
            def _(tensor):
                s_attV.wait(tensor)

            for ch in range(2):
                for n in range(8):
                    oi = cnt["oi"]
                    cnt["oi"] += 1
                    ppq = [bank_f32(4 * (oi % 2) + i) for i in range(4)]
                    for k in range(32):
                        wqi = cnt["wqi"]
                        cnt["wqi"] += 1

                        @blk.sync
                        def _(sync):
                            wq_p.p_wait(sync, wqi)
                            i1 = sync.dma_start(
                                out=wstg[wqi % 2][:],
                                in_=wo_d[k * 128:(k + 1) * 128,
                                         n * 512:(n + 1) * 512])
                            wq_p.fill_inc(i1, wqi, 16)
                        wq_p.p_done(wqi)

                        @blk.vector
                        def _(vector):
                            wq_p.c_wait(vector, wqi)
                            i1 = vector.tensor_copy(wbf[wqi % 2][:, 0:512],
                                                    wstg[wqi % 2][:, 0:512])
                            s_misc.inc(i1, 1)

                        @blk.tensor
                        def _(tensor):
                            s_misc.wait(tensor)
                            if k == 0:
                                pwo_p.p_wait(tensor, oi)
                            i1 = None
                            for m in range(4):
                                i1 = tensor.matmul(
                                    ppq[m],
                                    attnT[:, k * T + ch * 512 + m * 128:
                                          k * T + ch * 512 + (m + 1) * 128],
                                    wbf[wqi % 2][:, 0:512],
                                    start=(k == 0), stop=(k == 31))
                            wq_p.free.inc(i1, 1)
                        wq_p.c_done(wqi)
                        if k == 31:
                            pwo_p.p_done(oi, mark=(wq_p.free.h, wq_p.free.n))

                    for m in range(4):
                        ri = cnt["ri"]
                        cnt["ri"] += 1

                        @blk.sync
                        def _(sync):
                            xr_p.p_wait(sync, ri)
                            i1 = sync.dma_start(
                                out=xres[ri % 2][:],
                                in_=src_d[ch * 512 + m * 128:
                                          ch * 512 + (m + 1) * 128,
                                          n * 512:(n + 1) * 512])
                            xr_p.fill_inc(i1, ri, 16)
                        xr_p.p_done(ri)

                        @blk.vector
                        def _(vector):
                            if m == 0:
                                pwo_p.c_wait(vector, oi)
                            xr_p.c_wait(vector, ri)
                            xo_p.p_wait(vector, ri)
                            i1 = vector.tensor_tensor(
                                xout[ri % 2][:], ppq[m], xres[ri % 2][:],
                                OP.add)
                            xo_p.fill.inc(i1, 1)
                        xo_p.p_done(ri)
                        xr_p.c_done(ri, mark=(xo_p.fill.h, xo_p.fill.n))
                        if m == 3:
                            pwo_p.c_done(oi, mark=(xo_p.fill.h, xo_p.fill.n))

                        @blk.sync
                        def _(sync):
                            xo_p.c_wait(sync, ri)
                            i1 = sync.dma_start(
                                out=dst_ap[ch * 512 + m * 128:
                                           ch * 512 + (m + 1) * 128,
                                           n * 512:(n + 1) * 512],
                                in_=xout[ri % 2][:])
                            xoe[ri % 2].inc(i1, 16)
                        xo_p.c_done(ri, mark=(xoe[ri % 2].h, xoe[ri % 2].n))

    def make_ffn(scope, pref):
        sb = lambda n, s, d: scope.enter_context(nc.sbuf_tensor(pref + n, s, d))
        return dict(
            xt=sb("xt", [128, 32 * C], BF16),
            h=sb("h", [128, 86 * C], BF16),
            qb=[sb("qb0", [128, 512], U32), sb("qb1", [128, 512], U32)],
            sgb=[sb("sg0", [128, 512], BF16), sb("sg1", [128, 512], BF16)],
            z1gb=[sb("z10", [128, 512], BF16), sb("z11", [128, 512], BF16)],
            ub=[sb("ub0", [128, 512], U32), sb("ub1", [128, 512], U32)],
            wb=[sb("wb0", [128, 512], BF16), sb("wb1", [128, 512], BF16),
                sb("wb2", [128, 512], BF16), sb("wb3", [128, 512], BF16)],
            wdb=[sb("wdb0", [128, 512], BF16), sb("wdb1", [128, 512], BF16),
                 sb("wdb2", [128, 512], BF16), sb("wdb3", [128, 512], BF16)],
            sgact=[sb("sga", [128, 2 * C], BF16), sb("sgc", [128, 2 * C], BF16)],
            xres=[sb("xr0", [128, 512], F32), sb("xr1", [128, 512], F32)],
            xout=[sb("xo0", [128, 512], F32), sb("xo1", [128, 512], F32)],
            dn_mark=0,
        )

    def make_ln(scope, pref):
        sb = lambda n, s, d: scope.enter_context(nc.sbuf_tensor(pref + n, s, d))
        return dict(
            lnx=[sb("lnx0", [128, H], F32), sb("lnx1", [128, H], F32)],
            lnP=[sb("lnP0", [128, H], BF16), sb("lnP1", [128, H], BF16)],
            st1=sb("st1", [128, 1], F32), nm=sb("nm", [128, 1], F32),
            vr=sb("vr", [128, 1], F32), vr2=sb("vr2", [128, 1], F32),
        )

    with ExitStack() as scope:
        ln = make_ln(scope, "f_")
        fbx = make_ffn(scope, "f_")
        dbg_fbx = fbx

        @blk.sync
        def _(sync):
            sync.wait_ge(s_zw.h, zdone)

        @blk.scalar
        def _(scalar):
            scalar.wait_ge(s_zw.h, zdone)
            s_misc.wait(scalar, 1)

        dst1 = out_d if STAGE == 1 else xs1_d
        emit_ffn_pass(fbx, ln, x0_d, dst1, True)
        if DBG:
            s_dbg = sem("s_dbg")

            @blk.sync
            def _(sync):
                xoe[0].wait(sync)
                xoe[1].wait(sync)
                sync.dma_start(out=dbg_xt_d, in_=dbg_fbx["xt"][:]).then_inc(
                    s_dbg.h, 16)
                sync.dma_start(out=dbg_h_d, in_=dbg_fbx["h"][:]).then_inc(
                    s_dbg.h, 16)
                sync.dma_start(out=dbg_cgu_d,
                               in_=cgu_d[0:1024, :]).then_inc(s_dbg.h, 16)
                sync.dma_start(out=dbg_cdn_d,
                               in_=cdn_d[0:1024, :]).then_inc(s_dbg.h, 16)
                sync.wait_ge(s_dbg.h, 64)
            s_dbg.n += 64

    if STAGE >= 2:
        fence_all()
        with ExitStack() as scope:
            sb = lambda n, s, d: scope.enter_context(nc.sbuf_tensor(n, s, d))
            qt = sb("at_qt", [128, 32 * T], BF16)
            kts = sb("at_kt", [128, 3072], BF16)
            vsb = sb("at_v", [128, 3072], BF16)
            dst2 = out_d if STAGE == 2 else xs2_d
            emit_attention(xs1_d, dst2, qt, kts, vsb)

    if STAGE >= 3:
        fence_all()
        with ExitStack() as scope:
            ln = make_ln(scope, "g_")
            fbx = make_ffn(scope, "g_")
            emit_ffn_pass(fbx, ln, xs2_d, out_d, False)

    @blk.sync
    def _(sync):
        xoe[0].wait(sync)
        xoe[1].wait(sync)

    es.close()
    return nc, kvbar_num


_cache = {}


def kernel(**inputs):
    if "nc" not in _cache:
        _cache["nc"], _cache["kvbar"] = build_program()
    nc, kvbar_num = _cache["nc"], _cache["kvbar"]

    x = np.ascontiguousarray(inputs["L0_attn_hidden_states"])
    slopes = _alibi_slopes()

    pos_kv = np.concatenate(
        [np.concatenate([np.arange(g * 128, (g + 1) * 128)
                         for g in BLOCKS[0]]),
         np.concatenate([np.arange(g * 128, (g + 1) * 128)
                         for g in BLOCKS[1]])]).astype(np.float32)
    augk = np.empty((2, NH * 2048), np.float32)
    for h in range(NH):
        augk[0, h * 2048:(h + 1) * 2048] = slopes[h] * pos_kv
        augk[1, h * 2048:(h + 1) * 2048] = 1.0

    in_maps = []
    shard_rows = {}
    for c in range(8):
        b, par = c // 2, c & 1
        rows = np.concatenate([np.arange(g * 128, (g + 1) * 128)
                               for g in BLOCKS[par]])
        shard_rows[c] = rows
        pos_own = rows.astype(np.float32)
        posq_pp = np.ascontiguousarray(pos_own.reshape(8, 128).T)
        augq = np.empty((2, NH * T), np.float32)
        for h in range(NH):
            augq[0, h * T:(h + 1) * T] = 1.0
            augq[1, h * T:(h + 1) * T] = -slopes[h] * pos_own
        addr = TPB_BASES[L2P[c ^ 1]] + SEM_INC_REL + 4 * kvbar_num
        paddr = np.array([[addr & 0xFFFFFFFF, addr >> 32]], np.uint32)
        in_maps.append(dict(
            x0=np.ascontiguousarray(x[b, rows, :]),
            wqkv=inputs["L0_attn_Wqkv"],
            wo=inputs["L0_attn_Wo"],
            qwgu=inputs["L0_ffn_qweight_gu"],
            qzgu=inputs["L0_ffn_qzeros_gu"],
            scgu=inputs["L0_ffn_scales_gu"],
            qwdn=inputs["L0_ffn_qweight_down"],
            qzdn=inputs["L0_ffn_qzeros_down"],
            scdn=inputs["L0_ffn_scales_down"],
            posq=posq_pp,
            poskv=pos_kv.reshape(1, 2048),
            augq=augq,
            augk=augk,
            ident=np.eye(128, dtype=np.float32),
            par=np.array([[par]], np.int32),
            paddr=paddr,
        ))

    res = run_bass_kernel_spmd(nc, in_maps, core_ids=list(range(8)))
    out = np.empty((B, S, H), np.float32)
    for c in range(8):
        out[c // 2, shard_rows[c], :] = np.asarray(res.results[c]["out"])
    return out


# version 03
# speedup vs baseline: 1.5947x; 1.0358x over previous
import numpy as np
from contextlib import ExitStack

import concourse.bass as bass
import concourse.mybir as mybir
from concourse.alu_op_type import AluOpType as OP
from concourse.bass_utils import run_bass_kernel_spmd

F32 = mybir.dt.float32
BF16 = mybir.dt.bfloat16
U32 = mybir.dt.uint32
I32 = mybir.dt.int32
AF = mybir.ActivationFunctionType
AX = mybir.AxisListType

B, S, H = 4, 2048, 4096
NH, D = 32, 128
INTER = 11008
F2 = 2 * INTER
NQKV = H + 2 * D
T = 1024
C = 512
EPS = 1e-5
SCALE = float(np.float32(D) ** -0.5)
NEGBIG = -1e30

BLOCKS = {0: [0, 3, 4, 7, 8, 11, 12, 15], 1: [1, 2, 5, 6, 9, 10, 13, 14]}
SEM_INC_REL = 0x802701800
L2P = [4, 5, 6, 7, 2, 3, 0, 1]
TPB_BASES = [0x2000000000, 0x3000000000, 0x6000000000, 0x7000000000,
             0x802000000000, 0x803000000000, 0x806000000000, 0x807000000000]

STAGE = 3
DBG = False


def _alibi_slopes():
    import math
    n = NH
    cp2 = 2 ** math.floor(math.log2(n))
    base = 2.0 ** (-(2.0 ** (-(math.log2(cp2) - 3))))
    slopes = base ** np.arange(1, cp2 + 1, dtype=np.float64)
    if cp2 != n:
        extra_base = 2.0 ** (-(2.0 ** (-(math.log2(2 * cp2) - 3))))
        rem = min(cp2, n - cp2)
        extra = extra_base ** np.arange(1, 1 + 2 * rem, 2, dtype=np.float64)
        slopes = np.concatenate([slopes, extra])
    return slopes.astype(np.float32)


class Sem:
    def __init__(self, es, nc, name):
        self.h = es.enter_context(nc.semaphore(name))
        self.n = 0

    def inc(self, inst, k=1):
        inst.then_inc(self.h, k)
        self.n += k

    def wait(self, eng, n=None):
        v = self.n if n is None else n
        if v > 0:
            eng.wait_ge(self.h, v)


class Pipe:

    def __init__(self, es, nc, name, nslots, slot_fill_sems=False):
        if slot_fill_sems:
            self.fillS = [Sem(es, nc, f"{name}f{k}") for k in range(nslots)]
            self.fill = None
        else:
            self.fill = Sem(es, nc, name + "f")
            self.fillS = None
        self.free = Sem(es, nc, name + "e")
        self.n = nslots
        self.fill_marks = []
        self.free_marks = []

    def fill_inc(self, inst, i, k):
        s = self.fillS[i % self.n] if self.fillS is not None else self.fill
        s.inc(inst, k)

    def fill_mark(self, i):
        s = self.fillS[i % self.n] if self.fillS is not None else self.fill
        return (s.h, s.n)

    @staticmethod
    def _wait_marks(eng, marks):
        if isinstance(marks, tuple):
            marks = [marks]
        for h, v in marks:
            if v > 0:
                eng.wait_ge(h, v)

    def p_wait(self, eng, i):
        if i >= self.n:
            self._wait_marks(eng, self.free_marks[i - self.n])

    def p_done(self, i, mark=None):
        assert i == len(self.fill_marks), (i, len(self.fill_marks))
        self.fill_marks.append(mark if mark is not None
                               else self.fill_mark(i))

    def c_wait(self, eng, i):
        self._wait_marks(eng, self.fill_marks[i])

    def c_done(self, i, mark=None):
        while len(self.free_marks) <= i:
            self.free_marks.append(None)
        self.free_marks[i] = (mark if mark is not None
                              else (self.free.h, self.free.n))


def build_program():
    nc = bass.Bass("TRN2", target_bir_lowering=False)
    es = ExitStack()
    dt = nc.dram_tensor

    x0_d = dt("x0", [T, H], F32, kind="ExternalInput").ap()
    wqkv_d = dt("wqkv", [H, NQKV], F32, kind="ExternalInput").ap()
    wo_d = dt("wo", [H, H], F32, kind="ExternalInput").ap()
    qwgu_d = dt("qwgu", [512, F2], U32, kind="ExternalInput").ap()
    qzgu_d = dt("qzgu", [32, F2 // 8], U32, kind="ExternalInput").ap()
    scgu_d = dt("scgu", [32, F2], F32, kind="ExternalInput").ap()
    qwdn_d = dt("qwdn", [1376, H], U32, kind="ExternalInput").ap()
    qzdn_d = dt("qzdn", [86, H // 8], U32, kind="ExternalInput").ap()
    scdn_d = dt("scdn", [86, H], F32, kind="ExternalInput").ap()
    posd_d = dt("posd", [8 * 128, 2048], F32, kind="ExternalInput").ap()
    ident_d = dt("ident", [128, 128], F32, kind="ExternalInput").ap()
    par_d = dt("par", [1, 1], I32, kind="ExternalInput").ap()
    paddr_d = dt("paddr", [1, 2], U32, kind="ExternalInput").ap()
    out_d = dt("out", [T, H], F32, kind="ExternalOutput").ap()
    if DBG:
        dbg_xt_d = dt("dbg_xt", [128, 32 * C], BF16, kind="ExternalOutput").ap()
        dbg_h_d = dt("dbg_h", [128, 86 * C], BF16, kind="ExternalOutput").ap()
        dbg_cgu_d = dt("dbg_cgu", [1024, 512], BF16, kind="ExternalOutput").ap()
        dbg_cdn_d = dt("dbg_cdn", [1024, H], BF16, kind="ExternalOutput").ap()

    xs1_d = dt("xs1", [T, H], F32, kind="Internal").ap()
    xs2_d = dt("xs2", [T, H], F32, kind="Internal").ap()
    zsgu_d = dt("zsgu", [32, F2], BF16, kind="Internal").ap()
    zsdn_d = dt("zsdn", [86, H], BF16, kind="Internal").ap()
    scgub_d = dt("scgub", [32, F2], BF16, kind="Internal").ap()
    scdnb_d = dt("scdnb", [86, H], BF16, kind="Internal").ap()
    cgu_d = dt("cgu", [43 * 32 * 128, 512], BF16, kind="Internal").ap()
    cdn_d = dt("cdn", [INTER, H], BF16, kind="Internal").ap()
    kvsh_d = dt("kvsh", [2048, 256], BF16, kind="Internal",
                addr_space="Shared").ap()

    def psbuf(name, shape, dtyp):
        return es.enter_context(nc.sbuf_tensor(name, shape, dtyp))

    identbf = psbuf("identbf", [128, 128], BF16)
    poskb = psbuf("poskb", [128, 2048], F32)
    par_sb = psbuf("par_sb", [1, 1], I32)
    paddr_sb = psbuf("paddr_sb", [1, 2], U32)

    P = es.enter_context(nc.psum_tensor("P", [128, 4096], F32))
    Pap = P[:]
    Pbf_ap = P[:].bitcast(BF16)

    def bank_f32(i, w=512):
        return Pap[:, i * 512:i * 512 + w]

    def bank_bf16_tile(i):
        return Pbf_ap[:, i * 1024:i * 1024 + 128]

    def sem(name):
        return Sem(es, nc, name)

    def pipe(name, n, slot_fill_sems=False):
        return Pipe(es, nc, name, n, slot_fill_sems=slot_fill_sems)

    s_in = sem("s_in")
    s_misc = sem("s_misc")
    s_kvbar = sem("s_kvbar")
    kvbar_num = s_kvbar.h.num

    lnx_p = pipe("lnx", 2, slot_fill_sems=True)
    lnP_p = pipe("lnP", 2)
    pT_p = pipe("pT", 2)
    s_lnv = sem("s_lnv")
    s_lns = sem("s_lns")
    s_xt = sem("s_xt")

    q_p = pipe("q", 2, slot_fill_sems=True)
    sc_p = pipe("sc", 2, slot_fill_sems=True)
    u_p = pipe("u", 2)
    w_p = pipe("w", 4, slot_fill_sems=True)
    pg_p = pipe("pg", 2)
    sil_p = pipe("sil", 2)
    s_h = sem("s_h")
    wd_p = pipe("wd", 4, slot_fill_sems=True)
    pd_p = pipe("pd", 2)
    xr_p = pipe("xr", 2, slot_fill_sems=True)
    xo_p = pipe("xo", 2)
    xoe = [sem("xoe0"), sem("xoe1")]
    s_ffn = sem("s_ffn")

    wq_p = pipe("wq", 2, slot_fill_sems=True)
    qkv_p = pipe("qkv", 1)
    pwo_p = pipe("pwo", 2)
    ssc_p = pipe("ssc", 2)
    prb_p = pipe("prb", 2)
    pt2_p = pipe("pt2", 2)
    pbt_p = pipe("pbt", 1 << 30)
    pq_p = pipe("pq", 1, slot_fill_sems=True)
    s_vch = sem("s_vch")
    s_sch = sem("s_sch")

    def vchain(eng, inst, semobj):
        semobj.inc(inst, 1)
        eng.wait_ge(semobj.h, semobj.n)

    s_attS = sem("s_attS")
    s_attV = sem("s_attV")
    s_pv = sem("s_pv")
    s_cwS = [sem(f"s_cw{k}") for k in range(4)]

    cnt = {k: 0 for k in ["li", "ti", "fci", "ui", "uj", "di", "wdi", "ri",
                          "qi", "wqi", "ki", "pbi", "ai", "si", "oi", "pd"]}

    blk = es.enter_context(nc.Block())

    def fence_all():
        for engdec in (blk.tensor, blk.vector, blk.scalar, blk.sync,
                       blk.gpsimd):
            @engdec
            def _(eng):
                xoe[0].wait(eng)
                xoe[1].wait(eng)

    with ExitStack() as zscope:
        ident_f = zscope.enter_context(nc.sbuf_tensor("ident_f", [128, 128], F32))

        @blk.sync
        def _(sync):
            sync.dma_start(out=par_sb[:], in_=par_d).then_inc(s_in.h, 16)
            sync.dma_start(out=paddr_sb[:], in_=paddr_d).then_inc(s_in.h, 16)
            sync.dma_start(out=ident_f[:], in_=ident_d).then_inc(s_in.h, 16)
        s_in.n += 48

        @blk.gpsimd
        def _(gpsimd):
            gpsimd.dma_start(out=scgub_d, in_=scgu_d).then_inc(s_in.h, 16)
            gpsimd.dma_start(out=scdnb_d, in_=scdn_d).then_inc(s_in.h, 16)
        s_in.n += 32

        @blk.vector
        def _(vector):
            s_in.wait(vector)
            vector.tensor_copy(identbf[:], ident_f[:]).then_inc(s_misc.h, 1)
        s_misc.n += 1

        s_z = sem("s_z")
        s_zc = sem("s_zc")
        s_zw = sem("s_zw")
        zq_sb = zscope.enter_context(nc.sbuf_tensor("zq_sb", [32, 2752], U32))
        zpl = zscope.enter_context(nc.sbuf_tensor("zpl", [86, 2752], U32))
        sc_sb = zscope.enter_context(nc.sbuf_tensor("sc_sb", [32, 11008], F32))
        z1f = zscope.enter_context(nc.sbuf_tensor("z1f", [32, 11008], F32))
        z1b = zscope.enter_context(nc.sbuf_tensor("z1b", [32, 11008], BF16))

        for half in range(2):
            c0 = half * 11008

            @blk.sync
            def _(sync):
                if half == 1:
                    s_zw.wait(sync)
                sync.dma_start(out=zq_sb[:, 0:1376],
                               in_=qzgu_d[:, c0 // 8:(c0 + 11008) // 8]
                               ).then_inc(s_z.h, 16)
                sync.dma_start(out=sc_sb[:],
                               in_=scgu_d[:, c0:c0 + 11008]).then_inc(s_z.h, 16)
            s_z.n += 32

            @blk.vector
            def _(vector):
                s_z.wait(vector)
                for jj in range(8):
                    i0 = vector.tensor_scalar(
                        zpl[0:32, 0:1376], zq_sb[:, 0:1376], 4 * jj, 15,
                        OP.logical_shift_right, OP.bitwise_and)
                    vchain(vector, i0, s_vch)
                    i0 = vector.tensor_copy(
                        z1f[:].rearrange("p (c j) -> p c j", j=8)[:, :, jj],
                        zpl[0:32, 0:1376])
                    vchain(vector, i0, s_vch)
                i0 = vector.tensor_scalar(z1f[:], z1f[:], 1.0, None, OP.add)
                vchain(vector, i0, s_vch)
                vector.tensor_tensor(z1b[:], z1f[:], sc_sb[:],
                                     OP.mult).then_inc(s_zc.h, 1)
            s_zc.n += 1

            @blk.sync
            def _(sync):
                s_zc.wait(sync)
                sync.dma_start(out=zsgu_d[:, c0:c0 + 11008],
                               in_=z1b[:]).then_inc(s_zw.h, 16)
            s_zw.n += 16

        zqd_sb = zscope.enter_context(nc.sbuf_tensor("zqd_sb", [86, 512], U32))
        scd_sb = zscope.enter_context(nc.sbuf_tensor("scd_sb", [86, 4096], F32))
        z1fd = zscope.enter_context(nc.sbuf_tensor("z1fd", [86, 4096], F32))
        z1bd = zscope.enter_context(nc.sbuf_tensor("z1bd", [86, 4096], BF16))

        @blk.sync
        def _(sync):
            sync.dma_start(out=zqd_sb[:], in_=qzdn_d).then_inc(s_z.h, 16)
            sync.dma_start(out=scd_sb[:], in_=scdn_d).then_inc(s_z.h, 16)
        s_z.n += 32

        @blk.vector
        def _(vector):
            s_z.wait(vector)
            for jj in range(8):
                i0 = vector.tensor_scalar(
                    zpl[0:86, 0:512], zqd_sb[:], 4 * jj, 15,
                    OP.logical_shift_right, OP.bitwise_and)
                vchain(vector, i0, s_vch)
                i0 = vector.tensor_copy(
                    z1fd[:].rearrange("p (c j) -> p c j", j=8)[:, :, jj],
                    zpl[0:86, 0:512])
                vchain(vector, i0, s_vch)
            i0 = vector.tensor_scalar(z1fd[:], z1fd[:], 1.0, None, OP.add)
            vchain(vector, i0, s_vch)
            vector.tensor_tensor(z1bd[:], z1fd[:], scd_sb[:],
                                 OP.mult).then_inc(s_zc.h, 1)
        s_zc.n += 1

        @blk.sync
        def _(sync):
            s_zc.wait(sync)
            sync.dma_start(out=zsdn_d, in_=z1bd[:]).then_inc(s_zw.h, 16)
            s_zw.wait(sync)
        s_zw.n += 16
    zdone = s_zw.n

    def emit_ln_xt(src_d, xt_sb, tok0, ntt, ln, psum_guard=None):
        lnx, lnP = ln["lnx"], ln["lnP"]
        st1, nm, vr = ln["st1"], ln["nm"], ln["vr"]
        vr2 = ln["vr2"]
        Ctok = ntt * 128
        first_guard = [psum_guard]
        for tt in range(ntt):
            li = cnt["li"]
            cnt["li"] += 1

            @blk.sync
            def _(sync):
                lnx_p.p_wait(sync, li)
                i1 = sync.dma_start(
                    out=lnx[li % 2][:],
                    in_=src_d[tok0 + tt * 128: tok0 + (tt + 1) * 128, :])
                lnx_p.fill_inc(i1, li, 16)
            lnx_p.p_done(li)

            @blk.vector
            def _(vector):
                lnx_p.c_wait(vector, li)
                i0 = vector.tensor_reduce(st1[:], lnx[li % 2][:], AX.X, OP.add)
                vchain(vector, i0, s_vch)
                i1 = vector.tensor_scalar(nm[:], st1[:], -1.0 / H, None,
                                          OP.mult)
                s_lnv.inc(i1, 1)

            @blk.scalar
            def _(scalar):
                s_lnv.wait(scalar)
                lnP_p.p_wait(scalar, li)
                i1 = scalar.activation(lnP[li % 2][:], lnx[li % 2][:],
                                       AF.Square, bias=nm[:], scale=1.0,
                                       accum_out=vr[:])
                s_lns.inc(i1, 1)

            @blk.vector
            def _(vector):
                s_lns.wait(vector)
                i1 = vector.tensor_scalar(vr[:], vr[:], 1.0 / H, EPS, OP.mult,
                                          OP.add)
                s_lnv.inc(i1, 1)

            @blk.scalar
            def _(scalar):
                s_lnv.wait(scalar)
                i1 = scalar.activation(vr2[:], vr[:], AF.Ln)
                vchain(scalar, i1, s_sch)
                i1 = scalar.activation(st1[:], vr2[:], AF.Exp, scale=-0.5)
                s_lns.inc(i1, 1)

            @blk.vector
            def _(vector):
                s_lns.wait(vector)
                lnP_p.p_wait(vector, li)
                i1 = None
                for jj in range(8):
                    i1 = vector.tensor_scalar(
                        lnP[li % 2][:, jj * 512:(jj + 1) * 512],
                        lnx[li % 2][:].rearrange("p (r j) -> p r j", j=8)
                            [:, :, jj],
                        nm[:], st1[:], OP.add, OP.mult)
                lnP_p.fill.inc(i1, 1)
            lnP_p.p_done(li)
            lnx_p.c_done(li, mark=(lnP_p.fill.h, lnP_p.fill.n))

            for t in range(32):
                ti = cnt["ti"]
                cnt["ti"] += 1

                @blk.tensor
                def _(tensor):
                    if t == 0:
                        lnP_p.c_wait(tensor, li)
                        if first_guard[0] is not None:
                            g, gn = first_guard[0]
                            if gn > 0:
                                tensor.wait_ge(g, gn)
                            first_guard[0] = None
                    pT_p.p_wait(tensor, ti)
                    i1 = tensor.transpose(
                        bank_bf16_tile(ti % 2),
                        lnP[li % 2][:, t * 128:(t + 1) * 128], identbf[:])
                    pT_p.fill.inc(i1, 1)
                pT_p.p_done(ti)
                if t == 31:
                    lnP_p.c_done(li, mark=(pT_p.fill.h, pT_p.fill.n))

                @blk.scalar
                def _(scalar):
                    pT_p.c_wait(scalar, ti)
                    i1 = scalar.copy(
                        xt_sb[:, t * Ctok + tt * 128: t * Ctok + (tt + 1) * 128],
                        bank_bf16_tile(ti % 2))
                    s_xt.inc(i1, 1)
                pT_p.c_done(ti, mark=(s_xt.h, s_xt.n))

    def emit_dequant_unit(fbx, ui, rows, qload, scload, cache_tile_out,
                          consume_j):
        usl = ui % 2
        qb, sgb, z1gb, ub, wb = (fbx["qb"], fbx["sgb"], fbx["z1gb"],
                                 fbx["ub"], fbx["wb"])

        @blk.sync
        def _(sync):
            q_p.p_wait(sync, ui)
            qload(sync, qb[usl])
        q_p.p_done(ui)

        @blk.scalar
        def _(scalar):
            sc_p.p_wait(scalar, ui)
            scload(scalar, sgb[usl], z1gb[usl])
        sc_p.p_done(ui)

        for j in range(8):
            uj = cnt["uj"]
            cnt["uj"] += 1

            @blk.vector
            def _(vector):
                if j == 0:
                    q_p.c_wait(vector, ui)
                    sc_p.c_wait(vector, ui)
                w_p.p_wait(vector, uj)
                wsl = wb[uj % 4][0:rows, :]
                i0 = vector.tensor_scalar(
                    ub[uj % 2][0:rows, :], qb[usl][0:rows, :], 4 * j, 15,
                    OP.logical_shift_right, OP.bitwise_and)
                vchain(vector, i0, s_vch)
                i0 = vector.tensor_tensor(wsl, ub[uj % 2][0:rows, :],
                                          sgb[usl][0:rows, :], OP.mult)
                vchain(vector, i0, s_vch)
                i1 = vector.tensor_tensor(wsl, wsl, z1gb[usl][0:rows, :],
                                          OP.subtract)
                w_p.fill_inc(i1, uj, 1)
            w_p.p_done(uj)
            if j == 7:
                q_p.c_done(ui, mark=w_p.fill_mark(uj))
                sc_p.c_done(ui, mark=w_p.fill_mark(uj))

            @blk.sync
            def _(sync):
                w_p.c_wait(sync, uj)
                i1 = cache_tile_out(sync, wb[uj % 4], rows, j)
                s_cwS[uj % 4].inc(i1, 16)

            consume_j(uj, j)

    def emit_ffn_pass(fbx, ln, src_d, dst_ap, dequant):
        xt, hbuf = fbx["xt"], fbx["h"]
        wb, wdb, sgact = fbx["wb"], fbx["wdb"], fbx["sgact"]
        xres, xout = fbx["xres"], fbx["xout"]

        for ch in range(1 if (DBG and dequant) else 2):
            tok0 = ch * C
            if ch == 1:
                @blk.scalar
                def _(scalar):
                    scalar.wait_ge(w_p.free.h, w_p.free.n)

                @blk.vector
                def _(vector):
                    vector.wait_ge(wd_p.free.h, wd_p.free.n)

                @blk.tensor
                def _(tensor):
                    tensor.wait_ge(xo_p.fill.h, xo_p.fill.n)
            guard = (xo_p.fill.h, xo_p.fill.n)
            emit_ln_xt(src_d, xt, tok0, C // 128, ln, psum_guard=guard)
            xt_ready = s_xt.n
            dequant_now = dequant and ch == 0
            if dequant and ch == 1:
                @blk.sync
                def _(sync):
                    for s in s_cwS:
                        s.wait(sync)

            for fc in range(43):
                fci = cnt["fci"]
                cnt["fci"] += 1
                pp = [bank_f32(4 * (fci % 2) + i) for i in range(4)]
                for R in range(4):
                    ui = cnt["ui"]
                    cnt["ui"] += 1
                    gcol, ucol = fc * 256, INTER + fc * 256

                    def consume_j(uj, j, fc=fc, R=R, fci=fci, pp=pp,
                                  xt_ready=xt_ready):
                        @blk.tensor
                        def _(tensor):
                            w_p.c_wait(tensor, uj)
                            if R == 0 and j == 0:
                                if fc == 0:
                                    tensor.wait_ge(s_xt.h, xt_ready)
                                pg_p.p_wait(tensor, fci)
                            t = j * 4 + R
                            i1 = None
                            for fm in range(4):
                                i1 = tensor.matmul(
                                    pp[fm],
                                    wb[uj % 4][:, fm * 128:(fm + 1) * 128],
                                    xt[:, t * C:(t + 1) * C],
                                    start=(R == 0 and j == 0),
                                    stop=(R == 3 and j == 7))
                            w_p.free.inc(i1, 1)
                        if dequant_now:
                            w_p.c_done(uj, mark=[
                                (w_p.free.h, w_p.free.n),
                                (s_cwS[uj % 4].h, s_cwS[uj % 4].n)])
                        else:
                            w_p.c_done(uj)
                        if R == 3 and j == 7:
                            pg_p.p_done(fci, mark=(w_p.free.h, w_p.free.n))

                    if dequant_now:
                        def qload(sync, qtile, R=R, gcol=gcol, ucol=ucol):
                            i1 = sync.dma_start(
                                out=qtile[:, 0:256],
                                in_=qwgu_d[R * 128:(R + 1) * 128,
                                           gcol:gcol + 256])
                            q_p.fill_inc(i1, ui, 16)
                            i2 = sync.dma_start(
                                out=qtile[:, 256:512],
                                in_=qwgu_d[R * 128:(R + 1) * 128,
                                           ucol:ucol + 256])
                            q_p.fill_inc(i2, ui, 16)

                        def scload(scalar, sg, z1, R=R, gcol=gcol, ucol=ucol):
                            for hl, cc in ((0, gcol), (1, ucol)):
                                for dst, tbl in ((sg, scgub_d), (z1, zsgu_d)):
                                    i1 = scalar.dma_start(
                                        out=dst[:, hl * 256:(hl + 1) * 256],
                                        in_=tbl[8 * R:8 * R + 8, cc:cc + 256]
                                            .unsqueeze(1)
                                            .to_broadcast([8, 16, 256]))
                                    sc_p.fill_inc(i1, ui, 16)

                        def cache_out(sync, wtile, rows, j, fc=fc, R=R):
                            t = j * 4 + R
                            row = (fc * 32 + t) * 128
                            return sync.dma_start(out=cgu_d[row:row + 128, :],
                                                  in_=wtile[:])

                        emit_dequant_unit(fbx, ui, 128, qload, scload,
                                          cache_out, consume_j)
                    else:
                        for j in range(8):
                            uj = cnt["uj"]
                            cnt["uj"] += 1

                            @blk.sync
                            def _(sync):
                                w_p.p_wait(sync, uj)
                                t = j * 4 + R
                                row = (fc * 32 + t) * 128
                                i1 = sync.dma_start(
                                    out=wb[uj % 4][:],
                                    in_=cgu_d[row:row + 128, :])
                                w_p.fill_inc(i1, uj, 16)
                            w_p.p_done(uj)
                            consume_j(uj, j)

                @blk.scalar
                def _(scalar):
                    pg_p.c_wait(scalar, fci)
                    sil_p.p_wait(scalar, fci)
                    i0 = scalar.activation(sgact[fci % 2][:, 0:C], pp[0],
                                           AF.Silu)
                    vchain(scalar, i0, s_sch)
                    i1 = scalar.activation(sgact[fci % 2][:, C:2 * C], pp[1],
                                           AF.Silu)
                    sil_p.fill.inc(i1, 1)
                sil_p.p_done(fci)

                @blk.vector
                def _(vector):
                    sil_p.c_wait(vector, fci)
                    vector.tensor_tensor(
                        hbuf[:, (2 * fc) * C:(2 * fc + 1) * C],
                        sgact[fci % 2][:, 0:C], pp[2], OP.mult)
                    i1 = vector.tensor_tensor(
                        hbuf[:, (2 * fc + 1) * C:(2 * fc + 2) * C],
                        sgact[fci % 2][:, C:2 * C], pp[3], OP.mult)
                    s_h.inc(i1, 1)
                sil_p.c_done(fci, mark=(s_h.h, s_h.n))
                pg_p.c_done(fci, mark=(s_h.h, s_h.n))

            if dequant and ch == 0:
                for R in range(11):
                    rows = 128 if R < 10 else 96
                    ng = rows // 16
                    for nch in range(8):
                        ui = cnt["ui"]
                        cnt["ui"] += 1

                        def qload(sync, qtile, R=R, nch=nch, rows=rows):
                            i1 = sync.dma_start(
                                out=qtile[0:rows, :],
                                in_=qwdn_d[R * 128:R * 128 + rows,
                                           nch * 512:(nch + 1) * 512])
                            q_p.fill_inc(i1, ui, 16)

                        def scload(scalar, sg, z1, R=R, nch=nch, rows=rows,
                                   ng=ng):
                            for dst, tbl in ((sg, scdnb_d), (z1, zsdn_d)):
                                i1 = scalar.dma_start(
                                    out=dst[0:rows, 0:512],
                                    in_=tbl[8 * R:8 * R + ng,
                                            nch * 512:(nch + 1) * 512]
                                        .unsqueeze(1)
                                        .to_broadcast([ng, 16, 512]))
                                sc_p.fill_inc(i1, ui, 16)

                        def cache_out(sync, wtile, rows, j, R=R, nch=nch):
                            return sync.dma_start(
                                out=cdn_d.rearrange("(qq e) c -> qq e c", e=8)
                                    [R * 128:R * 128 + rows, j,
                                     nch * 512:(nch + 1) * 512],
                                in_=wtile[0:rows, :])

                        def consume_j(uj, j):
                            w_p.c_done(uj, mark=(s_cwS[uj % 4].h,
                                                 s_cwS[uj % 4].n))

                        emit_dequant_unit(fbx, ui, rows, qload, scload,
                                          cache_out, consume_j)
                fbx["dn_mark"] = [(s.h, s.n) for s in s_cwS]

            @blk.scalar
            def _(scalar):
                for hh, vv in fbx.get("dn_mark") or []:
                    if vv > 0:
                        scalar.wait_ge(hh, vv)

            for n in range(8):
                di = cnt["di"]
                cnt["di"] += 1
                ppd = [bank_f32(4 * (di % 2) + i) for i in range(4)]
                for kf in range(86):
                    wdi = cnt["wdi"]
                    cnt["wdi"] += 1

                    @blk.scalar
                    def _(scalar):
                        wd_p.p_wait(scalar, wdi)
                        i1 = scalar.dma_start(
                            out=wdb[wdi % 4][:],
                            in_=cdn_d[kf * 128:(kf + 1) * 128,
                                      n * 512:(n + 1) * 512])
                        wd_p.fill_inc(i1, wdi, 16)
                    wd_p.p_done(wdi)

                    @blk.tensor
                    def _(tensor):
                        wd_p.c_wait(tensor, wdi)
                        if kf == 0:
                            s_h.wait(tensor)
                            pd_p.p_wait(tensor, di)
                        i1 = None
                        for m in range(4):
                            i1 = tensor.matmul(
                                ppd[m],
                                hbuf[:, kf * C + m * 128:kf * C + (m + 1) * 128],
                                wdb[wdi % 4][:],
                                start=(kf == 0), stop=(kf == 85))
                        wd_p.free.inc(i1, 1)
                    wd_p.c_done(wdi)
                    if kf == 85:
                        pd_p.p_done(di, mark=(wd_p.free.h, wd_p.free.n))

                for m in range(4):
                    ri = cnt["ri"]
                    cnt["ri"] += 1

                    @blk.sync
                    def _(sync):
                        xr_p.p_wait(sync, ri)
                        i1 = sync.dma_start(
                            out=xres[ri % 2][:],
                            in_=src_d[tok0 + m * 128:tok0 + (m + 1) * 128,
                                      n * 512:(n + 1) * 512])
                        xr_p.fill_inc(i1, ri, 16)
                    xr_p.p_done(ri)

                    @blk.vector
                    def _(vector):
                        if m == 0:
                            pd_p.c_wait(vector, di)
                        xr_p.c_wait(vector, ri)
                        xo_p.p_wait(vector, ri)
                        i1 = vector.scalar_tensor_tensor(
                            xout[ri % 2][:], ppd[m], 0.5, xres[ri % 2][:],
                            OP.mult, OP.add)
                        xo_p.fill.inc(i1, 1)
                    xo_p.p_done(ri)
                    xr_p.c_done(ri, mark=(xo_p.fill.h, xo_p.fill.n))
                    if m == 3:
                        pd_p.c_done(di, mark=(xo_p.fill.h, xo_p.fill.n))

                    @blk.sync
                    def _(sync):
                        xo_p.c_wait(sync, ri)
                        i1 = sync.dma_start(
                            out=dst_ap[tok0 + m * 128:tok0 + (m + 1) * 128,
                                       n * 512:(n + 1) * 512],
                            in_=xout[ri % 2][:])
                        xoe[ri % 2].inc(i1, 16)
                    xo_p.c_done(ri, mark=(xoe[ri % 2].h, xoe[ri % 2].n))

    def emit_attention(src_d, dst_ap, qt, kts, vsb):
        with ExitStack() as sa:
            sb = lambda n, s, d: sa.enter_context(nc.sbuf_tensor(n, s, d))
            ln = dict(
                lnx=[sb("a_lnx0", [128, H], F32), sb("a_lnx1", [128, H], F32)],
                lnP=[sb("a_lnP0", [128, H], BF16), sb("a_lnP1", [128, H], BF16)],
                st1=sb("a_st1", [128, 1], F32), nm=sb("a_nm", [128, 1], F32),
                vr=sb("a_vr", [128, 1], F32), vr2=sb("a_vr2", [128, 1], F32),
            )
            xt = sb("a_xt", [128, 32 * T], BF16)
            wstg = [sb("a_ws0", [128, 512], F32), sb("a_ws1", [128, 512], F32)]
            wbf = [sb("a_wb0", [128, 512], BF16), sb("a_wb1", [128, 512], BF16)]
            kvout = sb("a_kvout", [128, 2048], BF16)

            emit_ln_xt(src_d, xt, 0, T // 128, ln, psum_guard=None)
            xt_ready = s_xt.n

            mgroups = [list(range(g * 4, min(34, g * 4 + 4))) for g in range(9)]
            for gi, mg in enumerate(mgroups):
                qi = cnt["qi"]
                cnt["qi"] += 1
                for k in range(32):
                    wqi = cnt["wqi"]
                    cnt["wqi"] += 1
                    jph, R = k // 4, k % 4
                    c0, c1 = mg[0] * 128, (mg[-1] + 1) * 128

                    @blk.sync
                    def _(sync):
                        wq_p.p_wait(sync, wqi)
                        i1 = sync.dma_start(
                            out=wstg[wqi % 2][:, 0:c1 - c0],
                            in_=wqkv_d.rearrange("(r j) c -> r j c", j=8)
                                [R * 128:(R + 1) * 128, jph, c0:c1])
                        wq_p.fill_inc(i1, wqi, 16)
                    wq_p.p_done(wqi)

                    @blk.vector
                    def _(vector):
                        wq_p.c_wait(vector, wqi)
                        i1 = vector.tensor_copy(wbf[wqi % 2][:, 0:c1 - c0],
                                                wstg[wqi % 2][:, 0:c1 - c0])
                        s_misc.inc(i1, 1)

                    @blk.tensor
                    def _(tensor):
                        s_misc.wait(tensor)
                        if k == 0:
                            if gi == 0:
                                tensor.wait_ge(s_xt.h, xt_ready)
                            qkv_p.p_wait(tensor, qi)
                        i1 = None
                        for mi, m in enumerate(mg):
                            for chh in range(2):
                                i1 = tensor.matmul(
                                    bank_f32(mi * 2 + chh),
                                    wbf[wqi % 2][:, mi * 128:(mi + 1) * 128],
                                    xt[:, k * T + chh * 512:
                                       k * T + (chh + 1) * 512],
                                    start=(k == 0), stop=(k == 31))
                        wq_p.free.inc(i1, 1)
                    wq_p.c_done(wqi)
                    if k == 31:
                        qkv_p.p_done(qi, mark=(wq_p.free.h, wq_p.free.n))

                @blk.scalar
                def _(scalar):
                    qkv_p.c_wait(scalar, qi)
                    i1 = None
                    for mi, m in enumerate(mg):
                        for chh in range(2):
                            src = bank_f32(mi * 2 + chh)
                            col = chh * 512
                            if i1 is not None:
                                vchain(scalar, i1, s_sch)
                            if m < 32:
                                i1 = scalar.activation(
                                    qt[:, m * T + col: m * T + col + 512],
                                    src, AF.Copy, scale=SCALE)
                            elif m == 32:
                                i1 = scalar.copy(
                                    kts[:, 2048 + col:2048 + col + 512], src)
                            else:
                                i1 = scalar.copy(
                                    vsb[:, 2048 + col:2048 + col + 512], src)
                    s_attS.inc(i1, 1)
                qkv_p.c_done(qi, mark=(s_attS.h, s_attS.n))

            for i in range(8):
                for which, srcbuf in ((0, kts), (1, vsb)):
                    ki = cnt["ki"]
                    cnt["ki"] += 1

                    @blk.tensor
                    def _(tensor):
                        if i == 0 and which == 0:
                            s_attS.wait(tensor)
                        pt2_p.p_wait(tensor, ki)
                        i1 = tensor.transpose(
                            bank_bf16_tile(ki % 2),
                            srcbuf[:, 2048 + i * 128:2048 + (i + 1) * 128],
                            identbf[:])
                        pt2_p.fill.inc(i1, 1)
                    pt2_p.p_done(ki)

                    @blk.vector
                    def _(vector):
                        pt2_p.c_wait(vector, ki)
                        i1 = vector.tensor_copy(
                            kvout[:, i * 256 + which * 128:
                                  i * 256 + (which + 1) * 128],
                            bank_bf16_tile(ki % 2))
                        s_attV.inc(i1, 1)
                    pt2_p.c_done(ki, mark=(s_attV.h, s_attV.n))

            @blk.gpsimd
            def _(gpsimd):
                s_attS.wait(gpsimd)
                s_attV.wait(gpsimd)
                pv = gpsimd.value_load(par_sb[0:1, 0:1])
                for i in range(8):
                    i1 = gpsimd.dma_start(
                        out=kvsh_d[bass.ts(pv * 8 + i, 128), :],
                        in_=kvout[:, i * 256:(i + 1) * 256])
                    s_misc.inc(i1, 16)
                s_misc.wait(gpsimd)
                with gpsimd.register64("pa64") as pa64:
                    gpsimd.reg_load(pa64, paddr_sb[0:1, 0:2])
                    gpsimd.store(pa64, 16)
                s_kvbar.n += 16
                s_kvbar.wait(gpsimd)
                i1 = gpsimd.dma_start(
                    out=vsb[:, 0:2048],
                    in_=kvsh_d.rearrange("(t p) d -> p t d", p=128)
                        [:, :, 128:256])
                s_misc.inc(i1, 16)
                i1 = gpsimd.dma_start(
                    out=kvout[:],
                    in_=kvsh_d.rearrange("(t p) d -> p t d", p=128)
                        [:, :, 0:128])
                s_misc.inc(i1, 16)

            for i in range(16):
                ki = cnt["ki"]
                cnt["ki"] += 1

                @blk.tensor
                def _(tensor):
                    if i == 0:
                        s_misc.wait(tensor)
                    pt2_p.p_wait(tensor, ki)
                    i1 = tensor.transpose(
                        bank_bf16_tile(ki % 2),
                        kvout[:, i * 128:(i + 1) * 128], identbf[:])
                    pt2_p.fill.inc(i1, 1)
                pt2_p.p_done(ki)

                @blk.vector
                def _(vector):
                    pt2_p.c_wait(vector, ki)
                    i1 = vector.tensor_copy(kts[:, i * 128:(i + 1) * 128],
                                            bank_bf16_tile(ki % 2))
                    s_attV.inc(i1, 1)
                pt2_p.c_done(ki, mark=(s_attV.h, s_attV.n))

        markA = (s_attS.n, s_attV.n)
        with ExitStack() as sbp:
            @blk.sync
            def _(sync):
                sync.wait_ge(s_attV.h, markA[1])

            @blk.scalar
            def _(scalar):
                scalar.wait_ge(s_attV.h, markA[1])

            sb = lambda n, s, d: sbp.enter_context(nc.sbuf_tensor(n, s, d))
            attnT = sb("b_at", [128, 32 * T], BF16)
            probs = [sb("b_pr0", [128, 2048], BF16),
                     sb("b_pr1", [128, 2048], BF16)]
            probsT = sb("b_pt", [128, 2048], BF16)
            nmv = sb("b_nmv", [128, 1], F32)
            sums = sb("b_sums", [128, 1], F32)
            rsum = sb("b_rsum", [128, 1], F32)
            wstg = [sb("b_ws0", [128, 512], F32), sb("b_ws1", [128, 512], F32)]
            wbf = [sb("b_wb0", [128, 512], BF16), sb("b_wb1", [128, 512], BF16)]
            xres = [sb("b_xr0", [128, 512], F32), sb("b_xr1", [128, 512], F32)]
            xout = [sb("b_xo0", [128, 512], F32), sb("b_xo1", [128, 512], F32)]

            @blk.tensor
            def _(tensor):
                s_attS.wait(tensor)
                s_attV.wait(tensor)
                s_misc.wait(tensor)

            def emit_tpv(ctx):
                h, qb, si, sblk, w1, wtot, r0 = (ctx["h"], ctx["qb"],
                                                 ctx["si"], ctx["sblk"],
                                                 ctx["w1"], ctx["wtot"],
                                                 ctx["r0"])

                def rg(cc, w):
                    return Pap[:, r0 + cc:r0 + cc + w]

                nkt = wtot // 128
                pbase = cnt["pbi"]
                for ktile in range(nkt):
                    ki = cnt["ki"]
                    cnt["ki"] += 1
                    pbi = cnt["pbi"]
                    cnt["pbi"] += 1
                    tslot = Pbf_ap[:, (si % 2) * 4096 + (ki % 2) * 1024:
                                   (si % 2) * 4096 + (ki % 2) * 1024 + 128]

                    @blk.tensor
                    def _(tensor):
                        if ktile == 0:
                            prb_p.c_wait(tensor, si)
                        pt2_p.p_wait(tensor, ki)
                        i1 = tensor.transpose(
                            tslot,
                            probs[si % 2][:, ktile * 128:(ktile + 1) * 128],
                            identbf[:])
                        pt2_p.fill.inc(i1, 1)
                    pt2_p.p_done(ki)

                    @blk.vector
                    def _(vector):
                        pt2_p.c_wait(vector, ki)
                        i1 = vector.tensor_copy(
                            probsT[:, ktile * 128:(ktile + 1) * 128], tslot)
                        pbt_p.fill.inc(i1, 1)
                    pbt_p.p_done(pbi)
                    pt2_p.c_done(ki, mark=(pbt_p.fill.h, pbt_p.fill.n))

                for ktile in range(nkt):
                    pbi = pbase + ktile

                    @blk.tensor
                    def _(tensor):
                        pbt_p.c_wait(tensor, pbi)
                        if ktile <= sblk:
                            vcol = ktile * 128
                        else:
                            vcol = 1024 + (ktile - (sblk + 1)) * 128
                        i1 = tensor.matmul(
                            rg(1024, 128),
                            vsb[:, vcol:vcol + 128],
                            probsT[:, ktile * 128:(ktile + 1) * 128],
                            start=(ktile == 0), stop=(ktile == nkt - 1))
                        if ktile == nkt - 1:
                            s_pv.inc(i1, 1)
                    pbt_p.c_done(pbi)
                prb_p.c_done(si, mark=(s_pv.h, s_pv.n))

                @blk.vector
                def _(vector):
                    s_pv.wait(vector)
                    i1 = vector.tensor_copy(
                        attnT[:, h * T + qb * 128: h * T + (qb + 1) * 128],
                        rg(1024, 128))
                    s_attV.inc(i1, 1)
                ssc_p.c_done(si, mark=(s_attV.h, s_attV.n))

            slopes_b = _alibi_slopes()
            pending = [None]
            for qb in range(8):
                sblk = qb
                w1 = 128 * (sblk + 1)
                wtot = 2 * w1
                pdi = cnt["pd"]
                cnt["pd"] += 1

                @blk.sync
                def _(sync):
                    pq_p.p_wait(sync, pdi)
                    i1 = sync.dma_start(
                        out=poskb[:],
                        in_=posd_d[qb * 128:(qb + 1) * 128, :])
                    pq_p.fill_inc(i1, pdi, 16)
                pq_p.p_done(pdi)

                for h in range(NH):
                    si = cnt["si"]
                    cnt["si"] += 1
                    r0 = (si % 2) * 2048

                    def rg(cc, w):
                        return Pap[:, r0 + cc:r0 + cc + w]

                    segs = []
                    for (cc0, k0) in ((0, 0), (w1, 1024)):
                        off = 0
                        while off < w1:
                            nn = min(512, w1 - off)
                            segs.append((cc0 + off, k0 + off, nn))
                            off += nn

                    @blk.tensor
                    def _(tensor):
                        ssc_p.p_wait(tensor, si)
                        lhs = qt[:, h * T + qb * 128: h * T + (qb + 1) * 128]
                        i1 = None
                        for gi, (cc, kk, nn) in enumerate(segs):
                            i1 = tensor.matmul(rg(cc, nn), lhs,
                                               kts[:, kk:kk + nn],
                                               start=(gi == 0),
                                               stop=(gi == len(segs) - 1))
                        ssc_p.fill.inc(i1, 1)
                    ssc_p.p_done(si)

                    @blk.vector
                    def _(vector):
                        ssc_p.c_wait(vector, si)
                        if h == 0:
                            pq_p.c_wait(vector, pdi)
                        sl = float(slopes_b[h])
                        vector.scalar_tensor_tensor(
                            rg(0, w1), poskb[:, 0:w1], sl, rg(0, w1),
                            OP.mult, OP.add)
                        i0 = vector.scalar_tensor_tensor(
                            rg(w1, w1), poskb[:, 1024:1024 + w1], sl,
                            rg(w1, w1), OP.mult, OP.add)
                        vchain(vector, i0, s_vch)
                        i1 = vector.tensor_reduce(nmv[:], rg(0, wtot), AX.X,
                                                  OP.max, negate=True)
                        s_attV.inc(i1, 1)

                    @blk.scalar
                    def _(scalar):
                        s_attV.wait(scalar)
                        prb_p.p_wait(scalar, si)
                        i1 = scalar.activation(
                            probs[si % 2][:, 0:wtot], rg(0, wtot), AF.Exp,
                            bias=nmv[:], scale=1.0, accum_out=sums[:])
                        s_attS.inc(i1, 1)

                    @blk.vector
                    def _(vector):
                        s_attS.wait(vector)
                        i0 = vector.reciprocal(rsum[:], sums[:])
                        vchain(vector, i0, s_vch)
                        i1 = vector.tensor_scalar(
                            probs[si % 2][:, 0:wtot], probs[si % 2][:, 0:wtot],
                            rsum[:], None, OP.mult)
                        prb_p.fill.inc(i1, 1)
                    prb_p.p_done(si)

                    if pending[0] is not None:
                        emit_tpv(pending[0])
                    pending[0] = dict(h=h, qb=qb, si=si, sblk=sblk, w1=w1,
                                      wtot=wtot, r0=r0)

                pq_p.c_done(pdi, mark=(s_vch.h, s_vch.n))

            emit_tpv(pending[0])

            @blk.tensor
            def _(tensor):
                s_attV.wait(tensor)

            for ch in range(2):
                for n in range(8):
                    oi = cnt["oi"]
                    cnt["oi"] += 1
                    ppq = [bank_f32(4 * (oi % 2) + i) for i in range(4)]
                    for k in range(32):
                        wqi = cnt["wqi"]
                        cnt["wqi"] += 1

                        @blk.sync
                        def _(sync):
                            wq_p.p_wait(sync, wqi)
                            i1 = sync.dma_start(
                                out=wstg[wqi % 2][:],
                                in_=wo_d[k * 128:(k + 1) * 128,
                                         n * 512:(n + 1) * 512])
                            wq_p.fill_inc(i1, wqi, 16)
                        wq_p.p_done(wqi)

                        @blk.vector
                        def _(vector):
                            wq_p.c_wait(vector, wqi)
                            i1 = vector.tensor_copy(wbf[wqi % 2][:, 0:512],
                                                    wstg[wqi % 2][:, 0:512])
                            s_misc.inc(i1, 1)

                        @blk.tensor
                        def _(tensor):
                            s_misc.wait(tensor)
                            if k == 0:
                                pwo_p.p_wait(tensor, oi)
                            i1 = None
                            for m in range(4):
                                i1 = tensor.matmul(
                                    ppq[m],
                                    attnT[:, k * T + ch * 512 + m * 128:
                                          k * T + ch * 512 + (m + 1) * 128],
                                    wbf[wqi % 2][:, 0:512],
                                    start=(k == 0), stop=(k == 31))
                            wq_p.free.inc(i1, 1)
                        wq_p.c_done(wqi)
                        if k == 31:
                            pwo_p.p_done(oi, mark=(wq_p.free.h, wq_p.free.n))

                    for m in range(4):
                        ri = cnt["ri"]
                        cnt["ri"] += 1

                        @blk.sync
                        def _(sync):
                            xr_p.p_wait(sync, ri)
                            i1 = sync.dma_start(
                                out=xres[ri % 2][:],
                                in_=src_d[ch * 512 + m * 128:
                                          ch * 512 + (m + 1) * 128,
                                          n * 512:(n + 1) * 512])
                            xr_p.fill_inc(i1, ri, 16)
                        xr_p.p_done(ri)

                        @blk.vector
                        def _(vector):
                            if m == 0:
                                pwo_p.c_wait(vector, oi)
                            xr_p.c_wait(vector, ri)
                            xo_p.p_wait(vector, ri)
                            i1 = vector.tensor_tensor(
                                xout[ri % 2][:], ppq[m], xres[ri % 2][:],
                                OP.add)
                            xo_p.fill.inc(i1, 1)
                        xo_p.p_done(ri)
                        xr_p.c_done(ri, mark=(xo_p.fill.h, xo_p.fill.n))
                        if m == 3:
                            pwo_p.c_done(oi, mark=(xo_p.fill.h, xo_p.fill.n))

                        @blk.sync
                        def _(sync):
                            xo_p.c_wait(sync, ri)
                            i1 = sync.dma_start(
                                out=dst_ap[ch * 512 + m * 128:
                                           ch * 512 + (m + 1) * 128,
                                           n * 512:(n + 1) * 512],
                                in_=xout[ri % 2][:])
                            xoe[ri % 2].inc(i1, 16)
                        xo_p.c_done(ri, mark=(xoe[ri % 2].h, xoe[ri % 2].n))

    def make_ffn(scope, pref):
        sb = lambda n, s, d: scope.enter_context(nc.sbuf_tensor(pref + n, s, d))
        return dict(
            xt=sb("xt", [128, 32 * C], BF16),
            h=sb("h", [128, 86 * C], BF16),
            qb=[sb("qb0", [128, 512], U32), sb("qb1", [128, 512], U32)],
            sgb=[sb("sg0", [128, 512], BF16), sb("sg1", [128, 512], BF16)],
            z1gb=[sb("z10", [128, 512], BF16), sb("z11", [128, 512], BF16)],
            ub=[sb("ub0", [128, 512], U32), sb("ub1", [128, 512], U32)],
            wb=[sb("wb0", [128, 512], BF16), sb("wb1", [128, 512], BF16),
                sb("wb2", [128, 512], BF16), sb("wb3", [128, 512], BF16)],
            wdb=[sb("wdb0", [128, 512], BF16), sb("wdb1", [128, 512], BF16),
                 sb("wdb2", [128, 512], BF16), sb("wdb3", [128, 512], BF16)],
            sgact=[sb("sga", [128, 2 * C], BF16), sb("sgc", [128, 2 * C], BF16)],
            xres=[sb("xr0", [128, 512], F32), sb("xr1", [128, 512], F32)],
            xout=[sb("xo0", [128, 512], F32), sb("xo1", [128, 512], F32)],
            dn_mark=0,
        )

    def make_ln(scope, pref):
        sb = lambda n, s, d: scope.enter_context(nc.sbuf_tensor(pref + n, s, d))
        return dict(
            lnx=[sb("lnx0", [128, H], F32), sb("lnx1", [128, H], F32)],
            lnP=[sb("lnP0", [128, H], BF16), sb("lnP1", [128, H], BF16)],
            st1=sb("st1", [128, 1], F32), nm=sb("nm", [128, 1], F32),
            vr=sb("vr", [128, 1], F32), vr2=sb("vr2", [128, 1], F32),
        )

    with ExitStack() as scope:
        ln = make_ln(scope, "f_")
        fbx = make_ffn(scope, "f_")
        dbg_fbx = fbx

        @blk.sync
        def _(sync):
            sync.wait_ge(s_zw.h, zdone)

        @blk.scalar
        def _(scalar):
            scalar.wait_ge(s_zw.h, zdone)
            s_misc.wait(scalar, 1)

        dst1 = out_d if STAGE == 1 else xs1_d
        emit_ffn_pass(fbx, ln, x0_d, dst1, True)
        if DBG:
            s_dbg = sem("s_dbg")

            @blk.sync
            def _(sync):
                xoe[0].wait(sync)
                xoe[1].wait(sync)
                sync.dma_start(out=dbg_xt_d, in_=dbg_fbx["xt"][:]).then_inc(
                    s_dbg.h, 16)
                sync.dma_start(out=dbg_h_d, in_=dbg_fbx["h"][:]).then_inc(
                    s_dbg.h, 16)
                sync.dma_start(out=dbg_cgu_d,
                               in_=cgu_d[0:1024, :]).then_inc(s_dbg.h, 16)
                sync.dma_start(out=dbg_cdn_d,
                               in_=cdn_d[0:1024, :]).then_inc(s_dbg.h, 16)
                sync.wait_ge(s_dbg.h, 64)
            s_dbg.n += 64

    if STAGE >= 2:
        fence_all()
        with ExitStack() as scope:
            sb = lambda n, s, d: scope.enter_context(nc.sbuf_tensor(n, s, d))
            qt = sb("at_qt", [128, 32 * T], BF16)
            kts = sb("at_kt", [128, 3072], BF16)
            vsb = sb("at_v", [128, 3072], BF16)
            dst2 = out_d if STAGE == 2 else xs2_d
            emit_attention(xs1_d, dst2, qt, kts, vsb)

    if STAGE >= 3:
        fence_all()
        with ExitStack() as scope:
            ln = make_ln(scope, "g_")
            fbx = make_ffn(scope, "g_")
            emit_ffn_pass(fbx, ln, xs2_d, out_d, False)

    @blk.sync
    def _(sync):
        xoe[0].wait(sync)
        xoe[1].wait(sync)

    es.close()
    return nc, kvbar_num


_cache = {}


def kernel(**inputs):
    if "nc" not in _cache:
        _cache["nc"], _cache["kvbar"] = build_program()
    nc, kvbar_num = _cache["nc"], _cache["kvbar"]

    x = np.ascontiguousarray(inputs["L0_attn_hidden_states"])

    pos_kv = np.concatenate(
        [np.concatenate([np.arange(g * 128, (g + 1) * 128)
                         for g in BLOCKS[0]]),
         np.concatenate([np.arange(g * 128, (g + 1) * 128)
                         for g in BLOCKS[1]])]).astype(np.float32)

    in_maps = []
    shard_rows = {}
    for c in range(8):
        b, par = c // 2, c & 1
        rows = np.concatenate([np.arange(g * 128, (g + 1) * 128)
                               for g in BLOCKS[par]])
        shard_rows[c] = rows
        pos_own = rows.astype(np.float32)
        diff = pos_kv[None, :] - pos_own[:, None]
        posd = np.where(diff > 0, np.float32(-1e30),
                        diff.astype(np.float32)).astype(np.float32)
        addr = TPB_BASES[L2P[c ^ 1]] + SEM_INC_REL + 4 * kvbar_num
        paddr = np.array([[addr & 0xFFFFFFFF, addr >> 32]], np.uint32)
        in_maps.append(dict(
            x0=np.ascontiguousarray(x[b, rows, :]),
            wqkv=inputs["L0_attn_Wqkv"],
            wo=inputs["L0_attn_Wo"],
            qwgu=inputs["L0_ffn_qweight_gu"],
            qzgu=inputs["L0_ffn_qzeros_gu"],
            scgu=inputs["L0_ffn_scales_gu"],
            qwdn=inputs["L0_ffn_qweight_down"],
            qzdn=inputs["L0_ffn_qzeros_down"],
            scdn=inputs["L0_ffn_scales_down"],
            posd=np.ascontiguousarray(posd),
            ident=np.eye(128, dtype=np.float32),
            par=np.array([[par]], np.int32),
            paddr=paddr,
        ))

    res = run_bass_kernel_spmd(nc, in_maps, core_ids=list(range(8)))
    out = np.empty((B, S, H), np.float32)
    for c in range(8):
        out[c // 2, shard_rows[c], :] = np.asarray(res.results[c]["out"])
    return out


# version 04
# speedup vs baseline: 1.6229x; 1.0177x over previous
import numpy as np
from contextlib import ExitStack

import concourse.bass as bass
import concourse.mybir as mybir
from concourse.alu_op_type import AluOpType as OP
from concourse.bass_utils import run_bass_kernel_spmd

F32 = mybir.dt.float32
BF16 = mybir.dt.bfloat16
U32 = mybir.dt.uint32
I32 = mybir.dt.int32
AF = mybir.ActivationFunctionType
AX = mybir.AxisListType

B, S, H = 4, 2048, 4096
NH, D = 32, 128
INTER = 11008
F2 = 2 * INTER
NQKV = H + 2 * D
T = 1024
C = 512
EPS = 1e-5
SCALE = float(np.float32(D) ** -0.5)
NEGBIG = -1e30

BLOCKS = {0: [0, 3, 4, 7, 8, 11, 12, 15], 1: [1, 2, 5, 6, 9, 10, 13, 14]}
SEM_INC_REL = 0x802701800
L2P = [4, 5, 6, 7, 2, 3, 0, 1]
TPB_BASES = [0x2000000000, 0x3000000000, 0x6000000000, 0x7000000000,
             0x802000000000, 0x803000000000, 0x806000000000, 0x807000000000]

STAGE = 3
DBG = False


def _alibi_slopes():
    import math
    n = NH
    cp2 = 2 ** math.floor(math.log2(n))
    base = 2.0 ** (-(2.0 ** (-(math.log2(cp2) - 3))))
    slopes = base ** np.arange(1, cp2 + 1, dtype=np.float64)
    if cp2 != n:
        extra_base = 2.0 ** (-(2.0 ** (-(math.log2(2 * cp2) - 3))))
        rem = min(cp2, n - cp2)
        extra = extra_base ** np.arange(1, 1 + 2 * rem, 2, dtype=np.float64)
        slopes = np.concatenate([slopes, extra])
    return slopes.astype(np.float32)


class Sem:
    def __init__(self, es, nc, name):
        self.h = es.enter_context(nc.semaphore(name))
        self.n = 0

    def inc(self, inst, k=1):
        inst.then_inc(self.h, k)
        self.n += k

    def wait(self, eng, n=None):
        v = self.n if n is None else n
        if v > 0:
            eng.wait_ge(self.h, v)


class Pipe:

    def __init__(self, es, nc, name, nslots, slot_fill_sems=False):
        if slot_fill_sems:
            self.fillS = [Sem(es, nc, f"{name}f{k}") for k in range(nslots)]
            self.fill = None
        else:
            self.fill = Sem(es, nc, name + "f")
            self.fillS = None
        self.free = Sem(es, nc, name + "e")
        self.n = nslots
        self.fill_marks = []
        self.free_marks = []

    def fill_inc(self, inst, i, k):
        s = self.fillS[i % self.n] if self.fillS is not None else self.fill
        s.inc(inst, k)

    def fill_mark(self, i):
        s = self.fillS[i % self.n] if self.fillS is not None else self.fill
        return (s.h, s.n)

    @staticmethod
    def _wait_marks(eng, marks):
        if isinstance(marks, tuple):
            marks = [marks]
        for h, v in marks:
            if v > 0:
                eng.wait_ge(h, v)

    def p_wait(self, eng, i):
        if i >= self.n:
            self._wait_marks(eng, self.free_marks[i - self.n])

    def p_done(self, i, mark=None):
        assert i == len(self.fill_marks), (i, len(self.fill_marks))
        self.fill_marks.append(mark if mark is not None
                               else self.fill_mark(i))

    def c_wait(self, eng, i):
        self._wait_marks(eng, self.fill_marks[i])

    def c_done(self, i, mark=None):
        while len(self.free_marks) <= i:
            self.free_marks.append(None)
        self.free_marks[i] = (mark if mark is not None
                              else (self.free.h, self.free.n))


def build_program():
    nc = bass.Bass("TRN2", target_bir_lowering=False)
    es = ExitStack()
    dt = nc.dram_tensor

    x0_d = dt("x0", [T, H], F32, kind="ExternalInput").ap()
    wqkv_d = dt("wqkv", [H, NQKV], F32, kind="ExternalInput").ap()
    wo_d = dt("wo", [H, H], F32, kind="ExternalInput").ap()
    qwgu_d = dt("qwgu", [512, F2], U32, kind="ExternalInput").ap()
    qzgu_d = dt("qzgu", [32, F2 // 8], U32, kind="ExternalInput").ap()
    scgu_d = dt("scgu", [32, F2], F32, kind="ExternalInput").ap()
    qwdn_d = dt("qwdn", [1376, H], U32, kind="ExternalInput").ap()
    qzdn_d = dt("qzdn", [86, H // 8], U32, kind="ExternalInput").ap()
    scdn_d = dt("scdn", [86, H], F32, kind="ExternalInput").ap()
    posd_d = dt("posd", [8 * 128, 2048], F32, kind="ExternalInput").ap()
    ident_d = dt("ident", [128, 128], F32, kind="ExternalInput").ap()
    par_d = dt("par", [1, 1], I32, kind="ExternalInput").ap()
    paddr_d = dt("paddr", [1, 2], U32, kind="ExternalInput").ap()
    out_d = dt("out", [T, H], F32, kind="ExternalOutput").ap()
    if DBG:
        dbg_xt_d = dt("dbg_xt", [128, 32 * C], BF16, kind="ExternalOutput").ap()
        dbg_h_d = dt("dbg_h", [128, 86 * C], BF16, kind="ExternalOutput").ap()
        dbg_cgu_d = dt("dbg_cgu", [1024, 512], BF16, kind="ExternalOutput").ap()
        dbg_cdn_d = dt("dbg_cdn", [1024, H], BF16, kind="ExternalOutput").ap()

    xs1_d = dt("xs1", [T, H], F32, kind="Internal").ap()
    xs2_d = dt("xs2", [T, H], F32, kind="Internal").ap()
    zsgu_d = dt("zsgu", [32, F2], BF16, kind="Internal").ap()
    zsdn_d = dt("zsdn", [86, H], BF16, kind="Internal").ap()
    scgub_d = dt("scgub", [32, F2], BF16, kind="Internal").ap()
    scdnb_d = dt("scdnb", [86, H], BF16, kind="Internal").ap()
    cgu_d = dt("cgu", [43 * 32 * 128, 512], BF16, kind="Internal").ap()
    cdn_d = dt("cdn", [INTER, H], BF16, kind="Internal").ap()
    kvsh_d = dt("kvsh", [2048, 256], BF16, kind="Internal",
                addr_space="Shared").ap()

    def psbuf(name, shape, dtyp):
        return es.enter_context(nc.sbuf_tensor(name, shape, dtyp))

    identbf = psbuf("identbf", [128, 128], BF16)
    poskb = psbuf("poskb", [128, 2048], F32)
    par_sb = psbuf("par_sb", [1, 1], I32)
    paddr_sb = psbuf("paddr_sb", [1, 2], U32)

    P = es.enter_context(nc.psum_tensor("P", [128, 4096], F32))
    Pap = P[:]
    Pbf_ap = P[:].bitcast(BF16)

    def bank_f32(i, w=512):
        return Pap[:, i * 512:i * 512 + w]

    def bank_bf16_tile(i):
        return Pbf_ap[:, i * 1024:i * 1024 + 128]

    def sem(name):
        return Sem(es, nc, name)

    def pipe(name, n, slot_fill_sems=False):
        return Pipe(es, nc, name, n, slot_fill_sems=slot_fill_sems)

    s_in = sem("s_in")
    s_swd = sem("s_swd")
    s_misc = sem("s_misc")
    s_kvbar = sem("s_kvbar")
    kvbar_num = s_kvbar.h.num

    lnx_p = pipe("lnx", 2, slot_fill_sems=True)
    lnP_p = pipe("lnP", 2)
    pT_p = pipe("pT", 2)
    s_lnv = sem("s_lnv")
    s_lns = sem("s_lns")
    s_xt = sem("s_xt")

    q_p = pipe("q", 2, slot_fill_sems=True)
    sc_p = pipe("sc", 2, slot_fill_sems=True)
    u_p = pipe("u", 2)
    w_p = pipe("w", 4, slot_fill_sems=True)
    pg_p = pipe("pg", 2)
    sil_p = pipe("sil", 2)
    s_h = sem("s_h")
    wd_p = pipe("wd", 4, slot_fill_sems=True)
    pd_p = pipe("pd", 2)
    xr_p = pipe("xr", 2, slot_fill_sems=True)
    xo_p = pipe("xo", 2)
    xoe = [sem("xoe0"), sem("xoe1")]
    s_ffn = sem("s_ffn")

    wq_p = pipe("wq", 2, slot_fill_sems=True)
    qkv_p = pipe("qkv", 1)
    pwo_p = pipe("pwo", 2)
    ssc_p = pipe("ssc", 2)
    prb_p = pipe("prb", 2)
    pt2_p = pipe("pt2", 2)
    pbt_p = pipe("pbt", 1 << 30)
    pq_p = pipe("pq", 1, slot_fill_sems=True)
    s_vch = sem("s_vch")
    s_sch = sem("s_sch")

    def vchain(eng, inst, semobj):
        semobj.inc(inst, 1)
        eng.wait_ge(semobj.h, semobj.n)

    s_attS = sem("s_attS")
    s_attV = sem("s_attV")
    s_pv = sem("s_pv")
    s_cwS = [sem(f"s_cw{k}") for k in range(4)]

    cnt = {k: 0 for k in ["li", "ti", "fci", "ui", "uj", "di", "wdi", "ri",
                          "qi", "wqi", "ki", "pbi", "ai", "si", "oi", "pd"]}

    blk = es.enter_context(nc.Block())

    def fence_all():
        for engdec in (blk.tensor, blk.vector, blk.scalar, blk.sync,
                       blk.gpsimd):
            @engdec
            def _(eng):
                xoe[0].wait(eng)
                xoe[1].wait(eng)

    with ExitStack() as zscope:
        ident_f = zscope.enter_context(nc.sbuf_tensor("ident_f", [128, 128], F32))

        @blk.sync
        def _(sync):
            sync.dma_start(out=par_sb[:], in_=par_d).then_inc(s_in.h, 16)
            sync.dma_start(out=paddr_sb[:], in_=paddr_d).then_inc(s_in.h, 16)
            sync.dma_start(out=ident_f[:], in_=ident_d).then_inc(s_in.h, 16)
        s_in.n += 48

        @blk.gpsimd
        def _(gpsimd):
            gpsimd.dma_start(out=scgub_d, in_=scgu_d).then_inc(s_swd.h, 16)
            gpsimd.dma_start(out=scdnb_d, in_=scdn_d).then_inc(s_swd.h, 16)
        s_swd.n += 32

        @blk.vector
        def _(vector):
            s_in.wait(vector)
            s_swd.wait(vector)
            vector.tensor_copy(identbf[:], ident_f[:]).then_inc(s_misc.h, 1)
        s_misc.n += 1

        s_z = sem("s_z")
        s_zc = sem("s_zc")
        s_zw = sem("s_zw")
        zq_sb = zscope.enter_context(nc.sbuf_tensor("zq_sb", [32, 2752], U32))
        zpl = zscope.enter_context(nc.sbuf_tensor("zpl", [86, 2752], U32))
        sc_sb = zscope.enter_context(nc.sbuf_tensor("sc_sb", [32, 11008], F32))
        z1f = zscope.enter_context(nc.sbuf_tensor("z1f", [32, 11008], F32))
        z1b = zscope.enter_context(nc.sbuf_tensor("z1b", [32, 11008], BF16))

        for half in range(2):
            c0 = half * 11008

            @blk.sync
            def _(sync):
                if half == 1:
                    s_zw.wait(sync)
                sync.dma_start(out=zq_sb[:, 0:1376],
                               in_=qzgu_d[:, c0 // 8:(c0 + 11008) // 8]
                               ).then_inc(s_z.h, 16)
                sync.dma_start(out=sc_sb[:],
                               in_=scgu_d[:, c0:c0 + 11008]).then_inc(s_z.h, 16)
            s_z.n += 32

            @blk.vector
            def _(vector):
                s_z.wait(vector)
                for jj in range(8):
                    i0 = vector.tensor_scalar(
                        zpl[0:32, 0:1376], zq_sb[:, 0:1376], 4 * jj, 15,
                        OP.logical_shift_right, OP.bitwise_and)
                    vchain(vector, i0, s_vch)
                    i0 = vector.tensor_copy(
                        z1f[:].rearrange("p (c j) -> p c j", j=8)[:, :, jj],
                        zpl[0:32, 0:1376])
                    vchain(vector, i0, s_vch)
                i0 = vector.tensor_scalar(z1f[:], z1f[:], 1.0, None, OP.add)
                vchain(vector, i0, s_vch)
                vector.tensor_tensor(z1b[:], z1f[:], sc_sb[:],
                                     OP.mult).then_inc(s_zc.h, 1)
            s_zc.n += 1

            @blk.sync
            def _(sync):
                s_zc.wait(sync)
                sync.dma_start(out=zsgu_d[:, c0:c0 + 11008],
                               in_=z1b[:]).then_inc(s_zw.h, 16)
            s_zw.n += 16

        zqd_sb = zscope.enter_context(nc.sbuf_tensor("zqd_sb", [86, 512], U32))
        scd_sb = zscope.enter_context(nc.sbuf_tensor("scd_sb", [86, 4096], F32))
        z1fd = zscope.enter_context(nc.sbuf_tensor("z1fd", [86, 4096], F32))
        z1bd = zscope.enter_context(nc.sbuf_tensor("z1bd", [86, 4096], BF16))

        @blk.sync
        def _(sync):
            sync.dma_start(out=zqd_sb[:], in_=qzdn_d).then_inc(s_z.h, 16)
            sync.dma_start(out=scd_sb[:], in_=scdn_d).then_inc(s_z.h, 16)
        s_z.n += 32

        @blk.vector
        def _(vector):
            s_z.wait(vector)
            for jj in range(8):
                i0 = vector.tensor_scalar(
                    zpl[0:86, 0:512], zqd_sb[:], 4 * jj, 15,
                    OP.logical_shift_right, OP.bitwise_and)
                vchain(vector, i0, s_vch)
                i0 = vector.tensor_copy(
                    z1fd[:].rearrange("p (c j) -> p c j", j=8)[:, :, jj],
                    zpl[0:86, 0:512])
                vchain(vector, i0, s_vch)
            i0 = vector.tensor_scalar(z1fd[:], z1fd[:], 1.0, None, OP.add)
            vchain(vector, i0, s_vch)
            vector.tensor_tensor(z1bd[:], z1fd[:], scd_sb[:],
                                 OP.mult).then_inc(s_zc.h, 1)
        s_zc.n += 1

        @blk.sync
        def _(sync):
            s_zc.wait(sync)
            sync.dma_start(out=zsdn_d, in_=z1bd[:]).then_inc(s_zw.h, 16)
            s_zw.wait(sync)
        s_zw.n += 16
    zdone = s_zw.n

    def emit_ln_xt(src_d, xt_sb, tok0, ntt, ln, psum_guard=None):
        lnx, lnP = ln["lnx"], ln["lnP"]
        st1, nm, vr = ln["st1"], ln["nm"], ln["vr"]
        vr2 = ln["vr2"]
        Ctok = ntt * 128
        first_guard = [psum_guard]
        for tt in range(ntt):
            li = cnt["li"]
            cnt["li"] += 1

            @blk.sync
            def _(sync):
                lnx_p.p_wait(sync, li)
                i1 = sync.dma_start(
                    out=lnx[li % 2][:],
                    in_=src_d[tok0 + tt * 128: tok0 + (tt + 1) * 128, :])
                lnx_p.fill_inc(i1, li, 16)
            lnx_p.p_done(li)

            @blk.vector
            def _(vector):
                lnx_p.c_wait(vector, li)
                i0 = vector.tensor_reduce(st1[:], lnx[li % 2][:], AX.X, OP.add)
                vchain(vector, i0, s_vch)
                i1 = vector.tensor_scalar(nm[:], st1[:], -1.0 / H, None,
                                          OP.mult)
                s_lnv.inc(i1, 1)

            @blk.scalar
            def _(scalar):
                s_lnv.wait(scalar)
                lnP_p.p_wait(scalar, li)
                i1 = scalar.activation(lnP[li % 2][:], lnx[li % 2][:],
                                       AF.Square, bias=nm[:], scale=1.0,
                                       accum_out=vr[:])
                s_lns.inc(i1, 1)

            @blk.vector
            def _(vector):
                s_lns.wait(vector)
                i1 = vector.tensor_scalar(vr[:], vr[:], 1.0 / H, EPS, OP.mult,
                                          OP.add)
                s_lnv.inc(i1, 1)

            @blk.scalar
            def _(scalar):
                s_lnv.wait(scalar)
                i1 = scalar.activation(vr2[:], vr[:], AF.Ln)
                vchain(scalar, i1, s_sch)
                i1 = scalar.activation(st1[:], vr2[:], AF.Exp, scale=-0.5)
                s_lns.inc(i1, 1)

            @blk.vector
            def _(vector):
                s_lns.wait(vector)
                lnP_p.p_wait(vector, li)
                i1 = None
                for jj in range(8):
                    i1 = vector.tensor_scalar(
                        lnP[li % 2][:, jj * 512:(jj + 1) * 512],
                        lnx[li % 2][:].rearrange("p (r j) -> p r j", j=8)
                            [:, :, jj],
                        nm[:], st1[:], OP.add, OP.mult)
                lnP_p.fill.inc(i1, 1)
            lnP_p.p_done(li)
            lnx_p.c_done(li, mark=(lnP_p.fill.h, lnP_p.fill.n))

            for t in range(32):
                ti = cnt["ti"]
                cnt["ti"] += 1

                @blk.tensor
                def _(tensor):
                    if t == 0:
                        lnP_p.c_wait(tensor, li)
                        if first_guard[0] is not None:
                            g, gn = first_guard[0]
                            if gn > 0:
                                tensor.wait_ge(g, gn)
                            first_guard[0] = None
                    pT_p.p_wait(tensor, ti)
                    i1 = tensor.transpose(
                        bank_bf16_tile(ti % 2),
                        lnP[li % 2][:, t * 128:(t + 1) * 128], identbf[:])
                    pT_p.fill.inc(i1, 1)
                pT_p.p_done(ti)
                if t == 31:
                    lnP_p.c_done(li, mark=(pT_p.fill.h, pT_p.fill.n))

                @blk.scalar
                def _(scalar):
                    pT_p.c_wait(scalar, ti)
                    i1 = scalar.copy(
                        xt_sb[:, t * Ctok + tt * 128: t * Ctok + (tt + 1) * 128],
                        bank_bf16_tile(ti % 2))
                    s_xt.inc(i1, 1)
                pT_p.c_done(ti, mark=(s_xt.h, s_xt.n))

    def emit_dequant_unit(fbx, ui, rows, qload, scload, cache_tile_out,
                          consume_j):
        usl = ui % 2
        qb, sgb, z1gb, ub, wb = (fbx["qb"], fbx["sgb"], fbx["z1gb"],
                                 fbx["ub"], fbx["wb"])

        @blk.sync
        def _(sync):
            q_p.p_wait(sync, ui)
            qload(sync, qb[usl])
        q_p.p_done(ui)

        @blk.scalar
        def _(scalar):
            sc_p.p_wait(scalar, ui)
            scload(scalar, sgb[usl], z1gb[usl])
        sc_p.p_done(ui)

        for j in range(8):
            uj = cnt["uj"]
            cnt["uj"] += 1

            @blk.vector
            def _(vector):
                if j == 0:
                    q_p.c_wait(vector, ui)
                    sc_p.c_wait(vector, ui)
                w_p.p_wait(vector, uj)
                wsl = wb[uj % 4][0:rows, :]
                i0 = vector.tensor_scalar(
                    ub[uj % 2][0:rows, :], qb[usl][0:rows, :], 4 * j, 15,
                    OP.logical_shift_right, OP.bitwise_and)
                vchain(vector, i0, s_vch)
                i0 = vector.tensor_tensor(wsl, ub[uj % 2][0:rows, :],
                                          sgb[usl][0:rows, :], OP.mult)
                vchain(vector, i0, s_vch)
                i1 = vector.tensor_tensor(wsl, wsl, z1gb[usl][0:rows, :],
                                          OP.subtract)
                w_p.fill_inc(i1, uj, 1)
            w_p.p_done(uj)
            if j == 7:
                q_p.c_done(ui, mark=w_p.fill_mark(uj))
                sc_p.c_done(ui, mark=w_p.fill_mark(uj))

            @blk.sync
            def _(sync):
                w_p.c_wait(sync, uj)
                i1 = cache_tile_out(sync, wb[uj % 4], rows, j)
                s_cwS[uj % 4].inc(i1, 16)

            consume_j(uj, j)

    def emit_ffn_pass(fbx, ln, src_d, dst_ap, dequant):
        xt, hbuf = fbx["xt"], fbx["h"]
        wb, wdb, sgact = fbx["wb"], fbx["wdb"], fbx["sgact"]
        xres, xout = fbx["xres"], fbx["xout"]

        for ch in range(1 if (DBG and dequant) else 2):
            tok0 = ch * C
            if ch == 1:
                @blk.scalar
                def _(scalar):
                    scalar.wait_ge(w_p.free.h, w_p.free.n)

                @blk.vector
                def _(vector):
                    vector.wait_ge(wd_p.free.h, wd_p.free.n)

                @blk.tensor
                def _(tensor):
                    tensor.wait_ge(xo_p.fill.h, xo_p.fill.n)
            guard = (xo_p.fill.h, xo_p.fill.n)
            emit_ln_xt(src_d, xt, tok0, C // 128, ln, psum_guard=guard)
            xt_ready = s_xt.n
            dequant_now = dequant and ch == 0
            if dequant and ch == 1:
                @blk.sync
                def _(sync):
                    for s in s_cwS:
                        s.wait(sync)

            for fc in range(43):
                fci = cnt["fci"]
                cnt["fci"] += 1
                pp = [bank_f32(4 * (fci % 2) + i) for i in range(4)]
                for R in range(4):
                    ui = cnt["ui"]
                    cnt["ui"] += 1
                    gcol, ucol = fc * 256, INTER + fc * 256

                    def consume_j(uj, j, fc=fc, R=R, fci=fci, pp=pp,
                                  xt_ready=xt_ready):
                        @blk.tensor
                        def _(tensor):
                            w_p.c_wait(tensor, uj)
                            if R == 0 and j == 0:
                                if fc == 0:
                                    tensor.wait_ge(s_xt.h, xt_ready)
                                pg_p.p_wait(tensor, fci)
                            t = j * 4 + R
                            i1 = None
                            for fm in range(4):
                                i1 = tensor.matmul(
                                    pp[fm],
                                    wb[uj % 4][:, fm * 128:(fm + 1) * 128],
                                    xt[:, t * C:(t + 1) * C],
                                    start=(R == 0 and j == 0),
                                    stop=(R == 3 and j == 7))
                            w_p.free.inc(i1, 1)
                        if dequant_now:
                            w_p.c_done(uj, mark=[
                                (w_p.free.h, w_p.free.n),
                                (s_cwS[uj % 4].h, s_cwS[uj % 4].n)])
                        else:
                            w_p.c_done(uj)
                        if R == 3 and j == 7:
                            pg_p.p_done(fci, mark=(w_p.free.h, w_p.free.n))

                    if dequant_now:
                        def qload(sync, qtile, R=R, gcol=gcol, ucol=ucol):
                            i1 = sync.dma_start(
                                out=qtile[:, 0:256],
                                in_=qwgu_d[R * 128:(R + 1) * 128,
                                           gcol:gcol + 256])
                            q_p.fill_inc(i1, ui, 16)
                            i2 = sync.dma_start(
                                out=qtile[:, 256:512],
                                in_=qwgu_d[R * 128:(R + 1) * 128,
                                           ucol:ucol + 256])
                            q_p.fill_inc(i2, ui, 16)

                        def scload(scalar, sg, z1, R=R, gcol=gcol, ucol=ucol):
                            for hl, cc in ((0, gcol), (1, ucol)):
                                for dst, tbl in ((sg, scgub_d), (z1, zsgu_d)):
                                    i1 = scalar.dma_start(
                                        out=dst[:, hl * 256:(hl + 1) * 256],
                                        in_=tbl[8 * R:8 * R + 8, cc:cc + 256]
                                            .unsqueeze(1)
                                            .to_broadcast([8, 16, 256]))
                                    sc_p.fill_inc(i1, ui, 16)

                        def cache_out(sync, wtile, rows, j, fc=fc, R=R):
                            t = j * 4 + R
                            row = (fc * 32 + t) * 128
                            return sync.dma_start(out=cgu_d[row:row + 128, :],
                                                  in_=wtile[:])

                        emit_dequant_unit(fbx, ui, 128, qload, scload,
                                          cache_out, consume_j)
                    else:
                        for j in range(8):
                            uj = cnt["uj"]
                            cnt["uj"] += 1

                            @blk.sync
                            def _(sync):
                                w_p.p_wait(sync, uj)
                                t = j * 4 + R
                                row = (fc * 32 + t) * 128
                                i1 = sync.dma_start(
                                    out=wb[uj % 4][:],
                                    in_=cgu_d[row:row + 128, :])
                                w_p.fill_inc(i1, uj, 16)
                            w_p.p_done(uj)
                            consume_j(uj, j)

                @blk.scalar
                def _(scalar):
                    pg_p.c_wait(scalar, fci)
                    sil_p.p_wait(scalar, fci)
                    i0 = scalar.activation(sgact[fci % 2][:, 0:C], pp[0],
                                           AF.Silu)
                    vchain(scalar, i0, s_sch)
                    i1 = scalar.activation(sgact[fci % 2][:, C:2 * C], pp[1],
                                           AF.Silu)
                    sil_p.fill.inc(i1, 1)
                sil_p.p_done(fci)

                @blk.vector
                def _(vector):
                    sil_p.c_wait(vector, fci)
                    vector.tensor_tensor(
                        hbuf[:, (2 * fc) * C:(2 * fc + 1) * C],
                        sgact[fci % 2][:, 0:C], pp[2], OP.mult)
                    i1 = vector.tensor_tensor(
                        hbuf[:, (2 * fc + 1) * C:(2 * fc + 2) * C],
                        sgact[fci % 2][:, C:2 * C], pp[3], OP.mult)
                    s_h.inc(i1, 1)
                sil_p.c_done(fci, mark=(s_h.h, s_h.n))
                pg_p.c_done(fci, mark=(s_h.h, s_h.n))

            if dequant and ch == 0:
                for R in range(11):
                    rows = 128 if R < 10 else 96
                    ng = rows // 16
                    for nch in range(8):
                        ui = cnt["ui"]
                        cnt["ui"] += 1

                        def qload(sync, qtile, R=R, nch=nch, rows=rows):
                            i1 = sync.dma_start(
                                out=qtile[0:rows, :],
                                in_=qwdn_d[R * 128:R * 128 + rows,
                                           nch * 512:(nch + 1) * 512])
                            q_p.fill_inc(i1, ui, 16)

                        def scload(scalar, sg, z1, R=R, nch=nch, rows=rows,
                                   ng=ng):
                            for dst, tbl in ((sg, scdnb_d), (z1, zsdn_d)):
                                i1 = scalar.dma_start(
                                    out=dst[0:rows, 0:512],
                                    in_=tbl[8 * R:8 * R + ng,
                                            nch * 512:(nch + 1) * 512]
                                        .unsqueeze(1)
                                        .to_broadcast([ng, 16, 512]))
                                sc_p.fill_inc(i1, ui, 16)

                        def cache_out(sync, wtile, rows, j, R=R, nch=nch):
                            return sync.dma_start(
                                out=cdn_d.rearrange("(qq e) c -> qq e c", e=8)
                                    [R * 128:R * 128 + rows, j,
                                     nch * 512:(nch + 1) * 512],
                                in_=wtile[0:rows, :])

                        def consume_j(uj, j):
                            w_p.c_done(uj, mark=(s_cwS[uj % 4].h,
                                                 s_cwS[uj % 4].n))

                        emit_dequant_unit(fbx, ui, rows, qload, scload,
                                          cache_out, consume_j)
                fbx["dn_mark"] = [(s.h, s.n) for s in s_cwS]

            @blk.scalar
            def _(scalar):
                for hh, vv in fbx.get("dn_mark") or []:
                    if vv > 0:
                        scalar.wait_ge(hh, vv)

            for n in range(8):
                di = cnt["di"]
                cnt["di"] += 1
                ppd = [bank_f32(4 * (di % 2) + i) for i in range(4)]
                for kf in range(86):
                    wdi = cnt["wdi"]
                    cnt["wdi"] += 1

                    @blk.scalar
                    def _(scalar):
                        wd_p.p_wait(scalar, wdi)
                        i1 = scalar.dma_start(
                            out=wdb[wdi % 4][:],
                            in_=cdn_d[kf * 128:(kf + 1) * 128,
                                      n * 512:(n + 1) * 512])
                        wd_p.fill_inc(i1, wdi, 16)
                    wd_p.p_done(wdi)

                    @blk.tensor
                    def _(tensor):
                        wd_p.c_wait(tensor, wdi)
                        if kf == 0:
                            s_h.wait(tensor)
                            pd_p.p_wait(tensor, di)
                        i1 = None
                        for m in range(4):
                            i1 = tensor.matmul(
                                ppd[m],
                                hbuf[:, kf * C + m * 128:kf * C + (m + 1) * 128],
                                wdb[wdi % 4][:],
                                start=(kf == 0), stop=(kf == 85))
                        wd_p.free.inc(i1, 1)
                    wd_p.c_done(wdi)
                    if kf == 85:
                        pd_p.p_done(di, mark=(wd_p.free.h, wd_p.free.n))

                for m in range(4):
                    ri = cnt["ri"]
                    cnt["ri"] += 1

                    @blk.sync
                    def _(sync):
                        xr_p.p_wait(sync, ri)
                        i1 = sync.dma_start(
                            out=xres[ri % 2][:],
                            in_=src_d[tok0 + m * 128:tok0 + (m + 1) * 128,
                                      n * 512:(n + 1) * 512])
                        xr_p.fill_inc(i1, ri, 16)
                    xr_p.p_done(ri)

                    @blk.vector
                    def _(vector):
                        if m == 0:
                            pd_p.c_wait(vector, di)
                        xr_p.c_wait(vector, ri)
                        xo_p.p_wait(vector, ri)
                        i1 = vector.scalar_tensor_tensor(
                            xout[ri % 2][:], ppd[m], 0.5, xres[ri % 2][:],
                            OP.mult, OP.add)
                        xo_p.fill.inc(i1, 1)
                    xo_p.p_done(ri)
                    xr_p.c_done(ri, mark=(xo_p.fill.h, xo_p.fill.n))
                    if m == 3:
                        pd_p.c_done(di, mark=(xo_p.fill.h, xo_p.fill.n))

                    @blk.sync
                    def _(sync):
                        xo_p.c_wait(sync, ri)
                        i1 = sync.dma_start(
                            out=dst_ap[tok0 + m * 128:tok0 + (m + 1) * 128,
                                       n * 512:(n + 1) * 512],
                            in_=xout[ri % 2][:])
                        xoe[ri % 2].inc(i1, 16)
                    xo_p.c_done(ri, mark=(xoe[ri % 2].h, xoe[ri % 2].n))

    def emit_attention(src_d, dst_ap, qt, kts, vsb):
        with ExitStack() as sa:
            sb = lambda n, s, d: sa.enter_context(nc.sbuf_tensor(n, s, d))
            ln = dict(
                lnx=[sb("a_lnx0", [128, H], F32), sb("a_lnx1", [128, H], F32)],
                lnP=[sb("a_lnP0", [128, H], BF16), sb("a_lnP1", [128, H], BF16)],
                st1=sb("a_st1", [128, 1], F32), nm=sb("a_nm", [128, 1], F32),
                vr=sb("a_vr", [128, 1], F32), vr2=sb("a_vr2", [128, 1], F32),
            )
            xt = sb("a_xt", [128, 32 * T], BF16)
            wstg = [sb("a_ws0", [128, 512], F32), sb("a_ws1", [128, 512], F32)]
            wbf = [sb("a_wb0", [128, 512], BF16), sb("a_wb1", [128, 512], BF16)]
            kvout = sb("a_kvout", [128, 2048], BF16)

            emit_ln_xt(src_d, xt, 0, T // 128, ln, psum_guard=None)
            xt_ready = s_xt.n

            mgroups = [list(range(g * 4, min(34, g * 4 + 4))) for g in range(9)]
            for gi, mg in enumerate(mgroups):
                qi = cnt["qi"]
                cnt["qi"] += 1
                for k in range(32):
                    wqi = cnt["wqi"]
                    cnt["wqi"] += 1
                    jph, R = k // 4, k % 4
                    c0, c1 = mg[0] * 128, (mg[-1] + 1) * 128

                    @blk.sync
                    def _(sync):
                        wq_p.p_wait(sync, wqi)
                        i1 = sync.dma_start(
                            out=wstg[wqi % 2][:, 0:c1 - c0],
                            in_=wqkv_d.rearrange("(r j) c -> r j c", j=8)
                                [R * 128:(R + 1) * 128, jph, c0:c1])
                        wq_p.fill_inc(i1, wqi, 16)
                    wq_p.p_done(wqi)

                    @blk.vector
                    def _(vector):
                        wq_p.c_wait(vector, wqi)
                        i1 = vector.tensor_copy(wbf[wqi % 2][:, 0:c1 - c0],
                                                wstg[wqi % 2][:, 0:c1 - c0])
                        s_misc.inc(i1, 1)

                    @blk.tensor
                    def _(tensor):
                        s_misc.wait(tensor)
                        if k == 0:
                            if gi == 0:
                                tensor.wait_ge(s_xt.h, xt_ready)
                            qkv_p.p_wait(tensor, qi)
                        i1 = None
                        for mi, m in enumerate(mg):
                            for chh in range(2):
                                i1 = tensor.matmul(
                                    bank_f32(mi * 2 + chh),
                                    wbf[wqi % 2][:, mi * 128:(mi + 1) * 128],
                                    xt[:, k * T + chh * 512:
                                       k * T + (chh + 1) * 512],
                                    start=(k == 0), stop=(k == 31))
                        wq_p.free.inc(i1, 1)
                    wq_p.c_done(wqi)
                    if k == 31:
                        qkv_p.p_done(qi, mark=(wq_p.free.h, wq_p.free.n))

                @blk.scalar
                def _(scalar):
                    qkv_p.c_wait(scalar, qi)
                    i1 = None
                    for mi, m in enumerate(mg):
                        for chh in range(2):
                            src = bank_f32(mi * 2 + chh)
                            col = chh * 512
                            if i1 is not None:
                                vchain(scalar, i1, s_sch)
                            if m < 32:
                                i1 = scalar.activation(
                                    qt[:, m * T + col: m * T + col + 512],
                                    src, AF.Copy, scale=SCALE)
                            elif m == 32:
                                i1 = scalar.copy(
                                    kts[:, 2048 + col:2048 + col + 512], src)
                            else:
                                i1 = scalar.copy(
                                    vsb[:, 2048 + col:2048 + col + 512], src)
                    s_attS.inc(i1, 1)
                qkv_p.c_done(qi, mark=(s_attS.h, s_attS.n))

            for i in range(8):
                for which, srcbuf in ((0, kts), (1, vsb)):
                    ki = cnt["ki"]
                    cnt["ki"] += 1

                    @blk.tensor
                    def _(tensor):
                        if i == 0 and which == 0:
                            s_attS.wait(tensor)
                        pt2_p.p_wait(tensor, ki)
                        i1 = tensor.transpose(
                            bank_bf16_tile(ki % 2),
                            srcbuf[:, 2048 + i * 128:2048 + (i + 1) * 128],
                            identbf[:])
                        pt2_p.fill.inc(i1, 1)
                    pt2_p.p_done(ki)

                    @blk.vector
                    def _(vector):
                        pt2_p.c_wait(vector, ki)
                        i1 = vector.tensor_copy(
                            kvout[:, i * 256 + which * 128:
                                  i * 256 + (which + 1) * 128],
                            bank_bf16_tile(ki % 2))
                        s_attV.inc(i1, 1)
                    pt2_p.c_done(ki, mark=(s_attV.h, s_attV.n))

            @blk.gpsimd
            def _(gpsimd):
                s_attS.wait(gpsimd)
                s_attV.wait(gpsimd)
                pv = gpsimd.value_load(par_sb[0:1, 0:1])
                for i in range(8):
                    i1 = gpsimd.dma_start(
                        out=kvsh_d[bass.ts(pv * 8 + i, 128), :],
                        in_=kvout[:, i * 256:(i + 1) * 256])
                    s_swd.inc(i1, 16)
                s_swd.wait(gpsimd)
                with gpsimd.register64("pa64") as pa64:
                    gpsimd.reg_load(pa64, paddr_sb[0:1, 0:2])
                    gpsimd.store(pa64, 16)
                s_kvbar.n += 16
                s_kvbar.wait(gpsimd)
                i1 = gpsimd.dma_start(
                    out=vsb[:, 0:2048],
                    in_=kvsh_d.rearrange("(t p) d -> p t d", p=128)
                        [:, :, 128:256])
                s_swd.inc(i1, 16)
                i1 = gpsimd.dma_start(
                    out=kvout[:],
                    in_=kvsh_d.rearrange("(t p) d -> p t d", p=128)
                        [:, :, 0:128])
                s_swd.inc(i1, 16)

            for i in range(16):
                ki = cnt["ki"]
                cnt["ki"] += 1

                @blk.tensor
                def _(tensor):
                    if i == 0:
                        s_misc.wait(tensor)
                        s_swd.wait(tensor)
                    pt2_p.p_wait(tensor, ki)
                    i1 = tensor.transpose(
                        bank_bf16_tile(ki % 2),
                        kvout[:, i * 128:(i + 1) * 128], identbf[:])
                    pt2_p.fill.inc(i1, 1)
                pt2_p.p_done(ki)

                @blk.vector
                def _(vector):
                    pt2_p.c_wait(vector, ki)
                    i1 = vector.tensor_copy(kts[:, i * 128:(i + 1) * 128],
                                            bank_bf16_tile(ki % 2))
                    s_attV.inc(i1, 1)
                pt2_p.c_done(ki, mark=(s_attV.h, s_attV.n))

        markA = (s_attS.n, s_attV.n)
        with ExitStack() as sbp:
            @blk.sync
            def _(sync):
                sync.wait_ge(s_attV.h, markA[1])

            @blk.scalar
            def _(scalar):
                scalar.wait_ge(s_attV.h, markA[1])

            sb = lambda n, s, d: sbp.enter_context(nc.sbuf_tensor(n, s, d))
            attnT = sb("b_at", [128, 32 * T], BF16)
            probs = [sb("b_pr0", [128, 2048], BF16),
                     sb("b_pr1", [128, 2048], BF16)]
            probsT = sb("b_pt", [128, 2048], BF16)
            nmv = sb("b_nmv", [128, 1], F32)
            sums = sb("b_sums", [128, 1], F32)
            rsum = sb("b_rsum", [128, 1], F32)
            wstg = [sb("b_ws0", [128, 512], F32), sb("b_ws1", [128, 512], F32)]
            wbf = [sb("b_wb0", [128, 512], BF16), sb("b_wb1", [128, 512], BF16)]
            xres = [sb("b_xr0", [128, 512], F32), sb("b_xr1", [128, 512], F32)]
            xout = [sb("b_xo0", [128, 512], F32), sb("b_xo1", [128, 512], F32)]

            @blk.tensor
            def _(tensor):
                s_attS.wait(tensor)
                s_attV.wait(tensor)
                s_misc.wait(tensor)
                s_swd.wait(tensor)

            def emit_tpv(ctx):
                h, qb, si, sblk, w1, wtot, r0 = (ctx["h"], ctx["qb"],
                                                 ctx["si"], ctx["sblk"],
                                                 ctx["w1"], ctx["wtot"],
                                                 ctx["r0"])

                def rg(cc, w):
                    return Pap[:, r0 + cc:r0 + cc + w]

                nkt = wtot // 128
                pbase = cnt["pbi"]
                for ktile in range(nkt):
                    ki = cnt["ki"]
                    cnt["ki"] += 1
                    pbi = cnt["pbi"]
                    cnt["pbi"] += 1
                    tslot = Pbf_ap[:, (si % 2) * 4096 + (ki % 2) * 1024:
                                   (si % 2) * 4096 + (ki % 2) * 1024 + 128]

                    @blk.tensor
                    def _(tensor):
                        if ktile == 0:
                            prb_p.c_wait(tensor, si)
                        pt2_p.p_wait(tensor, ki)
                        i1 = tensor.transpose(
                            tslot,
                            probs[si % 2][:, ktile * 128:(ktile + 1) * 128],
                            identbf[:])
                        pt2_p.fill.inc(i1, 1)
                    pt2_p.p_done(ki)

                    @blk.vector
                    def _(vector):
                        pt2_p.c_wait(vector, ki)
                        i1 = vector.tensor_copy(
                            probsT[:, ktile * 128:(ktile + 1) * 128], tslot)
                        pbt_p.fill.inc(i1, 1)
                    pbt_p.p_done(pbi)
                    pt2_p.c_done(ki, mark=(pbt_p.fill.h, pbt_p.fill.n))

                for ktile in range(nkt):
                    pbi = pbase + ktile

                    @blk.tensor
                    def _(tensor):
                        pbt_p.c_wait(tensor, pbi)
                        if ktile <= sblk:
                            vcol = ktile * 128
                        else:
                            vcol = 1024 + (ktile - (sblk + 1)) * 128
                        i1 = tensor.matmul(
                            rg(1024, 128),
                            vsb[:, vcol:vcol + 128],
                            probsT[:, ktile * 128:(ktile + 1) * 128],
                            start=(ktile == 0), stop=(ktile == nkt - 1))
                        if ktile == nkt - 1:
                            s_pv.inc(i1, 1)
                    pbt_p.c_done(pbi)
                prb_p.c_done(si, mark=(s_pv.h, s_pv.n))

                @blk.vector
                def _(vector):
                    s_pv.wait(vector)
                    i1 = vector.tensor_copy(
                        attnT[:, h * T + qb * 128: h * T + (qb + 1) * 128],
                        rg(1024, 128))
                    s_attV.inc(i1, 1)
                ssc_p.c_done(si, mark=(s_attV.h, s_attV.n))

            slopes_b = _alibi_slopes()
            pending = [None]
            for qb in range(8):
                sblk = qb
                w1 = 128 * (sblk + 1)
                wtot = 2 * w1
                pdi = cnt["pd"]
                cnt["pd"] += 1

                @blk.sync
                def _(sync):
                    pq_p.p_wait(sync, pdi)
                    i1 = sync.dma_start(
                        out=poskb[:],
                        in_=posd_d[qb * 128:(qb + 1) * 128, :])
                    pq_p.fill_inc(i1, pdi, 16)
                pq_p.p_done(pdi)

                for h in range(NH):
                    si = cnt["si"]
                    cnt["si"] += 1
                    r0 = (si % 2) * 2048

                    def rg(cc, w):
                        return Pap[:, r0 + cc:r0 + cc + w]

                    segs = []
                    for (cc0, k0) in ((0, 0), (w1, 1024)):
                        off = 0
                        while off < w1:
                            nn = min(512, w1 - off)
                            segs.append((cc0 + off, k0 + off, nn))
                            off += nn

                    @blk.tensor
                    def _(tensor):
                        ssc_p.p_wait(tensor, si)
                        lhs = qt[:, h * T + qb * 128: h * T + (qb + 1) * 128]
                        i1 = None
                        for gi, (cc, kk, nn) in enumerate(segs):
                            i1 = tensor.matmul(rg(cc, nn), lhs,
                                               kts[:, kk:kk + nn],
                                               start=(gi == 0),
                                               stop=(gi == len(segs) - 1))
                        ssc_p.fill.inc(i1, 1)
                    ssc_p.p_done(si)

                    @blk.vector
                    def _(vector):
                        ssc_p.c_wait(vector, si)
                        if h == 0:
                            pq_p.c_wait(vector, pdi)
                        sl = float(slopes_b[h])
                        vector.scalar_tensor_tensor(
                            rg(0, w1), poskb[:, 0:w1], sl, rg(0, w1),
                            OP.mult, OP.add)
                        i0 = vector.scalar_tensor_tensor(
                            rg(w1, w1), poskb[:, 1024:1024 + w1], sl,
                            rg(w1, w1), OP.mult, OP.add)
                        vchain(vector, i0, s_vch)
                        i1 = vector.tensor_reduce(nmv[:], rg(0, wtot), AX.X,
                                                  OP.max, negate=True)
                        s_attV.inc(i1, 1)

                    @blk.scalar
                    def _(scalar):
                        s_attV.wait(scalar)
                        prb_p.p_wait(scalar, si)
                        i1 = scalar.activation(
                            probs[si % 2][:, 0:wtot], rg(0, wtot), AF.Exp,
                            bias=nmv[:], scale=1.0, accum_out=sums[:])
                        s_attS.inc(i1, 1)

                    @blk.vector
                    def _(vector):
                        s_attS.wait(vector)
                        i0 = vector.reciprocal(rsum[:], sums[:])
                        vchain(vector, i0, s_vch)
                        i1 = vector.tensor_scalar(
                            probs[si % 2][:, 0:wtot], probs[si % 2][:, 0:wtot],
                            rsum[:], None, OP.mult)
                        prb_p.fill.inc(i1, 1)
                    prb_p.p_done(si)

                    if pending[0] is not None:
                        emit_tpv(pending[0])
                    pending[0] = dict(h=h, qb=qb, si=si, sblk=sblk, w1=w1,
                                      wtot=wtot, r0=r0)

                pq_p.c_done(pdi, mark=(s_vch.h, s_vch.n))

            emit_tpv(pending[0])

            @blk.tensor
            def _(tensor):
                s_attV.wait(tensor)

            for ch in range(2):
                for n in range(8):
                    oi = cnt["oi"]
                    cnt["oi"] += 1
                    ppq = [bank_f32(4 * (oi % 2) + i) for i in range(4)]
                    for k in range(32):
                        wqi = cnt["wqi"]
                        cnt["wqi"] += 1

                        @blk.sync
                        def _(sync):
                            wq_p.p_wait(sync, wqi)
                            i1 = sync.dma_start(
                                out=wstg[wqi % 2][:],
                                in_=wo_d[k * 128:(k + 1) * 128,
                                         n * 512:(n + 1) * 512])
                            wq_p.fill_inc(i1, wqi, 16)
                        wq_p.p_done(wqi)

                        @blk.vector
                        def _(vector):
                            wq_p.c_wait(vector, wqi)
                            i1 = vector.tensor_copy(wbf[wqi % 2][:, 0:512],
                                                    wstg[wqi % 2][:, 0:512])
                            s_misc.inc(i1, 1)

                        @blk.tensor
                        def _(tensor):
                            s_misc.wait(tensor)
                            if k == 0:
                                pwo_p.p_wait(tensor, oi)
                            i1 = None
                            for m in range(4):
                                i1 = tensor.matmul(
                                    ppq[m],
                                    attnT[:, k * T + ch * 512 + m * 128:
                                          k * T + ch * 512 + (m + 1) * 128],
                                    wbf[wqi % 2][:, 0:512],
                                    start=(k == 0), stop=(k == 31))
                            wq_p.free.inc(i1, 1)
                        wq_p.c_done(wqi)
                        if k == 31:
                            pwo_p.p_done(oi, mark=(wq_p.free.h, wq_p.free.n))

                    for m in range(4):
                        ri = cnt["ri"]
                        cnt["ri"] += 1

                        @blk.sync
                        def _(sync):
                            xr_p.p_wait(sync, ri)
                            i1 = sync.dma_start(
                                out=xres[ri % 2][:],
                                in_=src_d[ch * 512 + m * 128:
                                          ch * 512 + (m + 1) * 128,
                                          n * 512:(n + 1) * 512])
                            xr_p.fill_inc(i1, ri, 16)
                        xr_p.p_done(ri)

                        @blk.vector
                        def _(vector):
                            if m == 0:
                                pwo_p.c_wait(vector, oi)
                            xr_p.c_wait(vector, ri)
                            xo_p.p_wait(vector, ri)
                            i1 = vector.tensor_tensor(
                                xout[ri % 2][:], ppq[m], xres[ri % 2][:],
                                OP.add)
                            xo_p.fill.inc(i1, 1)
                        xo_p.p_done(ri)
                        xr_p.c_done(ri, mark=(xo_p.fill.h, xo_p.fill.n))
                        if m == 3:
                            pwo_p.c_done(oi, mark=(xo_p.fill.h, xo_p.fill.n))

                        @blk.sync
                        def _(sync):
                            xo_p.c_wait(sync, ri)
                            i1 = sync.dma_start(
                                out=dst_ap[ch * 512 + m * 128:
                                           ch * 512 + (m + 1) * 128,
                                           n * 512:(n + 1) * 512],
                                in_=xout[ri % 2][:])
                            xoe[ri % 2].inc(i1, 16)
                        xo_p.c_done(ri, mark=(xoe[ri % 2].h, xoe[ri % 2].n))

    def make_ffn(scope, pref):
        sb = lambda n, s, d: scope.enter_context(nc.sbuf_tensor(pref + n, s, d))
        return dict(
            xt=sb("xt", [128, 32 * C], BF16),
            h=sb("h", [128, 86 * C], BF16),
            qb=[sb("qb0", [128, 512], U32), sb("qb1", [128, 512], U32)],
            sgb=[sb("sg0", [128, 512], BF16), sb("sg1", [128, 512], BF16)],
            z1gb=[sb("z10", [128, 512], BF16), sb("z11", [128, 512], BF16)],
            ub=[sb("ub0", [128, 512], U32), sb("ub1", [128, 512], U32)],
            wb=[sb("wb0", [128, 512], BF16), sb("wb1", [128, 512], BF16),
                sb("wb2", [128, 512], BF16), sb("wb3", [128, 512], BF16)],
            wdb=[sb("wdb0", [128, 512], BF16), sb("wdb1", [128, 512], BF16),
                 sb("wdb2", [128, 512], BF16), sb("wdb3", [128, 512], BF16)],
            sgact=[sb("sga", [128, 2 * C], BF16), sb("sgc", [128, 2 * C], BF16)],
            xres=[sb("xr0", [128, 512], F32), sb("xr1", [128, 512], F32)],
            xout=[sb("xo0", [128, 512], F32), sb("xo1", [128, 512], F32)],
            dn_mark=0,
        )

    def make_ln(scope, pref):
        sb = lambda n, s, d: scope.enter_context(nc.sbuf_tensor(pref + n, s, d))
        return dict(
            lnx=[sb("lnx0", [128, H], F32), sb("lnx1", [128, H], F32)],
            lnP=[sb("lnP0", [128, H], BF16), sb("lnP1", [128, H], BF16)],
            st1=sb("st1", [128, 1], F32), nm=sb("nm", [128, 1], F32),
            vr=sb("vr", [128, 1], F32), vr2=sb("vr2", [128, 1], F32),
        )

    with ExitStack() as scope:
        ln = make_ln(scope, "f_")
        fbx = make_ffn(scope, "f_")
        dbg_fbx = fbx

        @blk.sync
        def _(sync):
            sync.wait_ge(s_zw.h, zdone)

        @blk.scalar
        def _(scalar):
            scalar.wait_ge(s_zw.h, zdone)
            s_misc.wait(scalar, 1)

        dst1 = out_d if STAGE == 1 else xs1_d
        emit_ffn_pass(fbx, ln, x0_d, dst1, True)
        if DBG:
            s_dbg = sem("s_dbg")

            @blk.sync
            def _(sync):
                xoe[0].wait(sync)
                xoe[1].wait(sync)
                sync.dma_start(out=dbg_xt_d, in_=dbg_fbx["xt"][:]).then_inc(
                    s_dbg.h, 16)
                sync.dma_start(out=dbg_h_d, in_=dbg_fbx["h"][:]).then_inc(
                    s_dbg.h, 16)
                sync.dma_start(out=dbg_cgu_d,
                               in_=cgu_d[0:1024, :]).then_inc(s_dbg.h, 16)
                sync.dma_start(out=dbg_cdn_d,
                               in_=cdn_d[0:1024, :]).then_inc(s_dbg.h, 16)
                sync.wait_ge(s_dbg.h, 64)
            s_dbg.n += 64

    if STAGE >= 2:
        fence_all()
        with ExitStack() as scope:
            sb = lambda n, s, d: scope.enter_context(nc.sbuf_tensor(n, s, d))
            qt = sb("at_qt", [128, 32 * T], BF16)
            kts = sb("at_kt", [128, 3072], BF16)
            vsb = sb("at_v", [128, 3072], BF16)
            dst2 = out_d if STAGE == 2 else xs2_d
            emit_attention(xs1_d, dst2, qt, kts, vsb)

    if STAGE >= 3:
        fence_all()
        with ExitStack() as scope:
            ln = make_ln(scope, "g_")
            fbx = make_ffn(scope, "g_")
            emit_ffn_pass(fbx, ln, xs2_d, out_d, False)

    @blk.sync
    def _(sync):
        xoe[0].wait(sync)
        xoe[1].wait(sync)

    es.close()
    return nc, kvbar_num


_cache = {}


def kernel(**inputs):
    if "nc" not in _cache:
        _cache["nc"], _cache["kvbar"] = build_program()
    nc, kvbar_num = _cache["nc"], _cache["kvbar"]

    x = np.ascontiguousarray(inputs["L0_attn_hidden_states"])
    for k, n in (("L0_ffn_g_idx_gu", 4096), ("L0_ffn_g_idx_down", 11008)):
        gi = np.asarray(inputs[k]).reshape(-1)
        assert gi.shape[0] == n and np.array_equal(
            gi, np.arange(n) // 128), f"{k}: unsupported g_idx layout"

    pos_kv = np.concatenate(
        [np.concatenate([np.arange(g * 128, (g + 1) * 128)
                         for g in BLOCKS[0]]),
         np.concatenate([np.arange(g * 128, (g + 1) * 128)
                         for g in BLOCKS[1]])]).astype(np.float32)

    in_maps = []
    shard_rows = {}
    for c in range(8):
        b, par = c // 2, c & 1
        rows = np.concatenate([np.arange(g * 128, (g + 1) * 128)
                               for g in BLOCKS[par]])
        shard_rows[c] = rows
        pos_own = rows.astype(np.float32)
        diff = pos_kv[None, :] - pos_own[:, None]
        posd = np.where(diff > 0, np.float32(-1e30),
                        diff.astype(np.float32)).astype(np.float32)
        addr = TPB_BASES[L2P[c ^ 1]] + SEM_INC_REL + 4 * kvbar_num
        paddr = np.array([[addr & 0xFFFFFFFF, addr >> 32]], np.uint32)
        in_maps.append(dict(
            x0=np.ascontiguousarray(x[b, rows, :]),
            wqkv=inputs["L0_attn_Wqkv"],
            wo=inputs["L0_attn_Wo"],
            qwgu=inputs["L0_ffn_qweight_gu"],
            qzgu=inputs["L0_ffn_qzeros_gu"],
            scgu=inputs["L0_ffn_scales_gu"],
            qwdn=inputs["L0_ffn_qweight_down"],
            qzdn=inputs["L0_ffn_qzeros_down"],
            scdn=inputs["L0_ffn_scales_down"],
            posd=np.ascontiguousarray(posd),
            ident=np.eye(128, dtype=np.float32),
            par=np.array([[par]], np.int32),
            paddr=paddr,
        ))

    res = run_bass_kernel_spmd(nc, in_maps, core_ids=list(range(8)))
    out = np.empty((B, S, H), np.float32)
    for c in range(8):
        out[c // 2, shard_rows[c], :] = np.asarray(res.results[c]["out"])
    return out
